# version 04
# baseline (speedup 1.0000x reference)
import contextlib

import numpy as np

import concourse.bass as bass
import concourse.mybir as mybir
from concourse.bass_utils import run_bass_kernel_spmd

F32 = mybir.dt.float32
BF16 = mybir.dt.bfloat16
AF = mybir.ActivationFunctionType
ALU = mybir.AluOpType
AX = mybir.AxisListType

NCORES = 8
BATCH, SEQ, D = 32, 2048, 1024
SEQ_PER_CORE = BATCH // NCORES
T = SEQ_PER_CORE * SEQ
TC = 512
NTC = T // TC
KT = D // 128
I_FFN = 2816
IT = I_FFN // 128
NHEAD, HD, WIN = 16, 64, 128
NCH = T // 128
RC = 12582912.0
LN_EPS = 1e-5
ATT_EPS = 1e-6
BN_EPS = 1e-5
QMAX = 127.0


class Sems:

    def __init__(self, nc, names):
        self.h = {}
        self.c = {}
        self._cm = []
        for n in names:
            cm = nc.semaphore(n)
            sem = cm.__enter__()
            self._cm.append(cm)
            self.h[n] = sem
            self.c[n] = 0

    def inc(self, ins, name, n=1):
        self.c[name] += n
        return ins.then_inc(self.h[name], n)

    def at(self, name):
        return self.c[name]


def build_nc(dev_phase=None, nch_limit=None, p1b_stage=5, ncl_limit=None,
             race_detect=False):
    nc = bass.Bass("TRN2", target_bir_lowering=False,
                   detect_race_conditions=race_detect)

    def kind_for(phase):
        if dev_phase is not None and phase <= dev_phase:
            return "ExternalOutput"
        return "Internal"

    x_in = nc.dram_tensor("x_in", [T, D], F32, kind="ExternalInput").ap()
    l0_wqT = nc.dram_tensor("l0_wqT", [D, D], F32, kind="ExternalInput").ap()
    l0_wkT = nc.dram_tensor("l0_wkT", [D, D], F32, kind="ExternalInput").ap()
    l0_wvT = nc.dram_tensor("l0_wvT", [D, D], F32, kind="ExternalInput").ap()
    l0_woT = nc.dram_tensor("l0_woT", [D, D], F32, kind="ExternalInput").ap()
    l0_cx = nc.dram_tensor("l0_cx", [1, 1], F32, kind="ExternalInput").ap()
    l0_cd = nc.dram_tensor("l0_cd", [1, 1], F32, kind="ExternalInput").ap()
    l0_wg = nc.dram_tensor("l0_wg", [I_FFN, D], F32, kind="ExternalInput").ap()
    l0_wu = nc.dram_tensor("l0_wu", [I_FFN, D], F32, kind="ExternalInput").ap()
    l0_wd = nc.dram_tensor("l0_wd", [D, I_FFN], F32, kind="ExternalInput").ap()
    l1_wq = nc.dram_tensor("l1_wq", [D, D], F32, kind="ExternalInput").ap()
    l1_wk = nc.dram_tensor("l1_wk", [D, D], F32, kind="ExternalInput").ap()
    l1_wv = nc.dram_tensor("l1_wv", [D, D], F32, kind="ExternalInput").ap()
    l1_wo = nc.dram_tensor("l1_wo", [D, D], F32, kind="ExternalInput").ap()
    l1_w = nc.dram_tensor("l1_w", [D, D], F32, kind="ExternalInput").ap()
    so_bd = nc.dram_tensor("so_bd", [D, 256], F32, kind="ExternalInput").ap()
    si_bd = nc.dram_tensor("si_bd", [D, 256], F32, kind="ExternalInput").ap()
    cvb_c = nc.dram_tensor("cvb_c", [128, 4], F32, kind="ExternalInput").ap()
    gam_c = nc.dram_tensor("gam_c", [128, 4], F32, kind="ExternalInput").ap()
    bet_c = nc.dram_tensor("bet_c", [128, 4], F32, kind="ExternalInput").ap()
    so_linT = nc.dram_tensor("so_linT", [256, D], F32, kind="ExternalInput").ap()
    si_linT = nc.dram_tensor("si_linT", [256, D], F32, kind="ExternalInput").ap()
    so_lb = nc.dram_tensor("so_lb", [128, D], F32, kind="ExternalInput").ap()
    si_lb = nc.dram_tensor("si_lb", [128, D], F32, kind="ExternalInput").ap()
    l1_b8 = nc.dram_tensor("l1_b8", [128, 8], F32, kind="ExternalInput").ap()
    out_h = nc.dram_tensor("out", [T, D], F32, kind="ExternalOutput").ap()

    xnT_h = nc.dram_tensor("xnT_h", [D, T], F32, kind=kind_for(0)).ap()
    xT_h = nc.dram_tensor("xT_h", [D, T], F32, kind=kind_for(0)).ap()
    qT_h = nc.dram_tensor("qT_h", [D, T], F32, kind=kind_for(1)).ap()
    kT_h = nc.dram_tensor("kT_h", [D, T], F32, kind=kind_for(1)).ap()
    v_h = nc.dram_tensor("v_h", [T, D], F32, kind=kind_for(1)).ap()
    p1T_h = nc.dram_tensor("p1T_h", [D, T], F32, kind=kind_for(2)).ap()
    h_h = nc.dram_tensor("h_h", [I_FFN, T], F32, kind=kind_for(3)).ap()
    x1T_h = nc.dram_tensor("x1T_h", [D, T], F32, kind=kind_for(4)).ap()
    xn1T_h = nc.dram_tensor("xn1T_h", [D, T], F32, kind=kind_for(5)).ap()
    q1T_h = nc.dram_tensor("q1T_h", [D, T], F32, kind=kind_for(5)).ap()
    k1T_h = nc.dram_tensor("k1T_h", [D, T], F32, kind=kind_for(5)).ap()
    v1_h = nc.dram_tensor("v1_h", [T, D], F32, kind=kind_for(5)).ap()
    fullT_h = nc.dram_tensor("fullT_h", [D, T], F32, kind=kind_for(6)).ap()
    xcT_h = nc.dram_tensor("xcT_h", [512, T], F32, kind=kind_for(6)).ap()
    p2T_h = nc.dram_tensor("p2T_h", [D, T], F32, kind=kind_for(6)).ap()

    ccx_in = nc.dram_tensor("ccx_in", [1, 1], F32, kind="Internal")
    ccx_out = nc.dram_tensor("ccx_out", [1, 1], F32, kind="Internal")
    cch_in = nc.dram_tensor("cch_in", [1, 1], F32, kind="Internal")
    cch_out = nc.dram_tensor("cch_out", [1, 1], F32, kind="Internal")
    ccbn_in = nc.dram_tensor("ccbn_in", [128, 8], F32, kind="Internal")
    ccbn_out = nc.dram_tensor("ccbn_out", [128, 8], F32, kind="Internal")

    gmax_dbg = None
    if dev_phase is not None:
        gmax_dbg = nc.dram_tensor("gmax_dbg", [1, 1], F32, kind="ExternalOutput").ap()

    import ml_dtypes
    ident_c = nc.inline_tensor(np.eye(128, dtype=np.float32), "ident_c")
    identb_c = nc.inline_tensor(np.eye(128, dtype=ml_dtypes.bfloat16), "identb_c")
    _i = np.arange(128)[:, None]
    _j = np.arange(256)[None, :]
    _band = ((_j >= _i + 1) & (_j <= _i + WIN)).astype(np.float32)
    maskG_c = nc.inline_tensor(np.ascontiguousarray(_band), "maskG_c")
    mask0_c = nc.inline_tensor(
        np.ascontiguousarray(_band * (_j >= WIN)), "mask0_c")
    _jj = np.arange(128)[:, None]
    _ii = np.arange(256)[None, :]
    l1m0_c = nc.inline_tensor(
        np.ascontiguousarray((_jj <= _ii).astype(np.float32)), "l1m0_c")
    l1m1_c = nc.inline_tensor(
        np.ascontiguousarray(((_jj + 128) <= _ii).astype(np.float32)), "l1m1_c")
    _il = np.zeros((128, 64), np.float32)
    _il[np.arange(128), np.arange(128) % 64] = 1.0
    identL_c = nc.inline_tensor(_il, "identL_c")

    sems = Sems(nc, [
        "s_ld", "s_ld2", "s_st", "s_st2", "s_sq", "s_sqr", "s_var", "s_std",
        "s_xn", "s_tpA", "s_tpB", "s_evA", "s_evB", "s_mm", "s_cc", "s_misc",
        "s_vh", "s_pj", "s_pjA", "s_pjB", "s_sc", "s_exp", "s_pf", "s_pt",
        "s_ptev", "s_av", "s_oev", "s_wo", "s_p1ev",
        "s_ld3", "s_wq", "s_wT", "s_scl", "s_qx", "s_gu", "s_guev", "s_hm",
        "s_dn", "s_x1ev", "s_guA", "s_bc",
        "s_ln", "s_lnb", "s_fm",
        "s_A", "s_msk", "s_den", "s_dnv", "s_rdn", "s_bcm", "s_num", "s_yml",
        "s_ktr", "s_knv", "s_bce", "s_Smm", "s_Sup", "s_seq", "s_p2", "s_wo2",
        "s_cst",
        "xldA", "xldB", "sxnA", "sxnB", "sxtA", "sxtB",
        "xiA", "xiB", "sq1s", "sk1s", "sv1s",
        "qkvA", "qkvB", "qkvC", "xrA", "xrB", "sp1",
        "wnA", "wnB", "xqA", "xqB", "shs",
        "wdA", "wdB", "hin1", "p1in1", "sx1s",
        "x1inA", "sxn1A", "sxn1B", "qcA", "qcB", "x1rA", "sp2",
        "gmld", "chld", "sout",
    ])

    ident = nc.alloc_sbuf_tensor("ident", [128, 128], F32).ap()
    identb = nc.alloc_sbuf_tensor("identb", [128, 128], BF16).ap()
    cst = nc.alloc_sbuf_tensor("cst", [128, 8], F32).ap()
    ones_r = nc.alloc_sbuf_tensor("ones_r", [1, 128], F32).ap()
    ones_c = nc.alloc_sbuf_tensor("ones_c", [128, 1], F32).ap()

    x_in_t = x_in.rearrange("(c a p) d -> c p a d", a=4, p=128)
    xnT_h_t = xnT_h.rearrange("(k p) t -> p k t", p=128)
    xT_h_t = xT_h.rearrange("(k p) t -> p k t", p=128)
    qT_h_t = qT_h.rearrange("(k p) t -> p k t", p=128)
    kT_h_t = kT_h.rearrange("(k p) t -> p k t", p=128)
    v_h_t = v_h.rearrange("(c a p) d -> c p a d", a=4, p=128)
    p1T_h_t = p1T_h.rearrange("(k p) t -> p k t", p=128)

    with contextlib.ExitStack() as ctx:
        sb = lambda nm, shp, dt=F32: ctx.enter_context(nc.sbuf_tensor(nm, shp, dt)).ap()
        ps = lambda nm, shp: ctx.enter_context(nc.psum_tensor(nm, shp, F32)).ap()
        xb = [sb(f"xb{i}", [128, 4 * 1024]) for i in range(2)]
        sqb = [sb(f"sqb{i}", [128, 1024]) for i in range(2)]
        xnb = [sb(f"xnb{i}", [128, 4 * 1024]) for i in range(2)]
        xnT_sb = [sb(f"xnT_sb{i}", [128, 8 * 512]) for i in range(2)]
        xT_sb = [sb(f"xT_sb{i}", [128, 8 * 512]) for i in range(2)]
        st_m = [sb(f"st_m{i}", [128, 4]) for i in range(2)]
        st_r = [sb(f"st_r{i}", [128, 4]) for i in range(2)]
        st_s1 = sb("st_s1", [128, 4])
        st_s2 = sb("st_s2", [128, 4])
        st_var = [sb(f"st_var{i}", [128, 4]) for i in range(2)]
        st_std = [sb(f"st_std{i}", [128, 4]) for i in range(2)]
        amax_run = sb("amax_run", [128, 1])
        amax_t = sb("amax_t", [128, 1])
        amax_row = sb("amax_row", [1, 128])
        gmax_sb = sb("gmax_sb", [1, 1])
        ps_tp = [ps(f"ps_tp{i}", [128, 512]) for i in range(4)]
        ps_sm = ps("ps_sm", [128, 128])

        with nc.Block() as blk:
            @blk.sync
            def _(sync):
                sync.dma_start(out=ident, in_=ident_c.ap()).then_inc(sems.h["s_misc"], 16)
                for tc in range(NTC):
                    if tc >= 2:
                        sync.wait_ge(sems.h["s_xn"], tc - 1)
                        sync.wait_ge(sems.h["s_sq"], 4 * (tc - 1))
                        sync.wait_ge(sems.h["s_tpB"], 8 * (tc - 1))
                    sems.inc(sync.dma_start(
                        out=xb[tc % 2].rearrange("p (a d) -> p a d", a=4),
                        in_=x_in_t[tc]), "xldA" if tc % 2 == 0 else "xldB", 16)

            @blk.scalar
            def _(scalar):
                for tc in range(NTC):
                    scalar.wait_ge(sems.h["xldA" if tc % 2 == 0 else "xldB"],
                                   16 * (tc // 2 + 1))
                    for sub in range(4):
                        q = 4 * tc + sub
                        if q >= 2:
                            scalar.wait_ge(sems.h["s_sqr"], q - 1)
                        sems.inc(scalar.activation(
                            out=sqb[q % 2],
                            in_=xb[tc % 2][:, sub * 1024:(sub + 1) * 1024],
                            func=AF.Square), "s_sq", 1)
                    scalar.wait_ge(sems.h["s_var"], tc + 1)
                    if tc == 0:
                        scalar.wait_ge(sems.h["s_misc"], 16)
                        scalar.wait_ge(sems.h["s_cst"], 1)
                    sems.inc(scalar.activation(
                        out=st_std[tc % 2], in_=st_var[tc % 2],
                        func=AF.Sqrt, bias=cst[:, 0:1]), "s_std", 1)
                    if tc >= 2:
                        scalar.wait_ge(sems.h["sxnA" if tc % 2 == 0 else "sxnB"],
                                       16 * (tc // 2))
                    for j in range(8):
                        g = 8 * tc + j
                        scalar.wait_ge(sems.h["s_tpA"], g + 1)
                        sems.inc(scalar.copy(
                            out=xnT_sb[tc % 2][:, j * 512:(j + 1) * 512],
                            in_=ps_tp[j % 2]), "s_evA", 1)

            @blk.vector
            def _(vector):
                vector.memset(amax_run, 0.0)
                vector.memset(cst[:, 0:1], LN_EPS)
                vector.memset(cst[:, 1:2], RC)
                vector.memset(ones_c, 1.0)
                vector.memset(cst[:, 3:4], BN_EPS)
                sems.inc(vector.memset(cst[:, 2:3], 0.125), "s_cst", 1)
                for tc in range(NTC):
                    vector.wait_ge(sems.h["xldA" if tc % 2 == 0 else "xldB"],
                                   16 * (tc // 2 + 1))
                    for sub in range(4):
                        q = 4 * tc + sub
                        vector.reduce_sum(
                            st_s1[:, sub:sub + 1],
                            xb[tc % 2][:, sub * 1024:(sub + 1) * 1024], axis=AX.X)
                        vector.wait_ge(sems.h["s_sq"], q + 1)
                        sems.inc(vector.reduce_sum(
                            st_s2[:, sub:sub + 1], sqb[q % 2], axis=AX.X), "s_sqr", 1)
                    vector.tensor_scalar_mul(st_m[tc % 2], st_s1, 1.0 / D)
                    vector.tensor_scalar_mul(st_s2, st_s2, 1.0 / D)
                    sems.inc(vector.tensor_mul(st_s1, st_m[tc % 2], st_m[tc % 2]),
                             "s_vh", 1)
                    vector.wait_ge(sems.h["s_vh"], sems.at("s_vh"))
                    sems.inc(vector.tensor_sub(st_var[tc % 2], st_s2, st_s1),
                             "s_var", 1)
                    vector.wait_ge(sems.h["s_std"], tc + 1)
                    sems.inc(vector.reciprocal(st_r[tc % 2], st_std[tc % 2]), "s_vh", 1)
                    vector.wait_ge(sems.h["s_vh"], sems.at("s_vh"))
                    for sub in range(4):
                        vector.tensor_scalar(
                            xnb[tc % 2][:, sub * 1024:(sub + 1) * 1024],
                            xb[tc % 2][:, sub * 1024:(sub + 1) * 1024],
                            st_m[tc % 2][:, sub:sub + 1],
                            st_r[tc % 2][:, sub:sub + 1],
                            ALU.subtract, ALU.mult)
                    sems.inc(vector.reduce_max(
                        amax_t, xnb[tc % 2], axis=AX.X,
                        apply_absolute_value=True), "s_vh", 1)
                    vector.wait_ge(sems.h["s_vh"], sems.at("s_vh"))
                    sems.inc(vector.tensor_max(amax_run, amax_run, amax_t), "s_xn", 1)
                    if tc >= 2:
                        vector.wait_ge(sems.h["sxtA" if tc % 2 == 0 else "sxtB"],
                                       16 * (tc // 2))
                    for j in range(8):
                        g = 8 * tc + j
                        vector.wait_ge(sems.h["s_tpB"], g + 1)
                        sems.inc(vector.tensor_copy(
                            xT_sb[tc % 2][:, j * 512:(j + 1) * 512],
                            ps_tp[2 + j % 2]), "s_evB", 1)

            @blk.tensor
            def _(tensor):
                tensor.wait_ge(sems.h["s_misc"], 16)
                for tc in range(NTC):
                    tensor.wait_ge(sems.h["s_xn"], tc + 1)
                    for j in range(8):
                        g = 8 * tc + j
                        if g >= 2:
                            tensor.wait_ge(sems.h["s_evA"], g - 1)
                        for sub in range(4):
                            ins = tensor.transpose(
                                ps_tp[j % 2][:, sub * 128:(sub + 1) * 128],
                                xnb[tc % 2][:, sub * 1024 + j * 128: sub * 1024 + (j + 1) * 128],
                                ident)
                            if sub == 3:
                                sems.inc(ins, "s_tpA", 1)
                    for j in range(8):
                        g = 8 * tc + j
                        if g >= 2:
                            tensor.wait_ge(sems.h["s_evB"], g - 1)
                        for sub in range(4):
                            ins = tensor.transpose(
                                ps_tp[2 + j % 2][:, sub * 128:(sub + 1) * 128],
                                xb[tc % 2][:, sub * 1024 + j * 128: sub * 1024 + (j + 1) * 128],
                                ident)
                            if sub == 3:
                                sems.inc(ins, "s_tpB", 1)
                tensor.wait_ge(sems.h["s_xn"], NTC)
                sems.inc(tensor.transpose(ps_sm[0:1, :], amax_run, ident), "s_mm", 1)

            @blk.gpsimd
            def _(g):
                for tc in range(NTC):
                    g.wait_ge(sems.h["s_evA"], 8 * (tc + 1))
                    sems.inc(g.dma_start(
                        out=xnT_h_t[:, :, tc * 512:(tc + 1) * 512],
                        in_=xnT_sb[tc % 2].rearrange("p (k t) -> p k t", t=512)),
                        "sxnA" if tc % 2 == 0 else "sxnB", 16)
                    g.wait_ge(sems.h["s_evB"], 8 * (tc + 1))
                    sems.inc(g.dma_start(
                        out=xT_h_t[:, :, tc * 512:(tc + 1) * 512],
                        in_=xT_sb[tc % 2].rearrange("p (k t) -> p k t", t=512)),
                        "sxtA" if tc % 2 == 0 else "sxtB", 16)

            @blk.vector
            def _(vector):
                vector.wait_ge(sems.h["s_mm"], 1)
                sems.inc(vector.tensor_copy(amax_row, ps_sm[0:1, :]), "s_vh", 1)
                vector.wait_ge(sems.h["s_vh"], sems.at("s_vh"))
                sems.inc(vector.reduce_max(gmax_sb, amax_row, axis=AX.X), "s_mm", 1)

            @blk.gpsimd
            def _(g):
                g.wait_ge(sems.h["s_mm"], 2)
                sems.inc(g.dma_start(out=ccx_in.ap(), in_=gmax_sb), "s_st", 16)
                g.wait_ge(sems.h["s_st"], sems.at("s_st"))
                sems.inc(g.collective_compute(
                    "AllReduce", ALU.max,
                    replica_groups=[list(range(NCORES))],
                    ins=[ccx_in.ap()], outs=[ccx_out.ap()]), "s_cc", 1)

    if dev_phase is not None and dev_phase == 0:
        with nc.Block() as blk:
            @blk.gpsimd
            def _(g):
                g.wait_ge(sems.h["s_cc"], 1)
                sems.inc(g.dma_start(out=gmax_dbg, in_=ccx_out.ap()), "s_st", 16)
                g.wait_ge(sems.h["s_st"], sems.at("s_st"))
        return nc

    with contextlib.ExitStack() as ctx:
        sb = lambda nm, shp, dt=F32: ctx.enter_context(nc.sbuf_tensor(nm, shp, dt)).ap()
        pst = lambda nm, shp: ctx.enter_context(nc.psum_tensor(nm, shp, F32)).ap()
        wstg = sb("wstg", [128, 2 * 1024])
        wsp = {}
        for wn in ("q", "k", "v"):
            wsp[wn] = (sb(f"w{wn}h", [128, 8 * 1024], BF16),
                       sb(f"w{wn}l", [128, 8 * 1024], BF16))
        xin_s = sb("xin_s", [128, 8 * 512])
        xhi = [sb(f"xhi{i}", [128, 8 * 512], BF16) for i in range(2)]
        xlo = [sb(f"xlo{i}", [128, 8 * 512], BF16) for i in range(2)]
        qTo_sb = sb("qTo_sb", [128, 8 * 512])
        kTo_sb = sb("kTo_sb", [128, 8 * 512])
        vo_sb = sb("vo_sb", [128, 4 * 1024])
        ps_pj = [pst(f"ps_pj{i}", [128, 512]) for i in range(4)]

        st20 = sems.at("s_st2")
        std0 = sems.at("s_std")
        var0 = sems.at("s_var")
        wspl0 = sems.at("s_wq")
        sq0p1 = sems.at("s_sq")
        with nc.Block() as blk:
            @blk.sync
            def _(sync):
                for nm in ("sxnA", "sxnB", "sxtA", "sxtB"):
                    sync.wait_ge(sems.h[nm], sems.at(nm))
                for wi, w_h in enumerate((l0_wqT, l0_wkT, l0_wvT)):
                    for hf in range(4):
                        WH = 4 * wi + hf
                        if WH >= 1:
                            sync.wait_ge(sems.h["s_wq"], wspl0 + WH)
                        sems.inc(sync.dma_start(
                            out=wstg.rearrange("p (k d) -> p k d", d=1024),
                            in_=w_h.rearrange("(k p) d -> p k d", p=128)[
                                :, hf * 2:(hf + 1) * 2]),
                            "wnA" if WH % 2 == 0 else "wnB", 16)
                for tc in range(NTC):
                    if tc >= 1:
                        sync.wait_ge(sems.h["s_std"], std0 + tc)
                        sync.wait_ge(sems.h["s_var"], var0 + tc)
                    sems.inc(sync.dma_start(
                        out=xin_s.rearrange("p (k t) -> p k t", t=512),
                        in_=xnT_h_t[:, :, tc * 512:(tc + 1) * 512]),
                        "xiA" if tc % 2 == 0 else "xiB", 16)

            @blk.scalar
            def _(scalar):
                for wi, wn in enumerate(("q", "k", "v")):
                    for hf in range(4):
                        WH = 4 * wi + hf
                        scalar.wait_ge(sems.h["wnA" if WH % 2 == 0 else "wnB"],
                                       16 * (WH // 2 + 1))
                        sems.inc(scalar.copy(
                            out=wsp[wn][0][:, hf * 2048:(hf + 1) * 2048],
                            in_=wstg), "s_sq", 1)
                for tc in range(NTC):
                    scalar.wait_ge(sems.h["xiA" if tc % 2 == 0 else "xiB"],
                                   16 * (tc // 2 + 1))
                    if tc >= 2:
                        scalar.wait_ge(sems.h["s_pj"], 24 * (tc - 1))
                    sems.inc(scalar.copy(out=xhi[tc % 2], in_=xin_s), "s_std", 1)

            @blk.vector
            def _(vector):
                for wi, wn in enumerate(("q", "k", "v")):
                    for hf in range(4):
                        WH = 4 * wi + hf
                        vector.wait_ge(sems.h["s_sq"], sq0p1 + WH + 1)
                        sems.inc(vector.tensor_sub(
                            wsp[wn][1][:, hf * 2048:(hf + 1) * 2048],
                            wstg, wsp[wn][0][:, hf * 2048:(hf + 1) * 2048]),
                            "s_wq", 1)
                for tc in range(NTC):
                    vector.wait_ge(sems.h["s_std"], std0 + tc + 1)
                    sems.inc(vector.tensor_sub(xlo[tc % 2], xin_s, xhi[tc % 2]),
                             "s_var", 1)
                    if tc >= 1:
                        vector.wait_ge(sems.h["sq1s"], 16 * tc)
                        vector.wait_ge(sems.h["sk1s"], 16 * tc)
                        vector.wait_ge(sems.h["sv1s"], 16 * tc)
                    for ti in range(24):
                        G = 24 * tc + ti
                        vector.wait_ge(sems.h["s_pj"], G + 1)
                        pb = ps_pj[G % 4]
                        if ti < 8:
                            dst = qTo_sb[:, ti * 512:(ti + 1) * 512]
                        elif ti < 16:
                            dst = kTo_sb[:, (ti - 8) * 512:(ti - 7) * 512]
                        else:
                            su, oh = (ti - 16) // 2, (ti - 16) % 2
                            dst = vo_sb[:, su * 1024 + oh * 512: su * 1024 + (oh + 1) * 512]
                        sems.inc(vector.tensor_copy(dst, pb),
                                 "s_pjA" if ti % 2 == 0 else "s_pjB", 1)

            @blk.tensor
            def _(tensor):
                tensor.wait_ge(sems.h["s_wq"], wspl0 + 12)
                for tc in range(NTC):
                    tensor.wait_ge(sems.h["s_std"], std0 + tc + 1)
                    tensor.wait_ge(sems.h["s_var"], var0 + tc + 1)
                    xh, xl = xhi[tc % 2], xlo[tc % 2]
                    tiles = [("q", oc) for oc in range(8)] + \
                            [("k", oc) for oc in range(8)] + \
                            [("v", (su, oh)) for su in range(4) for oh in range(2)]
                    for ti, (kindt, idx) in enumerate(tiles):
                        G = 24 * tc + ti
                        if G >= 4:
                            pG = G - 4
                            sname = "s_pjA" if pG % 2 == 0 else "s_pjB"
                            tensor.wait_ge(sems.h[sname], pG // 2 + 1)
                        pb = ps_pj[G % 4]
                        if kindt in ("q", "k"):
                            wh, wl = wsp[kindt]
                            oc = idx
                            np_ = 0
                            for k in range(8):
                                wsl = slice(k * 1024 + oc * 128,
                                            k * 1024 + (oc + 1) * 128)
                                xsl = slice(k * 512, (k + 1) * 512)
                                for a, b in ((wh, xh), (wh, xl), (wl, xh)):
                                    ins = tensor.matmul(
                                        pb, a[:, wsl], b[:, xsl],
                                        start=(np_ == 0), stop=(np_ == 23))
                                    np_ += 1
                        else:
                            su, oh = idx
                            np_ = 0
                            for k in range(8):
                                xsl = slice(k * 512 + su * 128,
                                            k * 512 + (su + 1) * 128)
                                wsl = slice(k * 1024 + oh * 512,
                                            k * 1024 + (oh + 1) * 512)
                                for a, b in ((xh, wsp["v"][0]), (xh, wsp["v"][1]),
                                             (xl, wsp["v"][0])):
                                    ins = tensor.matmul(
                                        pb, a[:, xsl], b[:, wsl],
                                        start=(np_ == 0), stop=(np_ == 23))
                                    np_ += 1
                        sems.inc(ins, "s_pj", 1)

            @blk.gpsimd
            def _(g):
                for tc in range(NTC):
                    g.wait_ge(sems.h["s_pjA"], 12 * (tc + 1))
                    g.wait_ge(sems.h["s_pjB"], 12 * (tc + 1))
                    sems.inc(g.dma_start(
                        out=qT_h_t[:, :, tc * 512:(tc + 1) * 512],
                        in_=qTo_sb.rearrange("p (k t) -> p k t", t=512)), "sq1s", 16)
                    sems.inc(g.dma_start(
                        out=kT_h_t[:, :, tc * 512:(tc + 1) * 512],
                        in_=kTo_sb.rearrange("p (k t) -> p k t", t=512)), "sk1s", 16)
                    sems.inc(g.dma_start(
                        out=v_h_t[tc],
                        in_=vo_sb.rearrange("p (a d) -> p a d", d=1024)), "sv1s", 16)

    with contextlib.ExitStack() as ctx:
        sb = lambda nm, shp, dt=F32: ctx.enter_context(nc.sbuf_tensor(nm, shp, dt)).ap()
        pst = lambda nm, shp: ctx.enter_context(nc.psum_tensor(nm, shp, F32)).ap()
        wo_sb = sb("wo_sb", [128, 8 * 1024])
        qb = [sb(f"qb{i}", [128, 8 * 128]) for i in range(3)]
        kroll = [sb(f"kroll{i}", [128, 8 * 128]) for i in range(3)]
        vroll = [sb(f"vroll{i}", [128, 1024]) for i in range(3)]
        p_sb = sb("p_sb", [128, 16 * 256])
        den_sb = sb("den_sb", [128, 16])
        rden_sb = sb("rden_sb", [128, 16])
        pT_sb = sb("pT_sb", [128, 16 * 256])
        attnT_sb = sb("attnT_sb", [128, 8 * 512])
        xr_sb = [sb(f"xr_sb{i}", [128, 8 * 512]) for i in range(2)]
        p1_sb = sb("p1_sb", [128, 8 * 512])
        maskG = sb("maskG", [128, 256])
        mask0 = sb("mask0", [128, 256])
        ps_sc = [pst(f"ps_sc{i}", [128, 256]) for i in range(2)]
        ps_pt = [pst(f"ps_pt{i}", [128, 256]) for i in range(2)]
        ps_ov = [pst(f"ps_ov{i}", [128, 128]) for i in range(2)]
        ps_wo = [pst(f"ps_wo{i}", [128, 512]) for i in range(2)]

        ld0 = sems.at("s_ld")
        cst0 = sems.at("s_cst") + 1
        NCHL = nch_limit if nch_limit is not None else NCH

        with nc.Block() as blk:
            @blk.sync
            def _(sync):
                sync.wait_ge(sems.h["s_st2"], sems.at("s_st2"))
                sems.inc(sync.dma_start(
                    out=wo_sb.rearrange("p (k d) -> p k d", d=1024),
                    in_=l0_woT.rearrange("(k p) d -> p k d", p=128)), "s_ld", 16)
                sems.inc(sync.dma_start(out=maskG, in_=maskG_c.ap()), "s_ld", 16)
                sems.inc(sync.dma_start(out=mask0, in_=mask0_c.ap()), "s_ld", 16)
                cst0s = sems.at("s_cst")
                for c in range(NCHL):
                    if c == 2:
                        sync.wait_ge(sems.h["s_cst"], cst0s + 1)
                    if c >= 3:
                        sync.wait_ge(sems.h["s_sc"], 16 * (c - 1))
                        sync.wait_ge(sems.h["s_av"], 16 * (c - 1))
                    qkv = ("qkvA", "qkvB", "qkvC")[c % 3]
                    sems.inc(sync.dma_start(
                        out=qb[c % 3].rearrange("p (k t) -> p k t", t=128),
                        in_=qT_h_t[:, :, c * 128:(c + 1) * 128]), qkv, 16)
                    sems.inc(sync.dma_start(
                        out=kroll[c % 3].rearrange("p (k t) -> p k t", t=128),
                        in_=kT_h_t[:, :, c * 128:(c + 1) * 128]), qkv, 16)
                    sems.inc(sync.dma_start(
                        out=vroll[c % 3], in_=v_h[c * 128:(c + 1) * 128, :]), qkv, 16)
                    if c % 4 == 0:
                        grp = c // 4
                        if grp >= 2:
                            sync.wait_ge(sems.h["s_p1ev"], 8 * (grp - 1))
                        sems.inc(sync.dma_start(
                            out=xr_sb[grp % 2].rearrange("p (k t) -> p k t", t=512),
                            in_=xT_h_t[:, :, grp * 512:(grp + 1) * 512]),
                            "xrA" if grp % 2 == 0 else "xrB", 16)

            @blk.vector
            def _(vector):
                vector.memset(kroll[2], 0.0)
                sems.inc(vector.memset(vroll[2], 0.0), "s_cst", 1)
                for c in range(NCHL):
                    if p1b_stage < 2:
                        break
                    mask = mask0 if c % 16 == 0 else maskG
                    for h in range(16):
                        E = 16 * c + h
                        vector.wait_ge(sems.h["s_exp"], E + 1)
                        vector.tensor_mul(p_sb[:, h * 256:(h + 1) * 256],
                                          p_sb[:, h * 256:(h + 1) * 256], mask)
                        ins = vector.reduce_sum(den_sb[:, h:h + 1],
                                                p_sb[:, h * 256:(h + 1) * 256],
                                                axis=AX.X)
                        if h == 15:
                            sems.inc(ins, "s_vh", 1)
                    vector.wait_ge(sems.h["s_vh"], sems.at("s_vh"))
                    sems.inc(vector.reciprocal(rden_sb, den_sb), "s_vh", 1)
                    vector.wait_ge(sems.h["s_vh"], sems.at("s_vh"))
                    for h in range(16):
                        sems.inc(vector.tensor_scalar_mul(
                            p_sb[:, h * 256:(h + 1) * 256],
                            p_sb[:, h * 256:(h + 1) * 256],
                            rden_sb[:, h:h + 1]), "s_pf", 1)
                    for hp in range(8):
                        if p1b_stage < 4:
                            break
                        O = 8 * c + hp
                        if c % 4 == 0 and c >= 4 and hp == 0:
                            vector.wait_ge(sems.h["s_wo"], 8 * (c // 4))
                        vector.wait_ge(sems.h["s_av"], 16 * c + 2 * hp + 2)
                        sems.inc(vector.tensor_copy(
                            attnT_sb[:, hp * 512 + (c % 4) * 128: hp * 512 + (c % 4 + 1) * 128],
                            ps_ov[hp % 2]), "s_oev", 1)
                    if c % 4 == 3 and p1b_stage >= 5:
                        grp = c // 4
                        vector.wait_ge(sems.h["xrA" if grp % 2 == 0 else "xrB"],
                                       16 * (grp // 2 + 1))
                        if grp >= 1:
                            vector.wait_ge(sems.h["sp1"], 16 * grp)
                        for oc in range(8):
                            vector.wait_ge(sems.h["s_wo"], 8 * grp + oc + 1)
                            sems.inc(vector.tensor_add(
                                p1_sb[:, oc * 512:(oc + 1) * 512],
                                ps_wo[oc % 2],
                                xr_sb[grp % 2][:, oc * 512:(oc + 1) * 512]), "s_p1ev", 1)

            @blk.scalar
            def _(scalar):
                for c in range(NCHL):
                    for h in range(16):
                        E = 16 * c + h
                        scalar.wait_ge(sems.h["s_sc"], E + 1)
                        if c >= 1 and p1b_stage >= 3:
                            scalar.wait_ge(sems.h["s_pt"], 16 * (c - 1) + h + 1)
                        sems.inc(scalar.activation(
                            out=p_sb[:, h * 256:(h + 1) * 256],
                            in_=ps_sc[h % 2],
                            func=AF.Exp, scale=cst[:, 2:3]), "s_exp", 1)
                    for h in range(16):
                        if p1b_stage < 3:
                            break
                        E = 16 * c + h
                        scalar.wait_ge(sems.h["s_pt"], E + 1)
                        if c >= 1 and p1b_stage >= 4:
                            scalar.wait_ge(sems.h["s_av"], 16 * (c - 1) + h + 1)
                        sems.inc(scalar.copy(
                            out=pT_sb[:, h * 256:(h + 1) * 256],
                            in_=ps_pt[h % 2]), "s_ptev", 1)

            @blk.tensor
            def _(tensor):
                tensor.wait_ge(sems.h["s_ld"], ld0 + 48)
                tensor.wait_ge(sems.h["s_cst"], cst0)
                for c in range(NCHL):
                    cm, pm = c % 3, (c - 1) % 3
                    tensor.wait_ge(sems.h[("qkvA", "qkvB", "qkvC")[c % 3]],
                                   48 * (c // 3 + 1))
                    for h in range(16):
                        E = 16 * c + h
                        if E >= 2:
                            tensor.wait_ge(sems.h["s_exp"], E - 1)
                        qs = qb[cm][(h % 2) * 64:(h % 2) * 64 + 64,
                                    (h // 2) * 128:(h // 2 + 1) * 128]
                        tensor.matmul(
                            ps_sc[h % 2][:, 0:128], qs,
                            kroll[pm][(h % 2) * 64:(h % 2) * 64 + 64,
                                      (h // 2) * 128:(h // 2 + 1) * 128],
                            start=True, stop=True)
                        sems.inc(tensor.matmul(
                            ps_sc[h % 2][:, 128:256], qs,
                            kroll[cm][(h % 2) * 64:(h % 2) * 64 + 64,
                                      (h // 2) * 128:(h // 2 + 1) * 128],
                            start=True, stop=True), "s_sc", 1)
                    for h in range(16):
                        if p1b_stage < 3:
                            break
                        E = 16 * c + h
                        tensor.wait_ge(sems.h["s_pf"], E + 1)
                        if E >= 2:
                            tensor.wait_ge(sems.h["s_ptev"], E - 1)
                        tensor.transpose(ps_pt[h % 2][:, 0:128],
                                         p_sb[:, h * 256:h * 256 + 128], ident)
                        sems.inc(tensor.transpose(
                            ps_pt[h % 2][:, 128:256],
                            p_sb[:, h * 256 + 128:(h + 1) * 256], ident), "s_pt", 1)
                    for h in range(16):
                        if p1b_stage < 4:
                            break
                        E = 16 * c + h
                        hp = h // 2
                        O = 8 * c + hp
                        tensor.wait_ge(sems.h["s_ptev"], E + 1)
                        if h % 2 == 0 and O >= 2:
                            tensor.wait_ge(sems.h["s_oev"], O - 1)
                        dst = ps_ov[hp % 2][(h % 2) * 64:(h % 2) * 64 + 64, :]
                        tensor.matmul(dst, vroll[pm][:, h * 64:(h + 1) * 64],
                                      pT_sb[:, h * 256:h * 256 + 128],
                                      start=True, stop=False)
                        sems.inc(tensor.matmul(
                            dst, vroll[cm][:, h * 64:(h + 1) * 64],
                            pT_sb[:, h * 256 + 128:(h + 1) * 256],
                            start=False, stop=True), "s_av", 1)
                    if c % 4 == 3 and p1b_stage >= 5:
                        grp = c // 4
                        tensor.wait_ge(sems.h["s_oev"], 8 * (c + 1))
                        for oc in range(8):
                            W = 8 * grp + oc
                            if W >= 2:
                                tensor.wait_ge(sems.h["s_p1ev"], W - 1)
                            for k in range(8):
                                ins = tensor.matmul(
                                    ps_wo[oc % 2],
                                    wo_sb[:, k * 1024 + oc * 128: k * 1024 + (oc + 1) * 128],
                                    attnT_sb[:, k * 512:(k + 1) * 512],
                                    start=(k == 0), stop=(k == 7))
                            sems.inc(ins, "s_wo", 1)

            @blk.gpsimd
            def _(g):
                for grp in range(NCHL // 4 if p1b_stage >= 5 else 0):
                    g.wait_ge(sems.h["s_p1ev"], 8 * (grp + 1))
                    sems.inc(g.dma_start(
                        out=p1T_h_t[:, :, grp * 512:(grp + 1) * 512],
                        in_=p1_sb.rearrange("p (k t) -> p k t", t=512)), "sp1", 16)

    if dev_phase is not None and dev_phase == 2:
        with nc.Block() as blk:
            @blk.gpsimd
            def _(g):
                g.wait_ge(sems.h["s_cc"], 1)
                g.wait_ge(sems.h["s_st"], sems.at("s_st"))
                sems.inc(g.dma_start(out=gmax_dbg, in_=ccx_out.ap()), "s_st", 16)
                g.wait_ge(sems.h["s_st"], sems.at("s_st"))
        return nc

    h_h_t = h_h.rearrange("(k p) t -> p k t", p=128)
    x1T_h_t = x1T_h.rearrange("(k p) t -> p k t", p=128)

    with contextlib.ExitStack() as ctx:
        sb = lambda nm, shp, dt=F32: ctx.enter_context(nc.sbuf_tensor(nm, shp, dt)).ap()
        pst = lambda nm, shp: ctx.enter_context(nc.psum_tensor(nm, shp, F32)).ap()
        wgqT = sb("wgqT", [128, 8 * I_FFN], BF16)
        wuqT = sb("wuqT", [128, 8 * I_FFN], BF16)
        wnat = [sb(f"wnat{i}", [128, 1024]) for i in range(2)]
        wq_s = sb("wq_s", [128, 1024])
        ws_g = sb("ws_g", [128, 22])
        ws_u = sb("ws_u", [128, 22])
        wsx_g = sb("wsx_g", [128, 22])
        wsx_u = sb("wsx_u", [128, 22])
        wiv = sb("wiv", [128, 2])
        xq_in = [sb(f"xq_in{i}", [128, 8 * 512]) for i in range(2)]
        xq_bf = [sb(f"xq_bf{i}", [128, 8 * 512], BF16) for i in range(2)]
        uq_s = sb("uq_s", [128, 512])
        h_ch = sb("h_ch", [128, 22 * 512])
        gs_t = sb("gs_t", [128, 512])
        us_t = sb("us_t", [128, 512])
        hm_all = sb("hm_all", [128, 22])
        hmax_run = sb("hmax_run", [128, 1])
        hmax_row = sb("hmax_row", [1, 128])
        hmax_sb = sb("hmax_sb", [1, 1])
        cx_sb = sb("cx_sb", [1, 1])
        gm_sb = sb("gm_sb", [1, 1])
        sig_sb = sb("sig_sb", [1, 1])
        t_sb = sb("t_sb", [1, 1])
        sx1 = sb("sx1", [1, 1])
        isx1 = sb("isx1", [1, 1])
        sxb = sb("sxb", [128, 1])
        invsx = sb("invsx", [128, 1])
        ps_w = [pst(f"ps_w{i}", [128, 512]) for i in range(2)]
        ps_g = [pst(f"ps_g{i}", [128, 512]) for i in range(2)]
        ps_u = [pst(f"ps_u{i}", [128, 512]) for i in range(2)]
        ps_b = pst("ps_b", [128, 128])

        ld0 = sems.at("s_ld3")
        st0 = sems.at("s_st")
        misc0 = sems.at("s_misc")
        sq0 = sems.at("s_sq")
        tpA0 = sems.at("s_tpA")
        wq0p2 = sems.at("s_wq")
        wnA0p2 = sems.at("wnA")
        wnB0p2 = sems.at("wnB")
        cst0p2 = sems.at("s_cst") + 1
        with nc.Block() as blk:
            @blk.sync
            def _(sync):
                sems.inc(sync.dma_start(out=identb, in_=identb_c.ap()), "s_ld3", 16)
                sems.inc(sync.dma_start(out=cx_sb, in_=l0_cx), "s_ld3", 16)
                for wi, w_src in enumerate((l0_wg, l0_wu)):
                    for it in range(IT):
                        W = wi * IT + it
                        if W >= 2:
                            sync.wait_ge(sems.h["s_wq"], wq0p2 + W - 1)
                            sync.wait_ge(sems.h["s_tpA"], tpA0 + 2 * (W - 1))
                        sems.inc(sync.dma_start(
                            out=wnat[W % 2],
                            in_=w_src[it * 128:(it + 1) * 128, :]),
                            "wnA" if W % 2 == 0 else "wnB", 16)
                for tc in range(NTC):
                    if tc >= 2:
                        sync.wait_ge(sems.h["s_qx"], 8 * (tc - 1))
                    sems.inc(sync.dma_start(
                        out=xq_in[tc % 2].rearrange("p (k t) -> p k t", t=512),
                        in_=xnT_h_t[:, :, tc * 512:(tc + 1) * 512]),
                        "xqA" if tc % 2 == 0 else "xqB", 16)

            @blk.gpsimd
            def _(g):
                g.wait_ge(sems.h["s_cc"], 1)
                sems.inc(g.dma_start(out=gm_sb, in_=ccx_out.ap()), "gmld", 16)
                for tc in range(NTC):
                    g.wait_ge(sems.h["s_guev"], 22 * (tc + 1))
                    sems.inc(g.dma_start(
                        out=h_h_t[:, :, tc * 512:(tc + 1) * 512],
                        in_=h_ch.rearrange("p (k t) -> p k t", t=512)), "shs", 16)
                g.wait_ge(sems.h["s_hm"], NTC + 3)
                sems.inc(g.dma_start(out=cch_in.ap(), in_=hmax_sb), "s_st", 16)
                g.wait_ge(sems.h["s_st"], sems.at("s_st"))
                sems.inc(g.collective_compute(
                    "AllReduce", ALU.max,
                    replica_groups=[list(range(NCORES))],
                    ins=[cch_in.ap()], outs=[cch_out.ap()]), "s_cc", 1)

            @blk.scalar
            def _(scalar):
                scalar.wait_ge(sems.h["s_ld3"], ld0 + 32)
                sems.inc(scalar.activation(out=sig_sb, in_=cx_sb,
                                           func=AF.Sigmoid), "s_scl", 1)
                for wi in range(2):
                    wdst = wgqT if wi == 0 else wuqT
                    for it in range(IT):
                        W = wi * IT + it
                        for g4 in range(2):
                            TP4 = 2 * W + g4
                            scalar.wait_ge(sems.h["s_tpA"], tpA0 + TP4 + 1)
                            dst = wdst.rearrange("p (k i) -> p k i", i=I_FFN)[
                                :, g4 * 4:(g4 + 1) * 4, it * 128:(it + 1) * 128]
                            sems.inc(scalar.copy(
                                out=dst,
                                in_=ps_w[TP4 % 2].rearrange("p (a b) -> p a b", b=128)),
                                "s_wT", 1)
                for tc in range(NTC):
                    for ic in range(IT):
                        GU = IT * tc + ic
                        if GU == 0:
                            scalar.wait_ge(sems.h["s_scl"], 6)
                        scalar.wait_ge(sems.h["s_gu"], GU + 1)
                        if GU >= 1:
                            scalar.wait_ge(sems.h["s_guev"], GU)
                        sems.inc(scalar.activation(
                            out=gs_t, in_=ps_g[GU % 2],
                            func=AF.Silu, scale=wsx_g[:, ic:ic + 1]), "s_guA", 1)

            @blk.vector
            def _(vector):
                sems.inc(vector.memset(ones_r, 1.0), "s_cst", 1)
                for wi in range(2):
                    ws_dst = ws_g if wi == 0 else ws_u
                    for it in range(IT):
                        W = wi * IT + it
                        vector.wait_ge(
                            sems.h["wnA" if W % 2 == 0 else "wnB"],
                            (wnA0p2 if W % 2 == 0 else wnB0p2) + 16 * (W // 2 + 1))
                        sems.inc(vector.reduce_max(
                            ws_dst[:, it:it + 1], wnat[W % 2], axis=AX.X,
                            apply_absolute_value=True), "s_vh", 1)
                        vector.wait_ge(sems.h["s_vh"], sems.at("s_vh"))
                        sems.inc(vector.reciprocal(wiv[:, W % 2:W % 2 + 1],
                                                   ws_dst[:, it:it + 1]), "s_vh", 1)
                        vector.wait_ge(sems.h["s_vh"], sems.at("s_vh"))
                        vector.tensor_scalar_mul(
                            wq_s, wnat[W % 2], wiv[:, W % 2:W % 2 + 1])
                        vector.tensor_scalar(wq_s, wq_s, QMAX, RC, ALU.mult, ALU.add)
                        vector.tensor_scalar(wq_s, wq_s, RC, QMAX, ALU.subtract, ALU.min)
                        sems.inc(vector.tensor_scalar(
                            wnat[W % 2], wq_s, -QMAX - 1.0, None, ALU.max),
                            "s_wq", 1)
                vector.wait_ge(sems.h["gmld"], 16)
                vector.wait_ge(sems.h["s_scl"], 1)
                sems.inc(vector.tensor_mul(t_sb, sig_sb, gm_sb), "s_vh", 1)
                vector.wait_ge(sems.h["s_vh"], sems.at("s_vh"))
                sems.inc(vector.tensor_scalar_mul(sx1, t_sb, 1.0 / QMAX), "s_vh", 1)
                vector.wait_ge(sems.h["s_vh"], sems.at("s_vh"))
                sems.inc(vector.reciprocal(isx1, sx1), "s_scl", 1)
                vector.wait_ge(sems.h["s_bc"], 2)
                vector.tensor_copy(sxb, ps_b[:, 0:1])
                sems.inc(vector.tensor_copy(invsx, ps_b[:, 1:2]), "s_vh", 1)
                vector.wait_ge(sems.h["s_vh"], sems.at("s_vh"))
                vector.wait_ge(sems.h["s_wq"], wq0p2 + 2 * IT)
                vector.tensor_scalar(wsx_g, ws_g, sxb, 1.0 / QMAX, ALU.mult, ALU.mult)
                sems.inc(vector.tensor_scalar(
                    wsx_u, ws_u, sxb, 1.0 / QMAX, ALU.mult, ALU.mult), "s_scl", 4)
                vector.memset(hmax_run, 0.0)
                for tc in range(NTC):
                    vector.wait_ge(sems.h["xqA" if tc % 2 == 0 else "xqB"],
                                   16 * (tc // 2 + 1))
                    for k in range(KT):
                        Q = 8 * tc + k
                        vector.tensor_scalar(
                            uq_s, xq_in[tc % 2][:, k * 512:(k + 1) * 512],
                            invsx, RC, ALU.mult, ALU.add)
                        vector.tensor_scalar(uq_s, uq_s, RC, QMAX,
                                             ALU.subtract, ALU.min)
                        sems.inc(vector.tensor_scalar_max(
                            xq_bf[tc % 2][:, k * 512:(k + 1) * 512],
                            uq_s, -QMAX - 1.0), "s_qx", 1)
                    if tc >= 1:
                        vector.wait_ge(sems.h["shs"], 16 * tc)
                    for ic in range(IT):
                        GU = IT * tc + ic
                        vector.wait_ge(sems.h["s_gu"], GU + 1)
                        vector.tensor_scalar_mul(us_t, ps_u[GU % 2],
                                                 wsx_u[:, ic:ic + 1])
                        vector.wait_ge(sems.h["s_guA"], GU + 1)
                        sems.inc(vector.tensor_mul(
                            h_ch[:, ic * 512:(ic + 1) * 512], gs_t, us_t),
                            "s_guev", 1)
                        vector.reduce_max(hm_all[:, ic:ic + 1],
                                          h_ch[:, ic * 512:(ic + 1) * 512],
                                          axis=AX.X, apply_absolute_value=True)
                    sems.inc(vector.engine_nop(), "s_vh", 1)
                    vector.wait_ge(sems.h["s_vh"], sems.at("s_vh"))
                    sems.inc(vector.reduce_max(hm_all[:, 0:1], hm_all, axis=AX.X),
                             "s_vh", 1)
                    vector.wait_ge(sems.h["s_vh"], sems.at("s_vh"))
                    sems.inc(vector.tensor_max(hmax_run, hmax_run, hm_all[:, 0:1]),
                             "s_hm", 1)
                vector.wait_ge(sems.h["s_hm"], NTC + 1)
                sems.inc(vector.tensor_copy(hmax_row, ps_b[0:1, 0:128]), "s_vh", 1)
                vector.wait_ge(sems.h["s_vh"], sems.at("s_vh"))
                sems.inc(vector.reduce_max(hmax_sb, hmax_row, axis=AX.X), "s_hm", 2)

            @blk.tensor
            def _(tensor):
                tensor.wait_ge(sems.h["s_ld3"], ld0 + 16)
                for wi in range(2):
                    wdst = wgqT if wi == 0 else wuqT
                    for it in range(IT):
                        W = wi * IT + it
                        tensor.wait_ge(sems.h["s_wq"], wq0p2 + W + 1)
                        for k in range(KT):
                            TP = KT * W + k
                            if TP >= 8:
                                tensor.wait_ge(sems.h["s_wT"], (TP - 8) // 4 + 1)
                            ins = tensor.transpose(
                                ps_w[(TP // 4) % 2][:, (k % 4) * 128:(k % 4 + 1) * 128],
                                wnat[W % 2][:, k * 128:(k + 1) * 128], ident)
                            if k % 4 == 3:
                                sems.inc(ins, "s_tpA", 1)
                tensor.wait_ge(sems.h["s_cst"], cst0p2)
                tensor.wait_ge(sems.h["s_scl"], 2)
                tensor.matmul(ps_b[:, 0:1], ones_r, sx1, start=True, stop=True)
                sems.inc(tensor.matmul(ps_b[:, 1:2], ones_r, isx1,
                                       start=True, stop=True), "s_bc", 2)
                tensor.wait_ge(sems.h["s_wT"], 4 * IT)
                for tc in range(NTC):
                    for ic in range(IT):
                        GU = IT * tc + ic
                        if ic == 0:
                            tensor.wait_ge(sems.h["s_qx"], 8 * (tc + 1))
                        if GU >= 2:
                            tensor.wait_ge(sems.h["s_guA"], GU - 1)
                            tensor.wait_ge(sems.h["s_guev"], GU - 1)
                        for k in range(KT):
                            tensor.matmul(
                                ps_g[GU % 2],
                                wgqT[:, k * I_FFN + ic * 128: k * I_FFN + (ic + 1) * 128],
                                xq_bf[tc % 2][:, k * 512:(k + 1) * 512],
                                start=(k == 0), stop=(k == 7))
                        for k in range(KT):
                            ins = tensor.matmul(
                                ps_u[GU % 2],
                                wuqT[:, k * I_FFN + ic * 128: k * I_FFN + (ic + 1) * 128],
                                xq_bf[tc % 2][:, k * 512:(k + 1) * 512],
                                start=(k == 0), stop=(k == 7))
                        sems.inc(ins, "s_gu", 1)
                tensor.wait_ge(sems.h["s_hm"], NTC)
                sems.inc(tensor.transpose(ps_b[0:1, 0:128], hmax_run, ident),
                         "s_hm", 1)


    if dev_phase is not None and dev_phase == 3:
        with nc.Block() as blk:
            @blk.gpsimd
            def _(g):
                g.wait_ge(sems.h["s_cc"], 2)
                g.wait_ge(sems.h["s_st"], sems.at("s_st"))
                sems.inc(g.dma_start(out=gmax_dbg, in_=cch_out.ap()), "s_st", 16)
                g.wait_ge(sems.h["s_st"], sems.at("s_st"))
        return nc

    with contextlib.ExitStack() as ctx:
        sb = lambda nm, shp, dt=F32: ctx.enter_context(nc.sbuf_tensor(nm, shp, dt)).ap()
        pst = lambda nm, shp: ctx.enter_context(nc.psum_tensor(nm, shp, F32)).ap()
        wdqT = sb("wdqT", [128, IT * 1024], BF16)
        wdn = [sb(f"wdn{i}", [128, I_FFN]) for i in range(2)]
        wdq_s = sb("wdq_s", [128, I_FFN])
        ws_d = sb("ws_d", [128, 8])
        wsx_d = sb("wsx_d", [128, 8])
        wiv_d = sb("wiv_d", [128, 2])
        h_in = sb("h_in", [128, IT * 512])
        qh_bf = [sb(f"qh_bf{i}", [128, IT * 512], BF16) for i in range(2)]
        uq3 = sb("uq3", [128, 512])
        p1_in = sb("p1_in", [128, 8 * 512])
        x1_sb = sb("x1_sb", [128, 8 * 512])
        ch_sb = sb("ch_sb", [1, 1])
        cd_sb = sb("cd_sb", [1, 1])
        sigd_sb = sb("sigd_sb", [1, 1])
        td_sb = sb("td_sb", [1, 1])
        sh1 = sb("sh1", [1, 1])
        ish1 = sb("ish1", [1, 1])
        shb = sb("shb", [128, 1])
        invsh = sb("invsh", [128, 1])
        ps_w3 = [pst(f"ps_w3{i}", [128, 512]) for i in range(2)]
        ps_d = [pst(f"ps_d{i}", [128, 512]) for i in range(2)]
        ps_b3 = pst("ps_b3", [128, 128])

        ld0 = sems.at("s_ld3")
        st0 = sems.at("s_st")
        sq0 = sems.at("s_sq")
        tpA0 = sems.at("s_tpA")
        wq0 = sems.at("s_wq")
        wT0 = sems.at("s_wT")
        scl0 = sems.at("s_scl")
        qx0 = sems.at("s_qx")
        with nc.Block() as blk:
            @blk.sync
            def _(sync):
                sems.inc(sync.dma_start(out=cd_sb, in_=l0_cd), "s_ld3", 16)
                for nm in ("shs",):
                    sync.wait_ge(sems.h[nm], sems.at(nm))
                for oc in range(8):
                    if oc >= 2:
                        sync.wait_ge(sems.h["s_wq"], wq0 + oc - 1)
                        sync.wait_ge(sems.h["s_tpA"], tpA0 + 6 * (oc - 1))
                    sems.inc(sync.dma_start(
                        out=wdn[oc % 2],
                        in_=l0_wd[oc * 128:(oc + 1) * 128, :]),
                        "wdA" if oc % 2 == 0 else "wdB", 16)
                for tc in range(NTC):
                    if tc >= 1:
                        sync.wait_ge(sems.h["s_qx"], qx0 + IT * tc)
                    sems.inc(sync.dma_start(
                        out=h_in.rearrange("p (k t) -> p k t", t=512),
                        in_=h_h_t[:, :, tc * 512:(tc + 1) * 512]), "hin1", 16)
                    if tc >= 1:
                        sync.wait_ge(sems.h["s_x1ev"], 8 * tc)
                    sems.inc(sync.dma_start(
                        out=p1_in.rearrange("p (k t) -> p k t", t=512),
                        in_=p1T_h_t[:, :, tc * 512:(tc + 1) * 512]), "p1in1", 16)

            @blk.scalar
            def _(scalar):
                scalar.wait_ge(sems.h["s_ld3"], ld0 + 16)
                sems.inc(scalar.activation(out=sigd_sb, in_=cd_sb,
                                           func=AF.Sigmoid), "s_scl", 1)
                for oc in range(8):
                    for g6 in range(6):
                        TP4 = 6 * oc + g6
                        nblk = 4 if g6 < 5 else 2
                        scalar.wait_ge(sems.h["s_tpA"], tpA0 + TP4 + 1)
                        dst = wdqT.rearrange("p (k i) -> p k i", i=1024)[
                            :, g6 * 4:g6 * 4 + nblk, oc * 128:(oc + 1) * 128]
                        sems.inc(scalar.copy(
                            out=dst,
                            in_=ps_w3[TP4 % 2].rearrange(
                                "p (a b) -> p a b", b=128)[:, 0:nblk]),
                            "s_wT", 1)

            @blk.vector
            def _(vector):
                for oc in range(8):
                    W = oc
                    vector.wait_ge(sems.h["wdA" if oc % 2 == 0 else "wdB"],
                                   16 * (oc // 2 + 1))
                    sems.inc(vector.reduce_max(
                        ws_d[:, oc:oc + 1], wdn[oc % 2], axis=AX.X,
                        apply_absolute_value=True), "s_vh", 1)
                    vector.wait_ge(sems.h["s_vh"], sems.at("s_vh"))
                    sems.inc(vector.reciprocal(wiv_d[:, oc % 2:oc % 2 + 1],
                                               ws_d[:, oc:oc + 1]), "s_vh", 1)
                    vector.wait_ge(sems.h["s_vh"], sems.at("s_vh"))
                    vector.tensor_scalar_mul(wdq_s, wdn[oc % 2],
                                              wiv_d[:, oc % 2:oc % 2 + 1])
                    vector.tensor_scalar(wdq_s, wdq_s, QMAX, RC, ALU.mult, ALU.add)
                    vector.tensor_scalar(wdq_s, wdq_s, RC, QMAX, ALU.subtract, ALU.min)
                    sems.inc(vector.tensor_scalar_max(
                        wdn[oc % 2], wdq_s, -QMAX - 1.0), "s_wq", 1)
                vector.wait_ge(sems.h["s_cc"], 2)
                vector.wait_ge(sems.h["chld"], 16)
                vector.wait_ge(sems.h["s_scl"], scl0 + 1)
                sems.inc(vector.tensor_mul(td_sb, sigd_sb, ch_sb), "s_vh", 1)
                vector.wait_ge(sems.h["s_vh"], sems.at("s_vh"))
                sems.inc(vector.tensor_scalar_mul(sh1, td_sb, 1.0 / QMAX), "s_vh", 1)
                vector.wait_ge(sems.h["s_vh"], sems.at("s_vh"))
                sems.inc(vector.reciprocal(ish1, sh1), "s_scl", 1)
                vector.wait_ge(sems.h["s_bc"], 4)
                vector.tensor_copy(shb, ps_b3[:, 0:1])
                sems.inc(vector.tensor_copy(invsh, ps_b3[:, 1:2]), "s_vh", 1)
                vector.wait_ge(sems.h["s_vh"], sems.at("s_vh"))
                vector.wait_ge(sems.h["s_wq"], wq0 + 8)
                sems.inc(vector.tensor_scalar(
                    wsx_d, ws_d, shb, 1.0 / QMAX, ALU.mult, ALU.mult), "s_scl", 4)
                for tc in range(NTC):
                    vector.wait_ge(sems.h["hin1"], 16 * (tc + 1))
                    for ic in range(IT):
                        Q = IT * tc + ic
                        vector.tensor_scalar(
                            uq3, h_in[:, ic * 512:(ic + 1) * 512],
                            invsh, RC, ALU.mult, ALU.add)
                        vector.tensor_scalar(uq3, uq3, RC, QMAX,
                                             ALU.subtract, ALU.min)
                        sems.inc(vector.tensor_scalar_max(
                            qh_bf[tc % 2][:, ic * 512:(ic + 1) * 512],
                            uq3, -QMAX - 1.0), "s_qx", 1)
                    vector.wait_ge(sems.h["p1in1"], 16 * (tc + 1))
                    if tc >= 1:
                        vector.wait_ge(sems.h["sx1s"], 16 * tc)
                    for oc in range(8):
                        DN = 8 * tc + oc
                        vector.wait_ge(sems.h["s_dn"], DN + 1)
                        sems.inc(vector.scalar_tensor_tensor(
                            out=x1_sb[:, oc * 512:(oc + 1) * 512],
                            in0=ps_d[DN % 2],
                            scalar=wsx_d[:, oc:oc + 1],
                            in1=p1_in[:, oc * 512:(oc + 1) * 512],
                            op0=ALU.mult, op1=ALU.add), "s_x1ev", 1)

            @blk.tensor
            def _(tensor):
                for oc in range(8):
                    tensor.wait_ge(sems.h["s_wq"], wq0 + oc + 1)
                    for k in range(IT):
                        TP = IT * oc + k
                        g6 = 6 * oc + (k // 4)
                        if g6 >= 2 and k % 4 == 0:
                            tensor.wait_ge(sems.h["s_wT"], wT0 + g6 - 1)
                        ins = tensor.transpose(
                            ps_w3[g6 % 2][:, (k % 4) * 128:(k % 4 + 1) * 128],
                            wdn[oc % 2][:, k * 128:(k + 1) * 128], ident)
                        if k % 4 == 3 or k == IT - 1:
                            sems.inc(ins, "s_tpA", 1)
                tensor.wait_ge(sems.h["s_scl"], scl0 + 2)
                tensor.matmul(ps_b3[:, 0:1], ones_r, sh1, start=True, stop=True)
                sems.inc(tensor.matmul(ps_b3[:, 1:2], ones_r, ish1,
                                       start=True, stop=True), "s_bc", 2)
                tensor.wait_ge(sems.h["s_wT"], wT0 + 48)
                for tc in range(NTC):
                    for oc in range(8):
                        DN = 8 * tc + oc
                        if oc == 0:
                            tensor.wait_ge(sems.h["s_qx"], qx0 + IT * (tc + 1))
                        if DN >= 2:
                            tensor.wait_ge(sems.h["s_x1ev"], DN - 1)
                        for k in range(IT):
                            ins = tensor.matmul(
                                ps_d[DN % 2],
                                wdqT[:, k * 1024 + oc * 128: k * 1024 + (oc + 1) * 128],
                                qh_bf[tc % 2][:, k * 512:(k + 1) * 512],
                                start=(k == 0), stop=(k == IT - 1))
                        sems.inc(ins, "s_dn", 1)

            @blk.gpsimd
            def _(g):
                g.wait_ge(sems.h["s_cc"], 2)
                sems.inc(g.dma_start(out=ch_sb, in_=cch_out.ap()), "chld", 16)
                for tc in range(NTC):
                    g.wait_ge(sems.h["s_x1ev"], 8 * (tc + 1))
                    sems.inc(g.dma_start(
                        out=x1T_h_t[:, :, tc * 512:(tc + 1) * 512],
                        in_=x1_sb.rearrange("p (k t) -> p k t", t=512)), "sx1s", 16)

    if dev_phase is not None and dev_phase == 4:
        with nc.Block() as blk:
            @blk.gpsimd
            def _(g):
                g.wait_ge(sems.h["s_st"], sems.at("s_st"))
                sems.inc(g.dma_start(out=gmax_dbg, in_=cch_out.ap()), "s_st", 16)
                g.wait_ge(sems.h["s_st"], sems.at("s_st"))
        return nc

    xn1T_h_t = xn1T_h.rearrange("(k p) t -> p k t", p=128)
    q1T_h_t = q1T_h.rearrange("(k p) t -> p k t", p=128)
    k1T_h_t = k1T_h.rearrange("(k p) t -> p k t", p=128)
    v1_h_t = v1_h.rearrange("(c a p) d -> c p a d", a=4, p=128)
    fullT_h_t = fullT_h.rearrange("(k p) t -> p k t", p=128)
    xcT_h_t = xcT_h.rearrange("(b p) t -> p b t", p=128)
    p2T_h_t = p2T_h.rearrange("(k p) t -> p k t", p=128)
    out_h_t = out_h.rearrange("(c a p) d -> c p a d", a=4, p=128)

    with contextlib.ExitStack() as ctx:
        sb = lambda nm, shp, dt=F32: ctx.enter_context(nc.sbuf_tensor(nm, shp, dt)).ap()
        pst = lambda nm, shp: ctx.enter_context(nc.psum_tensor(nm, shp, F32)).ap()
        w1q = sb("w1q", [128, 8 * 1024])
        w1k = sb("w1k", [128, 8 * 1024])
        w1v = sb("w1v", [128, 8 * 1024])
        x1in_ = sb("x1in", [128, 8 * 512])
        x1in = [x1in_, x1in_]
        xn1o = [sb(f"xn1o{i}", [128, 8 * 512]) for i in range(2)]
        sq1 = sb("sq1", [128, 512])
        mrow = sb("mrow", [1, 512])
        e2row = sb("e2row", [1, 512])
        vrow = sb("vrow", [1, 512])
        srow = sb("srow", [1, 512])
        rrow = sb("rrow", [1, 512])
        mb_sb = sb("mb_sb", [128, 512])
        rb_sb = sb("rb_sb", [128, 512])
        o1buf = sb("o1buf", [128, 8 * 512])
        o2buf = sb("o2buf", [128, 8 * 512])
        q1o, k1o, v1o = o1buf, o2buf, o1buf
        fm_e = sb("fm_e", [128, 512])
        fm_r = sb("fm_r", [128, 512])
        ps_s1 = pst("ps_s1", [128, 512])
        ps_bc1 = pst("ps_bc1", [128, 512])
        ps_p4 = [pst(f"ps_p4{i}", [128, 512]) for i in range(4)]

        ld0 = sems.at("s_ld2")
        st0 = sems.at("s_st2")
        xn0 = sems.at("s_xn")
        var0 = sems.at("s_var")
        std0 = sems.at("s_std")
        mm0 = sems.at("s_mm")
        ln0 = sems.at("s_ln")
        lnb0 = sems.at("s_lnb")
        sq0v = sems.at("s_sq")
        pj0 = sems.at("s_pj")
        pjA0 = sems.at("s_pjA")
        pjB0 = sems.at("s_pjB")
        fm0 = sems.at("s_fm")
        evA0 = sems.at("s_evA")
        sq10 = sems.at("sq1s")
        sv10 = sems.at("sv1s")
        with nc.Block() as blk:
            @blk.sync
            def _(sync):
                sync.wait_ge(sems.h["sx1s"], sems.at("sx1s"))
                for w_sb, w_hh in ((w1q, l1_wq), (w1k, l1_wk), (w1v, l1_wv)):
                    sems.inc(sync.dma_start(
                        out=w_sb.rearrange("p (k d) -> p k d", d=1024),
                        in_=w_hh.rearrange("(k p) d -> p k d", p=128)), "s_ld2", 16)
                for tc in range(NTC):
                    if tc >= 1:
                        sync.wait_ge(sems.h["s_xn"], xn0 + tc)
                    sems.inc(sync.dma_start(
                        out=x1in[tc % 2].rearrange("p (k t) -> p k t", t=512),
                        in_=x1T_h_t[:, :, tc * 512:(tc + 1) * 512]), "x1inA", 16)

            @blk.scalar
            def _(scalar):
                for tc in range(NTC):
                    scalar.wait_ge(sems.h["x1inA"], 16 * (tc + 1))
                    for k in range(KT):
                        SQ = 8 * tc + k
                        if SQ >= 1:
                            scalar.wait_ge(sems.h["s_ln"], ln0 + SQ)
                        sems.inc(scalar.activation(
                            out=sq1, in_=x1in[tc % 2][:, k * 512:(k + 1) * 512],
                            func=AF.Square), "s_sq", 1)
                    scalar.wait_ge(sems.h["s_var"], var0 + tc + 1)
                    sems.inc(scalar.activation(
                        out=srow, in_=vrow, func=AF.Sqrt, bias=cst[0:1, 0:1]),
                        "s_std", 1)
                    for j in range(16):
                        FM = 16 * tc + j
                        scalar.wait_ge(sems.h["s_fm"], fm0 + 3 * FM + 1)
                        sems.inc(scalar.activation(
                            out=fm_e, in_=fm_e, func=AF.Exp), "s_fm", 2)

            @blk.tensor
            def _(tensor):
                tensor.wait_ge(sems.h["s_ld2"], ld0 + 48)
                for tc in range(NTC):
                    tensor.wait_ge(sems.h["x1inA"], 16 * (tc + 1))
                    if tc >= 1:
                        tensor.wait_ge(sems.h["s_var"], var0 + tc)
                    for k in range(KT):
                        tensor.matmul(ps_s1[0:1, 0:512], ones_c,
                                      x1in[tc % 2][:, k * 512:(k + 1) * 512],
                                      start=(k == 0), stop=(k == 7))
                    for k in range(KT):
                        tensor.wait_ge(sems.h["s_sq"], sq0v + 8 * tc + k + 1)
                        ins = tensor.matmul(ps_s1[64:65, 0:512],
                                            ones_c, sq1,
                                            start=(k == 0), stop=(k == 7))
                        sems.inc(ins, "s_ln", 1)
                    tensor.wait_ge(sems.h["s_var"], var0 + tc + 1)
                    if tc >= 1:
                        tensor.wait_ge(sems.h["s_evA"], evA0 + 2 * tc)
                    tensor.matmul(ps_bc1[:, 0:256], ones_r, mrow[0:1, 0:256],
                                  start=True, stop=True)
                    sems.inc(tensor.matmul(ps_bc1[:, 256:512], ones_r,
                                           mrow[0:1, 256:512],
                                           start=True, stop=True), "s_lnb", 1)
                    tensor.wait_ge(sems.h["s_mm"], mm0 + tc + 1)
                    tensor.wait_ge(sems.h["s_evA"], evA0 + 2 * tc + 1)
                    tensor.matmul(ps_bc1[:, 0:256], ones_r, rrow[0:1, 0:256],
                                  start=True, stop=True)
                    sems.inc(tensor.matmul(ps_bc1[:, 256:512], ones_r,
                                           rrow[0:1, 256:512],
                                           start=True, stop=True), "s_lnb", 1)
                    tensor.wait_ge(sems.h["s_xn"], xn0 + tc + 1)
                    tiles = [("q", oc) for oc in range(8)] + \
                            [("k", oc) for oc in range(8)] + \
                            [("v", (su, oh)) for su in range(4) for oh in range(2)]
                    for ti, (kindt, idx) in enumerate(tiles):
                        G = 24 * tc + ti
                        if G >= 4:
                            tensor.wait_ge(sems.h["s_pjA"], pjA0 + G - 3)
                        pb = ps_p4[G % 4]
                        if kindt in ("q", "k"):
                            w = w1q if kindt == "q" else w1k
                            oc = idx
                            for k in range(KT):
                                ins = tensor.matmul(
                                    pb,
                                    w[:, k * 1024 + oc * 128: k * 1024 + (oc + 1) * 128],
                                    xn1o[tc % 2][:, k * 512:(k + 1) * 512],
                                    start=(k == 0), stop=(k == 7))
                        else:
                            su, oh = idx
                            for k in range(KT):
                                ins = tensor.matmul(
                                    pb,
                                    xn1o[tc % 2][:, k * 512 + su * 128: k * 512 + (su + 1) * 128],
                                    w1v[:, k * 1024 + oh * 512: k * 1024 + (oh + 1) * 512],
                                    start=(k == 0), stop=(k == 7))
                        sems.inc(ins, "s_pj", 1)

            @blk.vector
            def _(vector):
                for tc in range(NTC):
                    vector.wait_ge(sems.h["s_ln"], ln0 + 8 * (tc + 1))
                    vector.tensor_scalar_mul(mrow, ps_s1[0:1, 0:512], 1.0 / D)
                    sems.inc(vector.tensor_scalar_mul(
                        e2row, ps_s1[64:65, 0:512], 1.0 / D), "s_vh", 1)
                    vector.wait_ge(sems.h["s_vh"], sems.at("s_vh"))
                    sems.inc(vector.tensor_mul(vrow, mrow, mrow), "s_vh", 1)
                    vector.wait_ge(sems.h["s_vh"], sems.at("s_vh"))
                    sems.inc(vector.tensor_sub(vrow, e2row, vrow), "s_var", 1)
                    vector.wait_ge(sems.h["s_std"], std0 + tc + 1)
                    sems.inc(vector.reciprocal(rrow, srow), "s_mm", 1)
                    vector.wait_ge(sems.h["s_lnb"], lnb0 + 2 * tc + 1)
                    sems.inc(vector.tensor_copy(mb_sb, ps_bc1), "s_evA", 1)
                    vector.wait_ge(sems.h["s_lnb"], lnb0 + 2 * tc + 2)
                    sems.inc(vector.tensor_copy(rb_sb, ps_bc1), "s_evA", 1)
                    sems.inc(vector.engine_nop(), "s_vh", 1)
                    vector.wait_ge(sems.h["s_vh"], sems.at("s_vh"))
                    if tc >= 2:
                        vector.wait_ge(sems.h["s_pj"], pj0 + 24 * (tc - 1))
                        vector.wait_ge(sems.h["sxn1A" if tc % 2 == 0 else "sxn1B"],
                                       16 * (tc // 2))
                    for k in range(KT):
                        vector.tensor_sub(xn1o[tc % 2][:, k * 512:(k + 1) * 512],
                                          x1in[tc % 2][:, k * 512:(k + 1) * 512],
                                          mb_sb)
                        ins = vector.tensor_mul(
                            xn1o[tc % 2][:, k * 512:(k + 1) * 512],
                            xn1o[tc % 2][:, k * 512:(k + 1) * 512], rb_sb)
                    sems.inc(ins, "s_xn", 1)
                    for ti in range(24):
                        G = 24 * tc + ti
                        vector.wait_ge(sems.h["s_pj"], pj0 + G + 1)
                        pb = ps_p4[G % 4]
                        if ti == 0 and tc >= 1:
                            vector.wait_ge(sems.h["sv1s"], sv10 + 16 * tc)
                        if ti == 16:
                            vector.wait_ge(sems.h["sq1s"], sq10 + 16 * (tc + 1))
                        if ti < 16:
                            dst = (q1o if ti < 8 else k1o)[:, (ti % 8) * 512:(ti % 8 + 1) * 512]
                            FM = 16 * tc + ti
                            sems.inc(vector.tensor_scalar_min(fm_e, pb, 0.0),
                                     "s_fm", 1)
                            sems.inc(vector.tensor_relu(fm_r, pb), "s_pjA", 1)
                            vector.wait_ge(sems.h["s_fm"], fm0 + 3 * FM + 3)
                            sems.inc(vector.tensor_add(dst, fm_e, fm_r), "s_pjB", 1)
                        else:
                            su, oh = (ti - 16) // 2, (ti - 16) % 2
                            dst = v1o[:, su * 1024 + oh * 512: su * 1024 + (oh + 1) * 512]
                            sems.inc(vector.tensor_copy(dst, pb), "s_pjA", 1)
                            sems.inc(vector.engine_nop(), "s_pjB", 1)

            @blk.gpsimd
            def _(g):
                for tc in range(NTC):
                    g.wait_ge(sems.h["s_xn"], xn0 + tc + 1)
                    sems.inc(g.dma_start(
                        out=xn1T_h_t[:, :, tc * 512:(tc + 1) * 512],
                        in_=xn1o[tc % 2].rearrange("p (k t) -> p k t", t=512)),
                        "sxn1A" if tc % 2 == 0 else "sxn1B", 16)
                    g.wait_ge(sems.h["s_pjB"], pjB0 + 24 * tc + 8)
                    sems.inc(g.dma_start(
                        out=q1T_h_t[:, :, tc * 512:(tc + 1) * 512],
                        in_=q1o.rearrange("p (k t) -> p k t", t=512)), "sq1s", 16)
                    g.wait_ge(sems.h["s_pjB"], pjB0 + 24 * tc + 16)
                    sems.inc(g.dma_start(
                        out=k1T_h_t[:, :, tc * 512:(tc + 1) * 512],
                        in_=k1o.rearrange("p (k t) -> p k t", t=512)), "sk1s", 16)
                    g.wait_ge(sems.h["s_pjB"], pjB0 + 24 * (tc + 1))
                    sems.inc(g.dma_start(
                        out=v1_h_t[tc],
                        in_=v1o.rearrange("p (a d) -> p a d", d=1024)), "sv1s", 16)

    with contextlib.ExitStack() as ctx:
        sb = lambda nm, shp, dt=F32: ctx.enter_context(nc.sbuf_tensor(nm, shp, dt)).ap()
        pst = lambda nm, shp: ctx.enter_context(nc.psum_tensor(nm, shp, F32)).ap()
        l1w_sb = sb("l1w_sb", [128, 8 * 1024])
        sobd_sb = sb("sobd_sb", [128, 8 * 256])
        sibd_sb = sb("sibd_sb", [128, 8 * 256])
        cvb_sb = sb("cvb_sb", [128, 4])
        xnin = [sb(f"xnin{i}", [128, 8 * 512]) for i in range(2)]
        fullo = sb("fullo", [128, 8 * 512])
        xco = sb("xco", [128, 4 * 512])
        sqx = [sb(f"sqx{i}", [128, 512]) for i in range(2)]
        bn_acc = sb("bn_acc", [128, 8])
        bn_t = sb("bn_t", [128, 8])
        ps_f = [pst(f"ps_f{i}", [128, 512]) for i in range(2)]
        ps_x = [pst(f"ps_x{i}", [128, 512]) for i in range(2)]

        fu0 = sems.at("s_pj")
        fuev0 = sems.at("s_pjA")
        xc20 = sems.at("s_sc")
        xcev0 = sems.at("s_exp")
        sfu0 = sems.at("sq1s")
        sxc0 = sems.at("sk1s")
        bna0 = sems.at("s_hm")
        sq0v = sems.at("s_sq")
        xiA0 = sems.at("xiA")
        xiB0 = sems.at("xiB")
        ld20 = sems.at("s_ld2") + 64
        with nc.Block() as blk:
            @blk.sync
            def _(sync):
                for nm in ("sxn1A", "sxn1B"):
                    sync.wait_ge(sems.h[nm], sems.at(nm))
                sems.inc(sync.dma_start(
                    out=l1w_sb.rearrange("p (k d) -> p k d", d=1024),
                    in_=l1_w.rearrange("(k p) d -> p k d", p=128)), "s_ld2", 16)
                sems.inc(sync.dma_start(
                    out=sobd_sb.rearrange("p (k d) -> p k d", d=256),
                    in_=so_bd.rearrange("(k p) d -> p k d", p=128)), "s_ld2", 16)
                sems.inc(sync.dma_start(
                    out=sibd_sb.rearrange("p (k d) -> p k d", d=256),
                    in_=si_bd.rearrange("(k p) d -> p k d", p=128)), "s_ld2", 16)
                sems.inc(sync.dma_start(out=cvb_sb, in_=cvb_c), "s_ld2", 16)
                for tc in range(NTC):
                    if tc >= 2:
                        sync.wait_ge(sems.h["s_pj"], fu0 + 8 * (tc - 1))
                        sync.wait_ge(sems.h["s_sc"], xc20 + 4 * (tc - 1))
                    sems.inc(sync.dma_start(
                        out=xnin[tc % 2].rearrange("p (k t) -> p k t", t=512),
                        in_=xn1T_h_t[:, :, tc * 512:(tc + 1) * 512]),
                        "xiA" if tc % 2 == 0 else "xiB", 16)

            @blk.tensor
            def _(tensor):
                tensor.wait_ge(sems.h["s_ld2"], ld20)
                for tc in range(NTC):
                    tensor.wait_ge(sems.h["xiA" if tc % 2 == 0 else "xiB"],
                                   (xiA0 if tc % 2 == 0 else xiB0) + 16 * (tc // 2 + 1))
                    for oc in range(8):
                        G = 8 * tc + oc
                        if G >= 2:
                            tensor.wait_ge(sems.h["s_pjA"], fuev0 + G - 1)
                        for k in range(KT):
                            ins = tensor.matmul(
                                ps_f[G % 2],
                                l1w_sb[:, k * 1024 + oc * 128: k * 1024 + (oc + 1) * 128],
                                xnin[tc % 2][:, k * 512:(k + 1) * 512],
                                start=(k == 0), stop=(k == 7))
                        sems.inc(ins, "s_pj", 1)
                    for bj in range(4):
                        XG = 4 * tc + bj
                        br, gt = bj // 2, bj % 2
                        bd = sobd_sb if br == 0 else sibd_sb
                        if XG >= 2:
                            tensor.wait_ge(sems.h["s_exp"], xcev0 + XG - 1)
                        for k in range(KT):
                            ins = tensor.matmul(
                                ps_x[XG % 2],
                                bd[:, k * 256 + gt * 128: k * 256 + (gt + 1) * 128],
                                xnin[tc % 2][:, k * 512:(k + 1) * 512],
                                start=(k == 0), stop=(k == 7))
                        sems.inc(ins, "s_sc", 1)

            @blk.scalar
            def _(scalar):
                sqr0s = sems.at("s_sqr")
                for tc in range(NTC):
                    for bj in range(4):
                        XG = 4 * tc + bj
                        scalar.wait_ge(sems.h["s_exp"], xcev0 + XG + 1)
                        if XG >= 2:
                            scalar.wait_ge(sems.h["s_sqr"], sqr0s + XG - 1)
                        sems.inc(scalar.activation(
                            out=sqx[XG % 2], in_=xco[:, bj * 512:(bj + 1) * 512],
                            func=AF.Square), "s_sq", 1)

            @blk.vector
            def _(vector):
                sqr0v = sems.at("s_sqr")
                vector.memset(bn_acc, 0.0)
                for tc in range(NTC):
                    if tc >= 1:
                        vector.wait_ge(sems.h["sq1s"], sfu0 + 16 * tc)
                    for oc in range(8):
                        G = 8 * tc + oc
                        vector.wait_ge(sems.h["s_pj"], fu0 + G + 1)
                        sems.inc(vector.tensor_copy(
                            fullo[:, oc * 512:(oc + 1) * 512], ps_f[G % 2]),
                            "s_pjA", 1)
                    if tc >= 1:
                        vector.wait_ge(sems.h["sk1s"], sxc0 + 16 * tc)
                    for bj in range(4):
                        XG = 4 * tc + bj
                        vector.wait_ge(sems.h["s_sc"], xc20 + XG + 1)
                        sems.inc(vector.tensor_scalar_add(
                            xco[:, bj * 512:(bj + 1) * 512], ps_x[XG % 2],
                            cvb_sb[:, bj:bj + 1]), "s_exp", 1)
                        vector.reduce_sum(bn_t[:, bj:bj + 1],
                                          xco[:, bj * 512:(bj + 1) * 512],
                                          axis=AX.X)
                        vector.wait_ge(sems.h["s_sq"], sq0v + XG + 1)
                        sems.inc(vector.reduce_sum(
                            bn_t[:, 4 + bj:5 + bj], sqx[XG % 2], axis=AX.X),
                            "s_sqr", 1)
                    sems.inc(vector.engine_nop(), "s_vh", 1)
                    vector.wait_ge(sems.h["s_vh"], sems.at("s_vh"))
                    sems.inc(vector.tensor_add(bn_acc, bn_acc, bn_t), "s_hm", 1)

            @blk.gpsimd
            def _(g):
                for tc in range(NTC):
                    g.wait_ge(sems.h["s_pjA"], fuev0 + 8 * (tc + 1))
                    sems.inc(g.dma_start(
                        out=fullT_h_t[:, :, tc * 512:(tc + 1) * 512],
                        in_=fullo.rearrange("p (k t) -> p k t", t=512)), "sq1s", 16)
                    g.wait_ge(sems.h["s_hm"], bna0 + tc + 1)
                    sems.inc(g.dma_start(
                        out=xcT_h_t[:, :, tc * 512:(tc + 1) * 512],
                        in_=xco.rearrange("p (b t) -> p b t", t=512)), "sk1s", 16)
                g.wait_ge(sems.h["s_hm"], bna0 + NTC)
                sems.inc(g.dma_start(out=ccbn_in.ap(), in_=bn_acc), "s_st", 16)
                g.wait_ge(sems.h["s_st"], sems.at("s_st"))
                sems.inc(g.collective_compute(
                    "AllReduce", ALU.add,
                    replica_groups=[list(range(NCORES))],
                    ins=[ccbn_in.ap()], outs=[ccbn_out.ap()]), "s_cc", 1)

    if dev_phase is not None and dev_phase == 5:
        with nc.Block() as blk:
            @blk.gpsimd
            def _(g):
                g.wait_ge(sems.h["s_st"], sems.at("s_st"))
                g.wait_ge(sems.h["s_st2"], sems.at("s_st2"))
                sems.inc(g.dma_start(out=gmax_dbg, in_=cch_out.ap()), "s_st", 16)
                g.wait_ge(sems.h["s_st"], sems.at("s_st"))
        return nc

    CL = 256
    NCL = ncl_limit if ncl_limit is not None else T // CL
    CPS = SEQ // CL
    with contextlib.ExitStack() as ctx:
        sb = lambda nm, shp, dt=F32: ctx.enter_context(nc.sbuf_tensor(nm, shp, dt)).ap()
        pst = lambda nm, shp: ctx.enter_context(nc.psum_tensor(nm, shp, F32)).ap()
        w1o = sb("w1o", [128, 8 * 1024])
        qc = [sb(f"qc{i}", [128, 8 * CL]) for i in range(2)]
        kc = [sb(f"kc{i}", [128, 8 * CL]) for i in range(2)]
        vc = [sb(f"vc{i}", [128, 2 * 1024]) for i in range(2)]
        AT_sb = sb("AT_sb", [128, 2 * 512])
        knat_sb = sb("knat_sb", [128, 2 * 128])
        Sz_sb = sb("Sz_sb", [128, 8 * 65])
        zt_sb = sb("zt_sb", [128, 8])
        den_sb = sb("den_sb3", [1, 2 * 256])
        rden3 = sb("rden3", [1, 2 * 256])
        rdenb3 = sb("rdenb3", [128, 2 * 256])
        yT_sb = sb("yT_sb", [128, 8 * 512])
        x1r = sb("x1r", [128, 8 * 512])
        p2o = sb("p2o", [128, 8 * 512])
        l1m = [sb(f"l1m{i}", [128, 256]) for i in range(2)]
        identL = sb("identL", [128, 64])
        psA = [pst(f"psA{i}", [128, 512]) for i in range(2)]
        psN = [pst(f"psN{i}", [128, 256]) for i in range(2)]
        psM = [pst(f"psM{i}", [128, 512]) for i in range(2)]
        psD = [pst(f"psD{i}", [128, 512]) for i in range(2)]

        ld0 = sems.at("s_ld")

        with nc.Block() as blk:
            @blk.sync
            def _(sync):
                for nm in ("sq1s", "sk1s", "sv1s", "sxn1A", "sxn1B"):
                    sync.wait_ge(sems.h[nm], sems.at(nm))
                sems.inc(sync.dma_start(
                    out=w1o.rearrange("p (k d) -> p k d", d=1024),
                    in_=l1_wo.rearrange("(k p) d -> p k d", p=128)), "s_ld", 16)
                sems.inc(sync.dma_start(out=l1m[0], in_=l1m0_c.ap()), "s_ld", 16)
                sems.inc(sync.dma_start(out=l1m[1], in_=l1m1_c.ap()), "s_ld", 16)
                sems.inc(sync.dma_start(out=identL, in_=identL_c.ap()), "s_ld", 16)
                for ci in range(NCL):
                    if ci >= 2:
                        sync.wait_ge(sems.h["s_A"], 16 * (ci - 1))
                        sync.wait_ge(sems.h["s_num"], 16 * (ci - 1))
                        sync.wait_ge(sems.h["s_den"], 16 * (ci - 1))
                        sync.wait_ge(sems.h["s_Smm"], 16 * (ci - 1))
                        sync.wait_ge(sems.h["s_ktr"], 16 * (ci - 1))
                        sync.wait_ge(sems.h["s_Sup"], 16 * (ci - 1))
                    qkv = "qcA" if ci % 2 == 0 else "qcB"
                    sems.inc(sync.dma_start(
                        out=qc[ci % 2].rearrange("p (k t) -> p k t", t=CL),
                        in_=q1T_h_t[:, :, ci * CL:(ci + 1) * CL]), qkv, 16)
                    sems.inc(sync.dma_start(
                        out=kc[ci % 2].rearrange("p (k t) -> p k t", t=CL),
                        in_=k1T_h_t[:, :, ci * CL:(ci + 1) * CL]), qkv, 16)
                    sems.inc(sync.dma_start(
                        out=vc[ci % 2].rearrange("p (a d) -> p a d", d=1024),
                        in_=v1_h.rearrange("(c a p) d -> c p a d", a=2, p=128)[ci]),
                        qkv, 16)
                    if ci % 2 == 0:
                        grp = ci // 2
                        if grp >= 1:
                            sync.wait_ge(sems.h["s_p2"], 8 * grp)
                        sems.inc(sync.dma_start(
                            out=x1r.rearrange("p (k t) -> p k t", t=512),
                            in_=x1T_h_t[:, :, grp * 512:(grp + 1) * 512]), "x1rA", 16)

            @blk.tensor
            def _(tensor):
                tensor.wait_ge(sems.h["s_ld"], ld0 + 64)
                for ci in range(NCL):
                    tensor.wait_ge(sems.h["qcA" if ci % 2 == 0 else "qcB"],
                                   48 * (ci // 2 + 1))
                    hsl = lambda buf, h: buf[(h % 2) * 64:(h % 2) * 64 + 64,
                                             (h // 2) * CL:(h // 2 + 1) * CL]
                    for h in range(16):
                        E = 16 * ci + h
                        rows = slice((h % 2) * 64, (h % 2) * 64 + 64)
                        if ci % 2 == 0 and h < 2 and ci >= 2:
                            tensor.wait_ge(sems.h["s_p2"], 8 * (ci // 2))
                        if E >= 2:
                            tensor.wait_ge(sems.h["s_bce"], E - 1)
                            tensor.wait_ge(sems.h["s_msk"], 2 * (E - 2) + 2)
                        for sub in range(2):
                            ins = tensor.matmul(
                                psA[h % 2][:, sub * 256:(sub + 1) * 256],
                                kc[ci % 2][(h % 2) * 64:(h % 2) * 64 + 64,
                                           (h // 2) * CL + sub * 128:
                                           (h // 2) * CL + (sub + 1) * 128],
                                hsl(qc[ci % 2], h), start=True, stop=True)
                        sems.inc(ins, "s_A", 1)
                        tensor.wait_ge(sems.h["s_msk"], 2 * E + 2)
                        if E >= 2:
                            tensor.wait_ge(sems.h["s_dnv"], E - 1)
                        for sub in range(2):
                            tensor.matmul(
                                psD[h % 2][0:1, 0:256], ones_c,
                                AT_sb[:, (h % 2) * 512 + sub * 256:
                                      (h % 2) * 512 + (sub + 1) * 256],
                                start=(sub == 0), stop=False)
                        sems.inc(tensor.matmul(
                            psD[h % 2][0:1, 0:256],
                            Sz_sb[rows, (h // 2) * 65 + 64:(h // 2) * 65 + 65],
                            hsl(qc[ci % 2], h), start=False, stop=True), "s_den", 1)
                        if ci % CPS == 0:
                            tensor.wait_ge(sems.h["s_seq"], ci // CPS + 1)
                        else:
                            tensor.wait_ge(sems.h["s_Sup"], 16 * (ci - 1) + h + 1)
                        if E >= 2:
                            tensor.wait_ge(sems.h["s_yml"], E - 1)
                        for sub in range(2):
                            tensor.matmul(
                                psN[h % 2][rows, 0:256],
                                vc[ci % 2][:, sub * 1024 + h * 64:
                                           sub * 1024 + (h + 1) * 64],
                                AT_sb[:, (h % 2) * 512 + sub * 256:
                                      (h % 2) * 512 + (sub + 1) * 256],
                                start=(sub == 0), stop=False)
                        sems.inc(tensor.matmul(
                            psN[h % 2][rows, 0:256],
                            Sz_sb[rows, (h // 2) * 65:(h // 2) * 65 + 64],
                            hsl(qc[ci % 2], h), start=False, stop=True), "s_num", 1)
                        if E >= 2:
                            tensor.wait_ge(sems.h["s_knv"], E - 1)
                            tensor.wait_ge(sems.h["s_Sup"], E - 1)
                        for sub in range(2):
                            ins = tensor.transpose(
                                psM[h % 2][:, sub * 64:(sub + 1) * 64],
                                kc[ci % 2][(h % 2) * 64:(h % 2) * 64 + 64,
                                           (h // 2) * CL + sub * 128:
                                           (h // 2) * CL + (sub + 1) * 128],
                                identL[(h % 2) * 64:(h % 2) * 64 + 64, 0:64])
                        sems.inc(ins, "s_ktr", 1)
                        tensor.wait_ge(sems.h["s_knv"], E + 1)
                        for sub in range(2):
                            ins = tensor.matmul(
                                psM[h % 2][rows, 128:192],
                                knat_sb[:, (h % 2) * 128 + sub * 64:
                                        (h % 2) * 128 + (sub + 1) * 64],
                                vc[ci % 2][:, sub * 1024 + h * 64:
                                           sub * 1024 + (h + 1) * 64],
                                start=(sub == 0), stop=(sub == 1))
                        sems.inc(ins, "s_Smm", 1)
                        tensor.wait_ge(sems.h["s_rdn"], E + 1)
                        sems.inc(tensor.matmul(
                            psA[h % 2][:, 256:512], ones_r,
                            rden3[0:1, (h % 2) * 256:(h % 2 + 1) * 256],
                            start=True, stop=True), "s_bcm", 1)
                    if ci % 2 == 1:
                        grp = ci // 2
                        tensor.wait_ge(sems.h["s_yml"], 16 * (ci + 1))
                        tensor.wait_ge(sems.h["s_bce"], 16 * (ci + 1))
                        for oc in range(8):
                            W = 8 * grp + oc
                            if W >= 2:
                                tensor.wait_ge(sems.h["s_p2"], W - 1)
                            for k in range(KT):
                                ins = tensor.matmul(
                                    psA[oc % 2],
                                    w1o[:, k * 1024 + oc * 128: k * 1024 + (oc + 1) * 128],
                                    yT_sb[:, k * 512:(k + 1) * 512],
                                    start=(k == 0), stop=(k == 7))
                            sems.inc(ins, "s_wo2", 1)

            @blk.scalar
            def _(scalar):
                for ci in range(NCL):
                    for h in range(16):
                        E = 16 * ci + h
                        scalar.wait_ge(sems.h["s_den"], E + 1)
                        sems.inc(scalar.copy(
                            out=den_sb[0:1, (h % 2) * 256:(h % 2 + 1) * 256],
                            in_=psD[h % 2][0:1, 0:256]), "s_dnv", 1)
                        scalar.wait_ge(sems.h["s_ktr"], E + 1)
                        sems.inc(scalar.copy(
                            out=knat_sb[:, (h % 2) * 128:(h % 2 + 1) * 128],
                            in_=psM[h % 2][:, 0:128]), "s_knv", 1)
                        scalar.wait_ge(sems.h["s_bcm"], E + 1)
                        sems.inc(scalar.copy(
                            out=rdenb3[:, (h % 2) * 256:(h % 2 + 1) * 256],
                            in_=psA[h % 2][:, 256:512]), "s_bce", 1)

            @blk.vector
            def _(vector):
                for ci in range(NCL):
                    if ci % CPS == 0:
                        if ci > 0:
                            vector.wait_ge(sems.h["s_num"], 16 * ci)
                            vector.wait_ge(sems.h["s_den"], 16 * ci)
                        sems.inc(vector.memset(Sz_sb, 0.0), "s_seq", 1)
                    for h in range(16):
                        E = 16 * ci + h
                        rows = slice((h % 2) * 64, (h % 2) * 64 + 64)
                        vector.wait_ge(sems.h["s_A"], E + 1)
                        if E >= 2:
                            vector.wait_ge(sems.h["s_num"], E - 1)
                            vector.wait_ge(sems.h["s_den"], E - 1)
                        for sub in range(2):
                            vector.tensor_mul(
                                AT_sb[:, (h % 2) * 512 + sub * 256:
                                      (h % 2) * 512 + (sub + 1) * 256],
                                psA[h % 2][:, sub * 256:(sub + 1) * 256],
                                l1m[sub])
                        sems.inc(vector.engine_nop(), "s_msk", 2)
                        vector.reduce_sum(
                            zt_sb[rows, h // 2:h // 2 + 1],
                            kc[ci % 2][rows, (h // 2) * CL:(h // 2 + 1) * CL],
                            axis=AX.X)
                        vector.wait_ge(sems.h["s_dnv"], E + 1)
                        sems.inc(vector.tensor_scalar_add(
                            rden3[0:1, (h % 2) * 256:(h % 2 + 1) * 256],
                            den_sb[0:1, (h % 2) * 256:(h % 2 + 1) * 256],
                            ATT_EPS), "s_vh", 1)
                        vector.wait_ge(sems.h["s_vh"], sems.at("s_vh"))
                        sems.inc(vector.reciprocal(
                            rden3[0:1, (h % 2) * 256:(h % 2 + 1) * 256],
                            rden3[0:1, (h % 2) * 256:(h % 2 + 1) * 256]), "s_rdn", 1)
                        vector.wait_ge(sems.h["s_num"], E + 1)
                        vector.wait_ge(sems.h["s_bce"], E + 1)
                        if ci % 2 == 0 and ci >= 2 and h == 0:
                            vector.wait_ge(sems.h["s_wo2"], 8 * (ci // 2))
                        sems.inc(vector.tensor_mul(
                            yT_sb[rows, (h // 2) * 512 + (ci % 2) * 256:
                                  (h // 2) * 512 + (ci % 2) * 256 + 256],
                            psN[h % 2][rows, 0:256],
                            rdenb3[rows, (h % 2) * 256:(h % 2 + 1) * 256]),
                            "s_yml", 1)
                        vector.wait_ge(sems.h["s_Smm"], E + 1)
                        vector.tensor_add(
                            Sz_sb[rows, (h // 2) * 65:(h // 2) * 65 + 64],
                            Sz_sb[rows, (h // 2) * 65:(h // 2) * 65 + 64],
                            psM[h % 2][rows, 128:192])
                        sems.inc(vector.tensor_add(
                            Sz_sb[rows, (h // 2) * 65 + 64:(h // 2) * 65 + 65],
                            Sz_sb[rows, (h // 2) * 65 + 64:(h // 2) * 65 + 65],
                            zt_sb[rows, h // 2:h // 2 + 1]), "s_Sup", 1)
                    if ci % 2 == 1:
                        grp = ci // 2
                        vector.wait_ge(sems.h["x1rA"], 16 * (grp + 1))
                        if grp >= 1:
                            vector.wait_ge(sems.h["sp2"], 16 * grp)
                        for oc in range(8):
                            vector.wait_ge(sems.h["s_wo2"], 8 * grp + oc + 1)
                            sems.inc(vector.tensor_add(
                                p2o[:, oc * 512:(oc + 1) * 512],
                                psA[oc % 2],
                                x1r[:, oc * 512:(oc + 1) * 512]), "s_p2", 1)

            @blk.gpsimd
            def _(g):
                for grp in range(NCL // 2):
                    g.wait_ge(sems.h["s_p2"], 8 * (grp + 1))
                    sems.inc(g.dma_start(
                        out=p2T_h_t[:, :, grp * 512:(grp + 1) * 512],
                        in_=p2o.rearrange("p (k t) -> p k t", t=512)), "sp2", 16)

    with contextlib.ExitStack() as ctx:
        sb = lambda nm, shp, dt=F32: ctx.enter_context(nc.sbuf_tensor(nm, shp, dt)).ap()
        pst = lambda nm, shp: ctx.enter_context(nc.psum_tensor(nm, shp, F32)).ap()
        l1w2 = sb("l1w2", [128, 8 * 1024])
        solw = sb("solw", [128, 2 * 1024])
        silw = sb("silw", [128, 2 * 1024])
        solb = sb("solb", [128, 1024])
        silb = sb("silb", [128, 1024])
        gam_sb = sb("gam_sb", [128, 4])
        bet_sb = sb("bet_sb", [128, 4])
        b8_sb = sb("b8_sb", [128, 8])
        bn_g = sb("bn_g", [128, 8])
        bn_m = sb("bn_m", [128, 4])
        bn_e2 = sb("bn_e2", [128, 4])
        bn_v = sb("bn_v", [128, 4])
        bn_s = sb("bn_s", [128, 4])
        bn_r = sb("bn_r", [128, 4])
        sc_a = sb("sc_a", [128, 4])
        sh_b = sb("sh_b", [128, 4])
        xcin = sb("xcin", [128, 4 * 512])
        xnc = sb("xnc", [128, 8 * 512])
        fullc = sb("fullc", [128, 8 * 512])
        p2c = sb("p2c", [128, 8 * 512])
        xs_sb = sb("xs_sb", [128, 8 * 512])
        outB = sb("outB", [128, 8 * 512])
        out_nat = sb("out_nat", [128, 4 * 1024])
        logit_sb = sb("logit_sb", [128, 1024])
        p_sb3 = sb("p_sb3", [128, 1024])
        lgA = sb("lgA", [128, 1024])
        lgB = sb("lgB", [128, 1024])
        M_sb = sb("M_sb", [128, 1024])
        m8 = sb("m8", [128, 4 * 8])
        rsum = sb("rsum", [128, 1])
        ps_lg = [pst(f"ps_lg{i}", [128, 512]) for i in range(2)]
        ps_mt = [pst(f"ps_mt{i}", [128, 512]) for i in range(2)]
        ps_in = [pst(f"ps_in{i}", [128, 512]) for i in range(2)]
        ps_ot = pst("ps_ot", [128, 512])

        glA0 = sems.at("s_guA")
        gu0 = sems.at("s_gu")
        guev0 = sems.at("s_guev")
        qx0c = sems.at("s_qx")
        fm3 = sems.at("s_fm")
        tpB0 = sems.at("s_tpB")
        pf0 = sems.at("s_pf")
        dn0 = sems.at("s_dn")
        oev0 = sems.at("s_oev")
        wo0 = sems.at("s_wo")
        evA0c = sems.at("s_evA")
        scl3 = sems.at("s_scl")
        gm0 = sems.at("gmld")
        xqA0 = sems.at("xqA")
        xqB0 = sems.at("xqB")
        wnA0 = sems.at("wnA")
        wnB0 = sems.at("wnB")
        ld20c = sems.at("s_ld2")
        sqr3 = sems.at("s_sqr")

        with nc.Block() as blk:
            @blk.sync
            def _(sync):
                for nm in ("sq1s", "sk1s", "sp2"):
                    sync.wait_ge(sems.h[nm], sems.at(nm))
                sems.inc(sync.dma_start(
                    out=l1w2.rearrange("p (k d) -> p k d", d=1024),
                    in_=l1_w.rearrange("(k p) d -> p k d", p=128)), "s_ld2", 16)
                sems.inc(sync.dma_start(
                    out=solw.rearrange("p (k d) -> p k d", d=1024),
                    in_=so_linT.rearrange("(k p) d -> p k d", p=128)), "s_ld2", 16)
                sems.inc(sync.dma_start(
                    out=silw.rearrange("p (k d) -> p k d", d=1024),
                    in_=si_linT.rearrange("(k p) d -> p k d", p=128)), "s_ld2", 16)
                sems.inc(sync.dma_start(out=solb, in_=so_lb), "s_ld2", 16)
                sems.inc(sync.dma_start(out=silb, in_=si_lb), "s_ld2", 16)
                sems.inc(sync.dma_start(out=gam_sb, in_=gam_c), "s_ld2", 16)
                sems.inc(sync.dma_start(out=bet_sb, in_=bet_c), "s_ld2", 16)
                sems.inc(sync.dma_start(out=b8_sb, in_=l1_b8), "s_ld2", 16)
                for tc in range(NTC):
                    if tc >= 1:
                        sync.wait_ge(sems.h["s_guA"], glA0 + 4 * tc)
                        sync.wait_ge(sems.h["s_gu"], gu0 + 16 * tc)
                    sems.inc(sync.dma_start(
                        out=xcin.rearrange("p (b t) -> p b t", t=512),
                        in_=xcT_h_t[:, :, tc * 512:(tc + 1) * 512]), "xqA", 16)
                    if tc >= 1:
                        sync.wait_ge(sems.h["s_pf"], pf0 + 16 * tc)
                    sems.inc(sync.dma_start(
                        out=xnc.rearrange("p (k t) -> p k t", t=512),
                        in_=xn1T_h_t[:, :, tc * 512:(tc + 1) * 512]), "xqB", 16)
                    sems.inc(sync.dma_start(
                        out=fullc.rearrange("p (k t) -> p k t", t=512),
                        in_=fullT_h_t[:, :, tc * 512:(tc + 1) * 512]), "wnA", 16)
                    if tc >= 1:
                        sync.wait_ge(sems.h["s_oev"], oev0 + 8 * tc)
                    sems.inc(sync.dma_start(
                        out=p2c.rearrange("p (k t) -> p k t", t=512),
                        in_=p2T_h_t[:, :, tc * 512:(tc + 1) * 512]), "wnB", 16)

            @blk.gpsimd
            def _(g):
                g.wait_ge(sems.h["s_cc"], 3)
                sems.inc(g.dma_start(out=bn_g, in_=ccbn_out.ap()), "gmld", 16)
                for tc in range(NTC):
                    g.wait_ge(sems.h["s_evA"], evA0c + 8 * (tc + 1))
                    sems.inc(g.dma_start(
                        out=out_h_t[tc],
                        in_=out_nat.rearrange("p (a d) -> p a d", d=1024)),
                        "sout", 16)
                g.wait_ge(sems.h["sout"], 16 * NTC)

            @blk.scalar
            def _(scalar):
                scalar.wait_ge(sems.h["s_scl"], scl3 + 1)
                sems.inc(scalar.activation(out=bn_s, in_=bn_v, func=AF.Sqrt,
                                           bias=cst[:, 3:4]), "s_scl", 1)
                for tc in range(NTC):
                    scalar.wait_ge(sems.h["xqA"], xqA0 + 16 * (tc + 1))
                    if tc == 0:
                        scalar.wait_ge(sems.h["s_scl"], scl3 + 3)
                    for bj in range(4):
                        sems.inc(scalar.activation(
                            out=xcin[:, bj * 512:(bj + 1) * 512],
                            in_=xcin[:, bj * 512:(bj + 1) * 512],
                            func=AF.Gelu, scale=sc_a[:, bj:bj + 1],
                            bias=sh_b[:, bj:bj + 1]), "s_guA", 1)
                    for TB in range(8):
                        TBg = 8 * tc + TB
                        scalar.wait_ge(sems.h["s_guev"], guev0 + 2 * TBg + 2)
                        if TBg >= 1:
                            scalar.wait_ge(sems.h["s_qx"], qx0c + TBg)
                        sems.inc(scalar.activation(
                            out=p_sb3, in_=logit_sb, func=AF.Exp), "s_fm", 1)
                    for oc in range(8):
                        scalar.wait_ge(sems.h["s_wo"], wo0 + 8 * tc + oc + 1)
                        if oc == 0 and tc >= 1:
                            scalar.wait_ge(sems.h["sout"], 16 * tc)
                        sems.inc(scalar.copy(
                            out=out_nat.rearrange("p (a d) -> p a d", d=1024)[
                                :, :, oc * 128:(oc + 1) * 128],
                            in_=ps_ot.rearrange("p (a b) -> p a b", b=128)),
                            "s_evA", 1)

            @blk.tensor
            def _(tensor):
                tensor.wait_ge(sems.h["s_ld2"], ld20c + 128)
                for tc in range(NTC):
                    for sub in range(4):
                        for br in range(2):
                            TB = sub * 2 + br
                            TBg = 8 * tc + TB
                            tensor.wait_ge(sems.h["s_guA"],
                                           glA0 + 4 * tc + 2 * br + 2)
                            if TBg >= 1:
                                tensor.wait_ge(sems.h["s_guev"],
                                               guev0 + 2 * TBg)
                            lw = solw if br == 0 else silw
                            for half in range(2):
                                for gt in range(2):
                                    ins = tensor.matmul(
                                        ps_lg[half],
                                        xcin[:, (br * 2 + gt) * 512 + sub * 128:
                                             (br * 2 + gt) * 512 + (sub + 1) * 128],
                                        lw[:, gt * 1024 + half * 512:
                                           gt * 1024 + (half + 1) * 512],
                                        start=(gt == 0), stop=(gt == 1))
                                sems.inc(ins, "s_gu", 1)
                            tensor.wait_ge(sems.h["s_qx"], qx0c + TBg + 1)
                            if TBg >= 1:
                                tensor.wait_ge(sems.h["s_pf"], pf0 + 2 * TBg)
                            for gr in range(2):
                                for k4 in range(4):
                                    k = gr * 4 + k4
                                    ins = tensor.transpose(
                                        ps_mt[gr][:, k4 * 128:(k4 + 1) * 128],
                                        M_sb[:, k * 128:(k + 1) * 128], ident)
                                sems.inc(ins, "s_tpB", 1)
                    for oc in range(8):
                        if oc == 0:
                            tensor.wait_ge(sems.h["s_pf"], pf0 + 16 * (tc + 1))
                        DN = 8 * tc + oc
                        if DN >= 2:
                            tensor.wait_ge(sems.h["s_oev"], oev0 + DN - 1)
                        for k in range(KT):
                            ins = tensor.matmul(
                                ps_in[oc % 2],
                                l1w2[:, k * 1024 + oc * 128: k * 1024 + (oc + 1) * 128],
                                xs_sb[:, k * 512:(k + 1) * 512],
                                start=(k == 0), stop=(k == 7))
                        sems.inc(ins, "s_dn", 1)
                    for oc in range(8):
                        tensor.wait_ge(sems.h["s_oev"], oev0 + 8 * tc + oc + 1)
                        tensor.wait_ge(sems.h["s_evA"], evA0c + 8 * tc + oc)
                        for j in range(4):
                            ins = tensor.transpose(
                                ps_ot[:, j * 128:(j + 1) * 128],
                                outB[:, oc * 512 + j * 128: oc * 512 + (j + 1) * 128],
                                ident)
                        sems.inc(ins, "s_wo", 1)

            @blk.vector
            def _(vector):
                vector.wait_ge(sems.h["gmld"], gm0 + 16)
                vector.tensor_scalar_mul(bn_m, bn_g[:, 0:4], 1.0 / 65536.0)
                sems.inc(vector.tensor_scalar_mul(bn_e2, bn_g[:, 4:8],
                                                  1.0 / 65536.0), "s_vh", 1)
                vector.wait_ge(sems.h["s_vh"], sems.at("s_vh"))
                sems.inc(vector.tensor_mul(bn_v, bn_m, bn_m), "s_vh", 1)
                vector.wait_ge(sems.h["s_vh"], sems.at("s_vh"))
                sems.inc(vector.tensor_sub(bn_v, bn_e2, bn_v), "s_scl", 1)
                vector.wait_ge(sems.h["s_scl"], scl3 + 2)
                sems.inc(vector.reciprocal(bn_r, bn_s), "s_vh", 1)
                vector.wait_ge(sems.h["s_vh"], sems.at("s_vh"))
                sems.inc(vector.tensor_mul(sc_a, gam_sb, bn_r), "s_vh", 1)
                vector.wait_ge(sems.h["s_vh"], sems.at("s_vh"))
                sems.inc(vector.tensor_mul(bn_r, bn_m, sc_a), "s_vh", 1)
                vector.wait_ge(sems.h["s_vh"], sems.at("s_vh"))
                sems.inc(vector.tensor_sub(sh_b, bet_sb, bn_r), "s_scl", 1)
                NEG = -1.0e30
                for tc in range(NTC):
                    for sub in range(4):
                        for br in range(2):
                            TB = sub * 2 + br
                            TBg = 8 * tc + TB
                            lb = solb if br == 0 else silb
                            if TBg >= 1:
                                vector.wait_ge(sems.h["s_fm"], fm3 + TBg)
                            for half in range(2):
                                vector.wait_ge(sems.h["s_gu"],
                                               gu0 + 2 * TBg + half + 1)
                                sems.inc(vector.tensor_add(
                                    logit_sb[:, half * 512:(half + 1) * 512],
                                    ps_lg[half],
                                    lb[:, half * 512:(half + 1) * 512]),
                                    "s_guev", 1)
                            vector.wait_ge(sems.h["s_fm"], fm3 + TBg + 1)
                            sems.inc(vector.reduce_sum(rsum, p_sb3, axis=AX.X),
                                     "s_vh", 1)
                            vector.wait_ge(sems.h["s_vh"], sems.at("s_vh"))
                            vector.reciprocal(rsum, rsum)
                            sems.inc(vector.max(out=m8[:, 0:8], in_=logit_sb),
                                     "s_vh", 1)
                            vector.wait_ge(sems.h["s_vh"], sems.at("s_vh"))
                            vector.match_replace(out=lgA, in_to_replace=m8[:, 0:8],
                                                 in_values=logit_sb, imm_value=NEG)
                            sems.inc(vector.max(out=m8[:, 8:16], in_=lgA), "s_vh", 1)
                            vector.wait_ge(sems.h["s_vh"], sems.at("s_vh"))
                            vector.match_replace(out=lgB, in_to_replace=m8[:, 8:16],
                                                 in_values=lgA, imm_value=NEG)
                            sems.inc(vector.max(out=m8[:, 16:24], in_=lgB), "s_vh", 1)
                            vector.wait_ge(sems.h["s_vh"], sems.at("s_vh"))
                            vector.match_replace(out=lgA, in_to_replace=m8[:, 16:24],
                                                 in_values=lgB, imm_value=NEG)
                            sems.inc(vector.max(out=m8[:, 24:32], in_=lgA), "s_vh", 1)
                            vector.wait_ge(sems.h["s_vh"], sems.at("s_vh"))
                            vector.scalar_tensor_tensor(
                                out=M_sb, in0=logit_sb, scalar=m8[:, 31:32],
                                in1=p_sb3, op0=ALU.is_ge, op1=ALU.mult)
                            sems.inc(vector.tensor_scalar_mul(M_sb, M_sb, rsum),
                                     "s_qx", 1)
                            if br == 0:
                                vector.wait_ge(sems.h["wnA"],
                                               wnA0 + 16 * (tc + 1))
                                if TBg == 0 or (tc >= 1 and TB == 0):
                                    vector.wait_ge(sems.h["s_wo"], wo0 + 8 * tc)
                                other, dstb = fullc, outB
                            else:
                                vector.wait_ge(sems.h["xqB"],
                                               xqB0 + 16 * (tc + 1))
                                if tc >= 1 and TB <= 1:
                                    vector.wait_ge(sems.h["s_dn"], dn0 + 8 * tc)
                                other, dstb = xnc, xs_sb
                            for gr in range(2):
                                vector.wait_ge(sems.h["s_tpB"],
                                               tpB0 + 2 * TBg + gr + 1)
                                dv = dstb.rearrange("p (k t) -> p k t", t=512)[
                                    :, gr * 4:(gr + 1) * 4, sub * 128:(sub + 1) * 128]
                                ov = other.rearrange("p (k t) -> p k t", t=512)[
                                    :, gr * 4:(gr + 1) * 4, sub * 128:(sub + 1) * 128]
                                sems.inc(vector.tensor_mul(
                                    dv,
                                    ps_mt[gr].rearrange("p (a b) -> p a b", b=128),
                                    ov), "s_pf", 1)
                    vector.wait_ge(sems.h["wnB"], wnB0 + 16 * (tc + 1))
                    for oc in range(8):
                        DN = 8 * tc + oc
                        vector.wait_ge(sems.h["s_dn"], dn0 + DN + 1)
                        vector.scalar_tensor_tensor(
                            out=outB[:, oc * 512:(oc + 1) * 512],
                            in0=ps_in[oc % 2], scalar=b8_sb[:, oc:oc + 1],
                            in1=outB[:, oc * 512:(oc + 1) * 512],
                            op0=ALU.add, op1=ALU.add)
                        sems.inc(vector.tensor_add(
                            outB[:, oc * 512:(oc + 1) * 512],
                            outB[:, oc * 512:(oc + 1) * 512],
                            p2c[:, oc * 512:(oc + 1) * 512]), "s_oev", 1)

    if dev_phase is not None:
        return nc
    return nc


def _layouts(inputs):
    g = {}
    g["l0_wqT"] = np.ascontiguousarray(inputs["L0_attn_Wq"].T)
    g["l0_wkT"] = np.ascontiguousarray(inputs["L0_attn_Wk"].T)
    g["l0_wvT"] = np.ascontiguousarray(inputs["L0_attn_Wv"].T)
    g["l0_woT"] = np.ascontiguousarray(inputs["L0_attn_Wo"].T)
    g["l0_cx"] = inputs["L0_ffn_clip_x"].reshape(1, 1)
    g["l0_cd"] = inputs["L0_ffn_clip_down"].reshape(1, 1)
    g["l0_wg"] = inputs["L0_ffn_w_gate"]
    g["l0_wu"] = inputs["L0_ffn_w_up"]
    g["l0_wd"] = inputs["L0_ffn_w_down"]
    g["l1_wq"] = inputs["L1_attn_Wq"]
    g["l1_wk"] = inputs["L1_attn_Wk"]
    g["l1_wv"] = inputs["L1_attn_Wv"]
    g["l1_wo"] = inputs["L1_attn_Wo"]
    g["l1_w"] = inputs["L1_ffn_weight"]
    for nm, cw in (("so_bd", inputs["L1_ffn_so_conv_w"]),
                   ("si_bd", inputs["L1_ffn_si_conv_w"])):
        bd = np.zeros((D, 256), np.float32)
        bd[np.arange(D), np.arange(D) // 4] = np.asarray(cw, np.float32).reshape(-1)
        g[nm] = bd
    def pack4(a, b):
        out = np.empty((128, 4), np.float32)
        out[:, 0:2] = np.asarray(a, np.float32).reshape(2, 128).T
        out[:, 2:4] = np.asarray(b, np.float32).reshape(2, 128).T
        return out
    g["cvb_c"] = pack4(inputs["L1_ffn_so_conv_b"], inputs["L1_ffn_si_conv_b"])
    g["gam_c"] = pack4(inputs["L1_ffn_so_gamma"], inputs["L1_ffn_si_gamma"])
    g["bet_c"] = pack4(inputs["L1_ffn_so_beta"], inputs["L1_ffn_si_beta"])
    g["so_linT"] = np.ascontiguousarray(np.asarray(inputs["L1_ffn_so_lin_w"], np.float32).T)
    g["si_linT"] = np.ascontiguousarray(np.asarray(inputs["L1_ffn_si_lin_w"], np.float32).T)
    g["so_lb"] = np.ascontiguousarray(np.broadcast_to(
        np.asarray(inputs["L1_ffn_so_lin_b"], np.float32), (128, D)))
    g["si_lb"] = np.ascontiguousarray(np.broadcast_to(
        np.asarray(inputs["L1_ffn_si_lin_b"], np.float32), (128, D)))
    g["l1_b8"] = np.ascontiguousarray(
        np.asarray(inputs["L1_ffn_bias"], np.float32).reshape(8, 128).T)
    return g


def kernel(**inputs):
    nc = build_nc()
    shared = _layouts(inputs)
    x = np.ascontiguousarray(inputs["L0_attn_x"].astype(np.float32, copy=False))
    in_maps = []
    for c in range(NCORES):
        m = dict(shared)
        m["x_in"] = np.ascontiguousarray(
            x[c * SEQ_PER_CORE:(c + 1) * SEQ_PER_CORE].reshape(T, D))
        in_maps.append(m)
    res = run_bass_kernel_spmd(nc, in_maps, core_ids=list(range(NCORES)))
    out = np.concatenate([r["out"].reshape(SEQ_PER_CORE, SEQ, D)
                          for r in res.results], axis=0)
    return out


# version 06
# speedup vs baseline: 1.0457x; 1.0457x over previous
import contextlib

import numpy as np

import concourse.bass as bass
import concourse.mybir as mybir
from concourse.bass_utils import run_bass_kernel_spmd

F32 = mybir.dt.float32
BF16 = mybir.dt.bfloat16
AF = mybir.ActivationFunctionType
ALU = mybir.AluOpType
AX = mybir.AxisListType

NCORES = 8
BATCH, SEQ, D = 32, 2048, 1024
SEQ_PER_CORE = BATCH // NCORES
T = SEQ_PER_CORE * SEQ
TC = 512
NTC = T // TC
KT = D // 128
I_FFN = 2816
IT = I_FFN // 128
NHEAD, HD, WIN = 16, 64, 128
NCH = T // 128
RC = 12582912.0
LN_EPS = 1e-5
ATT_EPS = 1e-6
BN_EPS = 1e-5
QMAX = 127.0


class Sems:

    def __init__(self, nc, names):
        self.h = {}
        self.c = {}
        self._cm = []
        for n in names:
            cm = nc.semaphore(n)
            sem = cm.__enter__()
            self._cm.append(cm)
            self.h[n] = sem
            self.c[n] = 0

    def inc(self, ins, name, n=1):
        self.c[name] += n
        return ins.then_inc(self.h[name], n)

    def at(self, name):
        return self.c[name]


def build_nc(dev_phase=None, nch_limit=None, p1b_stage=5, ncl_limit=None,
             race_detect=False):
    nc = bass.Bass("TRN2", target_bir_lowering=False,
                   detect_race_conditions=race_detect)

    def kind_for(phase):
        if dev_phase is not None and phase <= dev_phase:
            return "ExternalOutput"
        return "Internal"

    x_in = nc.dram_tensor("x_in", [T, D], F32, kind="ExternalInput").ap()
    l0_wqT = nc.dram_tensor("l0_wqT", [D, D], F32, kind="ExternalInput").ap()
    l0_wkT = nc.dram_tensor("l0_wkT", [D, D], F32, kind="ExternalInput").ap()
    l0_wvT = nc.dram_tensor("l0_wvT", [D, D], F32, kind="ExternalInput").ap()
    l0_woT = nc.dram_tensor("l0_woT", [D, D], F32, kind="ExternalInput").ap()
    l0_cx = nc.dram_tensor("l0_cx", [1, 1], F32, kind="ExternalInput").ap()
    l0_cd = nc.dram_tensor("l0_cd", [1, 1], F32, kind="ExternalInput").ap()
    l0_wg = nc.dram_tensor("l0_wg", [I_FFN, D], F32, kind="ExternalInput").ap()
    l0_wu = nc.dram_tensor("l0_wu", [I_FFN, D], F32, kind="ExternalInput").ap()
    l0_wd = nc.dram_tensor("l0_wd", [D, I_FFN], F32, kind="ExternalInput").ap()
    l1_wq = nc.dram_tensor("l1_wq", [D, D], F32, kind="ExternalInput").ap()
    l1_wk = nc.dram_tensor("l1_wk", [D, D], F32, kind="ExternalInput").ap()
    l1_wv = nc.dram_tensor("l1_wv", [D, D], F32, kind="ExternalInput").ap()
    l1_wo = nc.dram_tensor("l1_wo", [D, D], F32, kind="ExternalInput").ap()
    l1_w = nc.dram_tensor("l1_w", [D, D], F32, kind="ExternalInput").ap()
    so_bd = nc.dram_tensor("so_bd", [D, 256], F32, kind="ExternalInput").ap()
    si_bd = nc.dram_tensor("si_bd", [D, 256], F32, kind="ExternalInput").ap()
    cvb_c = nc.dram_tensor("cvb_c", [128, 4], F32, kind="ExternalInput").ap()
    gam_c = nc.dram_tensor("gam_c", [128, 4], F32, kind="ExternalInput").ap()
    bet_c = nc.dram_tensor("bet_c", [128, 4], F32, kind="ExternalInput").ap()
    so_linT = nc.dram_tensor("so_linT", [256, D], F32, kind="ExternalInput").ap()
    si_linT = nc.dram_tensor("si_linT", [256, D], F32, kind="ExternalInput").ap()
    so_lb = nc.dram_tensor("so_lb", [128, D], F32, kind="ExternalInput").ap()
    si_lb = nc.dram_tensor("si_lb", [128, D], F32, kind="ExternalInput").ap()
    l1_b8 = nc.dram_tensor("l1_b8", [128, 8], F32, kind="ExternalInput").ap()
    out_h = nc.dram_tensor("out", [T, D], F32, kind="ExternalOutput").ap()

    xnT_h = nc.dram_tensor("xnT_h", [D, T], F32, kind=kind_for(0)).ap()
    xT_h = nc.dram_tensor("xT_h", [D, T], F32, kind=kind_for(0)).ap()
    qT_h = nc.dram_tensor("qT_h", [D, T], F32, kind=kind_for(1)).ap()
    kT_h = nc.dram_tensor("kT_h", [D, T], F32, kind=kind_for(1)).ap()
    v_h = nc.dram_tensor("v_h", [T, D], F32, kind=kind_for(1)).ap()
    p1T_h = nc.dram_tensor("p1T_h", [D, T], F32, kind=kind_for(2)).ap()
    h_h = nc.dram_tensor("h_h", [I_FFN, T], F32, kind=kind_for(3)).ap()
    x1T_h = nc.dram_tensor("x1T_h", [D, T], F32, kind=kind_for(4)).ap()
    xn1T_h = nc.dram_tensor("xn1T_h", [D, T], F32, kind=kind_for(5)).ap()
    q1T_h = nc.dram_tensor("q1T_h", [D, T], F32, kind=kind_for(5)).ap()
    k1T_h = nc.dram_tensor("k1T_h", [D, T], F32, kind=kind_for(5)).ap()
    v1_h = nc.dram_tensor("v1_h", [T, D], F32, kind=kind_for(5)).ap()
    fullT_h = nc.dram_tensor("fullT_h", [D, T], F32, kind=kind_for(6)).ap()
    xcT_h = nc.dram_tensor("xcT_h", [512, T], F32, kind=kind_for(6)).ap()
    p2T_h = nc.dram_tensor("p2T_h", [D, T], F32, kind=kind_for(6)).ap()

    ccx_in = nc.dram_tensor("ccx_in", [1, 1], F32, kind="Internal")
    ccx_out = nc.dram_tensor("ccx_out", [1, 1], F32, kind="Internal")
    cch_in = nc.dram_tensor("cch_in", [1, 1], F32, kind="Internal")
    cch_out = nc.dram_tensor("cch_out", [1, 1], F32, kind="Internal")
    ccbn_in = nc.dram_tensor("ccbn_in", [128, 8], F32, kind="Internal")
    ccbn_out = nc.dram_tensor("ccbn_out", [128, 8], F32, kind="Internal")

    gmax_dbg = None
    if dev_phase is not None:
        gmax_dbg = nc.dram_tensor("gmax_dbg", [1, 1], F32, kind="ExternalOutput").ap()

    import ml_dtypes
    ident_c = nc.inline_tensor(np.eye(128, dtype=np.float32), "ident_c")
    identb_c = nc.inline_tensor(np.eye(128, dtype=ml_dtypes.bfloat16), "identb_c")
    _i = np.arange(128)[:, None]
    _j = np.arange(256)[None, :]
    _band = ((_j >= _i + 1) & (_j <= _i + WIN)).astype(np.float32)
    maskG_c = nc.inline_tensor(np.ascontiguousarray(_band), "maskG_c")
    mask0_c = nc.inline_tensor(
        np.ascontiguousarray(_band * (_j >= WIN)), "mask0_c")
    _jj = np.arange(128)[:, None]
    _ii = np.arange(256)[None, :]
    l1m0_c = nc.inline_tensor(
        np.ascontiguousarray((_jj <= _ii).astype(np.float32)), "l1m0_c")
    l1m1_c = nc.inline_tensor(
        np.ascontiguousarray(((_jj + 128) <= _ii).astype(np.float32)), "l1m1_c")
    _il = np.zeros((128, 64), np.float32)
    _il[np.arange(128), np.arange(128) % 64] = 1.0
    identL_c = nc.inline_tensor(_il, "identL_c")

    sems = Sems(nc, [
        "s_ld", "s_ld2", "s_st", "s_st2", "s_sq", "s_sqr", "s_var", "s_std",
        "s_xn", "s_tpA", "s_tpB", "s_evA", "s_evB", "s_mm", "s_cc", "s_misc",
        "s_vh", "s_pj", "s_pjA", "s_pjB", "s_sc", "s_exp", "s_pf", "s_pt",
        "s_ptev", "s_av", "s_oev", "s_wo", "s_p1ev",
        "s_ld3", "s_wq", "s_wT", "s_scl", "s_qx", "s_gu", "s_guev", "s_hm",
        "s_dn", "s_x1ev", "s_guA", "s_bc",
        "s_ln", "s_lnb", "s_fm",
        "s_A", "s_msk", "s_den", "s_dnv", "s_rdn", "s_bcm", "s_num", "s_yml",
        "s_ktr", "s_knv", "s_bce", "s_Smm", "s_Sup", "s_seq", "s_p2", "s_wo2",
        "s_cst",
        "xldA", "xldB", "sxnA", "sxnB", "sxtA", "sxtB",
        "xiA", "xiB", "sq1s", "sk1s", "sv1s",
        "qkvA", "qkvB", "qkvC", "xrA", "xrB", "sp1",
        "wnA", "wnB", "xqA", "xqB", "shs",
        "wdA", "wdB", "hin1", "p1in1", "sx1s",
        "x1inA", "sxn1A", "sxn1B", "qcA", "qcB", "x1rA", "sp2",
        "gmld", "chld", "sout",
    ])

    ident = nc.alloc_sbuf_tensor("ident", [128, 128], F32).ap()
    identb = nc.alloc_sbuf_tensor("identb", [128, 128], BF16).ap()
    cst = nc.alloc_sbuf_tensor("cst", [128, 8], F32).ap()
    ones_r = nc.alloc_sbuf_tensor("ones_r", [1, 128], F32).ap()
    ones_c = nc.alloc_sbuf_tensor("ones_c", [128, 1], F32).ap()

    x_in_t = x_in.rearrange("(c a p) d -> c p a d", a=4, p=128)
    xnT_h_t = xnT_h.rearrange("(k p) t -> p k t", p=128)
    xT_h_t = xT_h.rearrange("(k p) t -> p k t", p=128)
    qT_h_t = qT_h.rearrange("(k p) t -> p k t", p=128)
    kT_h_t = kT_h.rearrange("(k p) t -> p k t", p=128)
    v_h_t = v_h.rearrange("(c a p) d -> c p a d", a=4, p=128)
    p1T_h_t = p1T_h.rearrange("(k p) t -> p k t", p=128)

    with contextlib.ExitStack() as ctx:
        sb = lambda nm, shp, dt=F32: ctx.enter_context(nc.sbuf_tensor(nm, shp, dt)).ap()
        ps = lambda nm, shp: ctx.enter_context(nc.psum_tensor(nm, shp, F32)).ap()
        xb = [sb(f"xb{i}", [128, 4 * 1024]) for i in range(2)]
        sqb = [sb(f"sqb{i}", [128, 1024]) for i in range(2)]
        xnb = [sb(f"xnb{i}", [128, 4 * 1024]) for i in range(2)]
        xnT_sb = [sb(f"xnT_sb{i}", [128, 8 * 512]) for i in range(2)]
        xT_sb = [sb(f"xT_sb{i}", [128, 8 * 512]) for i in range(2)]
        st_m = [sb(f"st_m{i}", [128, 4]) for i in range(2)]
        st_r = [sb(f"st_r{i}", [128, 4]) for i in range(2)]
        st_s1 = sb("st_s1", [128, 4])
        st_s2 = sb("st_s2", [128, 4])
        st_var = [sb(f"st_var{i}", [128, 4]) for i in range(2)]
        st_std = [sb(f"st_std{i}", [128, 4]) for i in range(2)]
        amax_run = sb("amax_run", [128, 1])
        amax_t = sb("amax_t", [128, 1])
        amax_row = sb("amax_row", [1, 128])
        gmax_sb = sb("gmax_sb", [1, 1])
        ps_tp = [ps(f"ps_tp{i}", [128, 512]) for i in range(4)]
        ps_sm = ps("ps_sm", [128, 128])

        with nc.Block() as blk:
            @blk.sync
            def _(sync):
                sync.dma_start(out=ident, in_=ident_c.ap()).then_inc(sems.h["s_misc"], 16)
                for tc in range(NTC):
                    if tc >= 2:
                        sync.wait_ge(sems.h["s_xn"], tc - 1)
                        sync.wait_ge(sems.h["s_sq"], 4 * (tc - 1))
                        sync.wait_ge(sems.h["s_tpB"], 8 * (tc - 1))
                    sems.inc(sync.dma_start(
                        out=xb[tc % 2].rearrange("p (a d) -> p a d", a=4),
                        in_=x_in_t[tc]), "xldA" if tc % 2 == 0 else "xldB", 16)

            @blk.scalar
            def _(scalar):
                for tc in range(NTC):
                    scalar.wait_ge(sems.h["xldA" if tc % 2 == 0 else "xldB"],
                                   16 * (tc // 2 + 1))
                    for sub in range(4):
                        q = 4 * tc + sub
                        if q >= 2:
                            scalar.wait_ge(sems.h["s_sqr"], q - 1)
                        sems.inc(scalar.activation(
                            out=sqb[q % 2],
                            in_=xb[tc % 2][:, sub * 1024:(sub + 1) * 1024],
                            func=AF.Square), "s_sq", 1)
                    scalar.wait_ge(sems.h["s_var"], tc + 1)
                    if tc == 0:
                        scalar.wait_ge(sems.h["s_misc"], 16)
                        scalar.wait_ge(sems.h["s_cst"], 1)
                    sems.inc(scalar.activation(
                        out=st_std[tc % 2], in_=st_var[tc % 2],
                        func=AF.Sqrt, bias=cst[:, 0:1]), "s_std", 1)
                    if tc >= 2:
                        scalar.wait_ge(sems.h["sxnA" if tc % 2 == 0 else "sxnB"],
                                       16 * (tc // 2))
                    for j in range(8):
                        g = 8 * tc + j
                        scalar.wait_ge(sems.h["s_tpA"], g + 1)
                        sems.inc(scalar.copy(
                            out=xnT_sb[tc % 2][:, j * 512:(j + 1) * 512],
                            in_=ps_tp[j % 2]), "s_evA", 1)

            @blk.vector
            def _(vector):
                vector.memset(amax_run, 0.0)
                vector.memset(cst[:, 0:1], LN_EPS)
                vector.memset(cst[:, 1:2], RC)
                vector.memset(ones_c, 1.0)
                vector.memset(cst[:, 3:4], BN_EPS)
                sems.inc(vector.memset(cst[:, 2:3], 0.125), "s_cst", 1)
                for tc in range(NTC):
                    vector.wait_ge(sems.h["xldA" if tc % 2 == 0 else "xldB"],
                                   16 * (tc // 2 + 1))
                    for sub in range(4):
                        q = 4 * tc + sub
                        vector.reduce_sum(
                            st_s1[:, sub:sub + 1],
                            xb[tc % 2][:, sub * 1024:(sub + 1) * 1024], axis=AX.X)
                        vector.wait_ge(sems.h["s_sq"], q + 1)
                        sems.inc(vector.reduce_sum(
                            st_s2[:, sub:sub + 1], sqb[q % 2], axis=AX.X), "s_sqr", 1)
                    vector.tensor_scalar_mul(st_m[tc % 2], st_s1, 1.0 / D)
                    vector.tensor_scalar_mul(st_s2, st_s2, 1.0 / D)
                    sems.inc(vector.tensor_mul(st_s1, st_m[tc % 2], st_m[tc % 2]),
                             "s_vh", 1)
                    vector.wait_ge(sems.h["s_vh"], sems.at("s_vh"))
                    sems.inc(vector.tensor_sub(st_var[tc % 2], st_s2, st_s1),
                             "s_var", 1)
                    vector.wait_ge(sems.h["s_std"], tc + 1)
                    sems.inc(vector.reciprocal(st_r[tc % 2], st_std[tc % 2]), "s_vh", 1)
                    vector.wait_ge(sems.h["s_vh"], sems.at("s_vh"))
                    for sub in range(4):
                        vector.tensor_scalar(
                            xnb[tc % 2][:, sub * 1024:(sub + 1) * 1024],
                            xb[tc % 2][:, sub * 1024:(sub + 1) * 1024],
                            st_m[tc % 2][:, sub:sub + 1],
                            st_r[tc % 2][:, sub:sub + 1],
                            ALU.subtract, ALU.mult)
                    sems.inc(vector.reduce_max(
                        amax_t, xnb[tc % 2], axis=AX.X,
                        apply_absolute_value=True), "s_vh", 1)
                    vector.wait_ge(sems.h["s_vh"], sems.at("s_vh"))
                    sems.inc(vector.tensor_max(amax_run, amax_run, amax_t), "s_xn", 1)
                    if tc >= 2:
                        vector.wait_ge(sems.h["sxtA" if tc % 2 == 0 else "sxtB"],
                                       16 * (tc // 2))
                    for j in range(8):
                        g = 8 * tc + j
                        vector.wait_ge(sems.h["s_tpB"], g + 1)
                        sems.inc(vector.tensor_copy(
                            xT_sb[tc % 2][:, j * 512:(j + 1) * 512],
                            ps_tp[2 + j % 2]), "s_evB", 1)

            @blk.tensor
            def _(tensor):
                tensor.wait_ge(sems.h["s_misc"], 16)
                for tc in range(NTC):
                    tensor.wait_ge(sems.h["s_xn"], tc + 1)
                    for j in range(8):
                        g = 8 * tc + j
                        if g >= 2:
                            tensor.wait_ge(sems.h["s_evA"], g - 1)
                        for sub in range(4):
                            ins = tensor.transpose(
                                ps_tp[j % 2][:, sub * 128:(sub + 1) * 128],
                                xnb[tc % 2][:, sub * 1024 + j * 128: sub * 1024 + (j + 1) * 128],
                                ident)
                            if sub == 3:
                                sems.inc(ins, "s_tpA", 1)
                    for j in range(8):
                        g = 8 * tc + j
                        if g >= 2:
                            tensor.wait_ge(sems.h["s_evB"], g - 1)
                        for sub in range(4):
                            ins = tensor.transpose(
                                ps_tp[2 + j % 2][:, sub * 128:(sub + 1) * 128],
                                xb[tc % 2][:, sub * 1024 + j * 128: sub * 1024 + (j + 1) * 128],
                                ident)
                            if sub == 3:
                                sems.inc(ins, "s_tpB", 1)
                tensor.wait_ge(sems.h["s_xn"], NTC)
                sems.inc(tensor.transpose(ps_sm[0:1, :], amax_run, ident), "s_mm", 1)

            @blk.gpsimd
            def _(g):
                for tc in range(NTC):
                    g.wait_ge(sems.h["s_evA"], 8 * (tc + 1))
                    sems.inc(g.dma_start(
                        out=xnT_h_t[:, :, tc * 512:(tc + 1) * 512],
                        in_=xnT_sb[tc % 2].rearrange("p (k t) -> p k t", t=512)),
                        "sxnA" if tc % 2 == 0 else "sxnB", 16)
                    g.wait_ge(sems.h["s_evB"], 8 * (tc + 1))
                    sems.inc(g.dma_start(
                        out=xT_h_t[:, :, tc * 512:(tc + 1) * 512],
                        in_=xT_sb[tc % 2].rearrange("p (k t) -> p k t", t=512)),
                        "sxtA" if tc % 2 == 0 else "sxtB", 16)

            @blk.vector
            def _(vector):
                vector.wait_ge(sems.h["s_mm"], 1)
                sems.inc(vector.tensor_copy(amax_row, ps_sm[0:1, :]), "s_vh", 1)
                vector.wait_ge(sems.h["s_vh"], sems.at("s_vh"))
                sems.inc(vector.reduce_max(gmax_sb, amax_row, axis=AX.X), "s_mm", 1)

            @blk.gpsimd
            def _(g):
                g.wait_ge(sems.h["s_mm"], 2)
                sems.inc(g.dma_start(out=ccx_in.ap(), in_=gmax_sb), "s_st", 16)
                g.wait_ge(sems.h["s_st"], sems.at("s_st"))
                sems.inc(g.collective_compute(
                    "AllReduce", ALU.max,
                    replica_groups=[list(range(NCORES))],
                    ins=[ccx_in.ap()], outs=[ccx_out.ap()]), "s_cc", 1)

    if dev_phase is not None and dev_phase == 0:
        with nc.Block() as blk:
            @blk.gpsimd
            def _(g):
                g.wait_ge(sems.h["s_cc"], 1)
                sems.inc(g.dma_start(out=gmax_dbg, in_=ccx_out.ap()), "s_st", 16)
                g.wait_ge(sems.h["s_st"], sems.at("s_st"))
        return nc

    with contextlib.ExitStack() as ctx:
        sb = lambda nm, shp, dt=F32: ctx.enter_context(nc.sbuf_tensor(nm, shp, dt)).ap()
        pst = lambda nm, shp: ctx.enter_context(nc.psum_tensor(nm, shp, F32)).ap()
        wstg = sb("wstg", [128, 2 * 1024])
        wsp = {}
        for wn in ("q", "k", "v"):
            wsp[wn] = (sb(f"w{wn}h", [128, 8 * 1024], BF16),
                       sb(f"w{wn}l", [128, 8 * 1024], BF16))
        xin_s = sb("xin_s", [128, 8 * 512])
        xhi = [sb(f"xhi{i}", [128, 8 * 512], BF16) for i in range(2)]
        xlo = [sb(f"xlo{i}", [128, 8 * 512], BF16) for i in range(2)]
        qTo_sb = sb("qTo_sb", [128, 8 * 512])
        kTo_sb = sb("kTo_sb", [128, 8 * 512])
        vo_sb = sb("vo_sb", [128, 4 * 1024])
        ps_pj = [pst(f"ps_pj{i}", [128, 512]) for i in range(4)]

        st20 = sems.at("s_st2")
        std0 = sems.at("s_std")
        var0 = sems.at("s_var")
        wspl0 = sems.at("s_wq")
        sq0p1 = sems.at("s_sq")
        with nc.Block() as blk:
            @blk.sync
            def _(sync):
                for nm in ("sxnA", "sxnB", "sxtA", "sxtB"):
                    sync.wait_ge(sems.h[nm], sems.at(nm))
                for wi, w_h in enumerate((l0_wqT, l0_wkT, l0_wvT)):
                    for hf in range(4):
                        WH = 4 * wi + hf
                        if WH >= 1:
                            sync.wait_ge(sems.h["s_wq"], wspl0 + WH)
                        sems.inc(sync.dma_start(
                            out=wstg.rearrange("p (k d) -> p k d", d=1024),
                            in_=w_h.rearrange("(k p) d -> p k d", p=128)[
                                :, hf * 2:(hf + 1) * 2]),
                            "wnA" if WH % 2 == 0 else "wnB", 16)
                for tc in range(NTC):
                    if tc >= 1:
                        sync.wait_ge(sems.h["s_std"], std0 + tc)
                        sync.wait_ge(sems.h["s_var"], var0 + tc)
                    sems.inc(sync.dma_start(
                        out=xin_s.rearrange("p (k t) -> p k t", t=512),
                        in_=xnT_h_t[:, :, tc * 512:(tc + 1) * 512]),
                        "xiA" if tc % 2 == 0 else "xiB", 16)

            @blk.scalar
            def _(scalar):
                for nm in ("sxnA", "sxnB", "sxtA", "sxtB"):
                    scalar.wait_ge(sems.h[nm], sems.at(nm))
                for wi, wn in enumerate(("q", "k", "v")):
                    for hf in range(4):
                        WH = 4 * wi + hf
                        scalar.wait_ge(sems.h["wnA" if WH % 2 == 0 else "wnB"],
                                       16 * (WH // 2 + 1))
                        sems.inc(scalar.copy(
                            out=wsp[wn][0][:, hf * 2048:(hf + 1) * 2048],
                            in_=wstg), "s_sq", 1)
                for tc in range(NTC):
                    scalar.wait_ge(sems.h["xiA" if tc % 2 == 0 else "xiB"],
                                   16 * (tc // 2 + 1))
                    if tc >= 2:
                        scalar.wait_ge(sems.h["s_pj"], 24 * (tc - 1))
                    sems.inc(scalar.copy(out=xhi[tc % 2], in_=xin_s), "s_std", 1)

            @blk.vector
            def _(vector):
                for nm in ("sxnA", "sxnB", "sxtA", "sxtB"):
                    vector.wait_ge(sems.h[nm], sems.at(nm))
                for wi, wn in enumerate(("q", "k", "v")):
                    for hf in range(4):
                        WH = 4 * wi + hf
                        vector.wait_ge(sems.h["s_sq"], sq0p1 + WH + 1)
                        sems.inc(vector.tensor_sub(
                            wsp[wn][1][:, hf * 2048:(hf + 1) * 2048],
                            wstg, wsp[wn][0][:, hf * 2048:(hf + 1) * 2048]),
                            "s_wq", 1)
                for tc in range(NTC):
                    vector.wait_ge(sems.h["s_std"], std0 + tc + 1)
                    sems.inc(vector.tensor_sub(xlo[tc % 2], xin_s, xhi[tc % 2]),
                             "s_var", 1)
                    if tc >= 1:
                        vector.wait_ge(sems.h["sq1s"], 16 * tc)
                        vector.wait_ge(sems.h["sk1s"], 16 * tc)
                        vector.wait_ge(sems.h["sv1s"], 16 * tc)
                    for ti in range(24):
                        G = 24 * tc + ti
                        vector.wait_ge(sems.h["s_pj"], G + 1)
                        pb = ps_pj[G % 4]
                        if ti < 8:
                            dst = qTo_sb[:, ti * 512:(ti + 1) * 512]
                        elif ti < 16:
                            dst = kTo_sb[:, (ti - 8) * 512:(ti - 7) * 512]
                        else:
                            su, oh = (ti - 16) // 2, (ti - 16) % 2
                            dst = vo_sb[:, su * 1024 + oh * 512: su * 1024 + (oh + 1) * 512]
                        sems.inc(vector.tensor_copy(dst, pb),
                                 "s_pjA" if ti % 2 == 0 else "s_pjB", 1)

            @blk.tensor
            def _(tensor):
                tensor.wait_ge(sems.h["s_wq"], wspl0 + 12)
                for tc in range(NTC):
                    tensor.wait_ge(sems.h["s_std"], std0 + tc + 1)
                    tensor.wait_ge(sems.h["s_var"], var0 + tc + 1)
                    xh, xl = xhi[tc % 2], xlo[tc % 2]
                    tiles = [("q", oc) for oc in range(8)] + \
                            [("k", oc) for oc in range(8)] + \
                            [("v", (su, oh)) for su in range(4) for oh in range(2)]
                    for ti, (kindt, idx) in enumerate(tiles):
                        G = 24 * tc + ti
                        if G >= 4:
                            pG = G - 4
                            sname = "s_pjA" if pG % 2 == 0 else "s_pjB"
                            tensor.wait_ge(sems.h[sname], pG // 2 + 1)
                        pb = ps_pj[G % 4]
                        if kindt in ("q", "k"):
                            wh, wl = wsp[kindt]
                            oc = idx
                            np_ = 0
                            for k in range(8):
                                wsl = slice(k * 1024 + oc * 128,
                                            k * 1024 + (oc + 1) * 128)
                                xsl = slice(k * 512, (k + 1) * 512)
                                for a, b in ((wh, xh), (wh, xl), (wl, xh)):
                                    ins = tensor.matmul(
                                        pb, a[:, wsl], b[:, xsl],
                                        start=(np_ == 0), stop=(np_ == 23))
                                    np_ += 1
                        else:
                            su, oh = idx
                            np_ = 0
                            for k in range(8):
                                xsl = slice(k * 512 + su * 128,
                                            k * 512 + (su + 1) * 128)
                                wsl = slice(k * 1024 + oh * 512,
                                            k * 1024 + (oh + 1) * 512)
                                for a, b in ((xh, wsp["v"][0]), (xh, wsp["v"][1]),
                                             (xl, wsp["v"][0])):
                                    ins = tensor.matmul(
                                        pb, a[:, xsl], b[:, wsl],
                                        start=(np_ == 0), stop=(np_ == 23))
                                    np_ += 1
                        sems.inc(ins, "s_pj", 1)

            @blk.gpsimd
            def _(g):
                for tc in range(NTC):
                    g.wait_ge(sems.h["s_pjA"], 12 * (tc + 1))
                    g.wait_ge(sems.h["s_pjB"], 12 * (tc + 1))
                    sems.inc(g.dma_start(
                        out=qT_h_t[:, :, tc * 512:(tc + 1) * 512],
                        in_=qTo_sb.rearrange("p (k t) -> p k t", t=512)), "sq1s", 16)
                    sems.inc(g.dma_start(
                        out=kT_h_t[:, :, tc * 512:(tc + 1) * 512],
                        in_=kTo_sb.rearrange("p (k t) -> p k t", t=512)), "sk1s", 16)
                    sems.inc(g.dma_start(
                        out=v_h_t[tc],
                        in_=vo_sb.rearrange("p (a d) -> p a d", d=1024)), "sv1s", 16)

    with contextlib.ExitStack() as ctx:
        sb = lambda nm, shp, dt=F32: ctx.enter_context(nc.sbuf_tensor(nm, shp, dt)).ap()
        pst = lambda nm, shp: ctx.enter_context(nc.psum_tensor(nm, shp, F32)).ap()
        wo_sb = sb("wo_sb", [128, 8 * 1024])
        qb = [sb(f"qb{i}", [128, 8 * 128]) for i in range(3)]
        kroll = [sb(f"kroll{i}", [128, 8 * 128]) for i in range(3)]
        vroll = [sb(f"vroll{i}", [128, 1024]) for i in range(3)]
        p_sb = sb("p_sb", [128, 16 * 256])
        den_sb = sb("den_sb", [128, 16])
        rden_sb = sb("rden_sb", [128, 16])
        pT_sb = sb("pT_sb", [128, 16 * 256])
        attnT_sb = sb("attnT_sb", [128, 8 * 512])
        xr_sb = [sb(f"xr_sb{i}", [128, 8 * 512]) for i in range(2)]
        p1_sb = sb("p1_sb", [128, 8 * 512])
        maskG = sb("maskG", [128, 256])
        mask0 = sb("mask0", [128, 256])
        ps_sc = [pst(f"ps_sc{i}", [128, 256]) for i in range(2)]
        ps_pt = [pst(f"ps_pt{i}", [128, 256]) for i in range(2)]
        ps_ov = [pst(f"ps_ov{i}", [128, 128]) for i in range(2)]
        ps_wo = [pst(f"ps_wo{i}", [128, 512]) for i in range(2)]

        ld0 = sems.at("s_ld")
        cst0 = sems.at("s_cst") + 1
        NCHL = nch_limit if nch_limit is not None else NCH

        with nc.Block() as blk:
            @blk.sync
            def _(sync):
                sync.wait_ge(sems.h["s_st2"], sems.at("s_st2"))
                sems.inc(sync.dma_start(
                    out=wo_sb.rearrange("p (k d) -> p k d", d=1024),
                    in_=l0_woT.rearrange("(k p) d -> p k d", p=128)), "s_ld", 16)
                sems.inc(sync.dma_start(out=maskG, in_=maskG_c.ap()), "s_ld", 16)
                sems.inc(sync.dma_start(out=mask0, in_=mask0_c.ap()), "s_ld", 16)
                cst0s = sems.at("s_cst")
                for c in range(NCHL):
                    if c == 2:
                        sync.wait_ge(sems.h["s_cst"], cst0s + 1)
                    if c >= 3:
                        sync.wait_ge(sems.h["s_sc"], 16 * (c - 1))
                        sync.wait_ge(sems.h["s_av"], 16 * (c - 1))
                    qkv = ("qkvA", "qkvB", "qkvC")[c % 3]
                    sems.inc(sync.dma_start(
                        out=qb[c % 3].rearrange("p (k t) -> p k t", t=128),
                        in_=qT_h_t[:, :, c * 128:(c + 1) * 128]), qkv, 16)
                    sems.inc(sync.dma_start(
                        out=kroll[c % 3].rearrange("p (k t) -> p k t", t=128),
                        in_=kT_h_t[:, :, c * 128:(c + 1) * 128]), qkv, 16)
                    sems.inc(sync.dma_start(
                        out=vroll[c % 3], in_=v_h[c * 128:(c + 1) * 128, :]), qkv, 16)
                    if c % 4 == 0:
                        grp = c // 4
                        if grp >= 2:
                            sync.wait_ge(sems.h["s_p1ev"], 8 * (grp - 1))
                        sems.inc(sync.dma_start(
                            out=xr_sb[grp % 2].rearrange("p (k t) -> p k t", t=512),
                            in_=xT_h_t[:, :, grp * 512:(grp + 1) * 512]),
                            "xrA" if grp % 2 == 0 else "xrB", 16)

            @blk.vector
            def _(vector):
                for nm in ("sq1s", "sk1s", "sv1s"):
                    vector.wait_ge(sems.h[nm], sems.at(nm))
                vector.memset(kroll[2], 0.0)
                sems.inc(vector.memset(vroll[2], 0.0), "s_cst", 1)
                for c in range(NCHL):
                    if p1b_stage < 2:
                        break
                    mask = mask0 if c % 16 == 0 else maskG
                    for h in range(16):
                        E = 16 * c + h
                        vector.wait_ge(sems.h["s_exp"], E + 1)
                        vector.tensor_mul(p_sb[:, h * 256:(h + 1) * 256],
                                          p_sb[:, h * 256:(h + 1) * 256], mask)
                        ins = vector.reduce_sum(den_sb[:, h:h + 1],
                                                p_sb[:, h * 256:(h + 1) * 256],
                                                axis=AX.X)
                        if h == 15:
                            sems.inc(ins, "s_vh", 1)
                    vector.wait_ge(sems.h["s_vh"], sems.at("s_vh"))
                    sems.inc(vector.reciprocal(rden_sb, den_sb), "s_vh", 1)
                    vector.wait_ge(sems.h["s_vh"], sems.at("s_vh"))
                    for h in range(16):
                        sems.inc(vector.tensor_scalar_mul(
                            p_sb[:, h * 256:(h + 1) * 256],
                            p_sb[:, h * 256:(h + 1) * 256],
                            rden_sb[:, h:h + 1]), "s_pf", 1)
                    for hp in range(8):
                        if p1b_stage < 4:
                            break
                        O = 8 * c + hp
                        if c % 4 == 0 and c >= 4 and hp == 0:
                            vector.wait_ge(sems.h["s_wo"], 8 * (c // 4))
                        vector.wait_ge(sems.h["s_av"], 16 * c + 2 * hp + 2)
                        sems.inc(vector.tensor_copy(
                            attnT_sb[:, hp * 512 + (c % 4) * 128: hp * 512 + (c % 4 + 1) * 128],
                            ps_ov[hp % 2]), "s_oev", 1)
                    if c % 4 == 3 and p1b_stage >= 5:
                        grp = c // 4
                        vector.wait_ge(sems.h["xrA" if grp % 2 == 0 else "xrB"],
                                       16 * (grp // 2 + 1))
                        if grp >= 1:
                            vector.wait_ge(sems.h["sp1"], 16 * grp)
                        for oc in range(8):
                            vector.wait_ge(sems.h["s_wo"], 8 * grp + oc + 1)
                            sems.inc(vector.tensor_add(
                                p1_sb[:, oc * 512:(oc + 1) * 512],
                                ps_wo[oc % 2],
                                xr_sb[grp % 2][:, oc * 512:(oc + 1) * 512]), "s_p1ev", 1)

            @blk.scalar
            def _(scalar):
                for nm in ("sq1s", "sk1s", "sv1s"):
                    scalar.wait_ge(sems.h[nm], sems.at(nm))
                for c in range(NCHL):
                    for h in range(16):
                        E = 16 * c + h
                        scalar.wait_ge(sems.h["s_sc"], E + 1)
                        if c >= 1 and p1b_stage >= 3:
                            scalar.wait_ge(sems.h["s_pt"], 16 * (c - 1) + h + 1)
                        sems.inc(scalar.activation(
                            out=p_sb[:, h * 256:(h + 1) * 256],
                            in_=ps_sc[h % 2],
                            func=AF.Exp, scale=cst[:, 2:3]), "s_exp", 1)
                    for h in range(16):
                        if p1b_stage < 3:
                            break
                        E = 16 * c + h
                        scalar.wait_ge(sems.h["s_pt"], E + 1)
                        if c >= 1 and p1b_stage >= 4:
                            scalar.wait_ge(sems.h["s_av"], 16 * (c - 1) + h + 1)
                        sems.inc(scalar.copy(
                            out=pT_sb[:, h * 256:(h + 1) * 256],
                            in_=ps_pt[h % 2]), "s_ptev", 1)

            @blk.tensor
            def _(tensor):
                tensor.wait_ge(sems.h["s_ld"], ld0 + 48)
                tensor.wait_ge(sems.h["s_cst"], cst0)
                for c in range(NCHL):
                    cm, pm = c % 3, (c - 1) % 3
                    tensor.wait_ge(sems.h[("qkvA", "qkvB", "qkvC")[c % 3]],
                                   48 * (c // 3 + 1))
                    for h in range(16):
                        E = 16 * c + h
                        if E >= 2:
                            tensor.wait_ge(sems.h["s_exp"], E - 1)
                        qs = qb[cm][(h % 2) * 64:(h % 2) * 64 + 64,
                                    (h // 2) * 128:(h // 2 + 1) * 128]
                        tensor.matmul(
                            ps_sc[h % 2][:, 0:128], qs,
                            kroll[pm][(h % 2) * 64:(h % 2) * 64 + 64,
                                      (h // 2) * 128:(h // 2 + 1) * 128],
                            start=True, stop=True)
                        sems.inc(tensor.matmul(
                            ps_sc[h % 2][:, 128:256], qs,
                            kroll[cm][(h % 2) * 64:(h % 2) * 64 + 64,
                                      (h // 2) * 128:(h // 2 + 1) * 128],
                            start=True, stop=True), "s_sc", 1)
                    for h in range(16):
                        if p1b_stage < 3:
                            break
                        E = 16 * c + h
                        tensor.wait_ge(sems.h["s_pf"], E + 1)
                        if E >= 2:
                            tensor.wait_ge(sems.h["s_ptev"], E - 1)
                        tensor.transpose(ps_pt[h % 2][:, 0:128],
                                         p_sb[:, h * 256:h * 256 + 128], ident)
                        sems.inc(tensor.transpose(
                            ps_pt[h % 2][:, 128:256],
                            p_sb[:, h * 256 + 128:(h + 1) * 256], ident), "s_pt", 1)
                    for h in range(16):
                        if p1b_stage < 4:
                            break
                        E = 16 * c + h
                        hp = h // 2
                        O = 8 * c + hp
                        tensor.wait_ge(sems.h["s_ptev"], E + 1)
                        if h % 2 == 0 and O >= 2:
                            tensor.wait_ge(sems.h["s_oev"], O - 1)
                        dst = ps_ov[hp % 2][(h % 2) * 64:(h % 2) * 64 + 64, :]
                        tensor.matmul(dst, vroll[pm][:, h * 64:(h + 1) * 64],
                                      pT_sb[:, h * 256:h * 256 + 128],
                                      start=True, stop=False)
                        sems.inc(tensor.matmul(
                            dst, vroll[cm][:, h * 64:(h + 1) * 64],
                            pT_sb[:, h * 256 + 128:(h + 1) * 256],
                            start=False, stop=True), "s_av", 1)
                    if c % 4 == 3 and p1b_stage >= 5:
                        grp = c // 4
                        tensor.wait_ge(sems.h["s_oev"], 8 * (c + 1))
                        for oc in range(8):
                            W = 8 * grp + oc
                            if W >= 2:
                                tensor.wait_ge(sems.h["s_p1ev"], W - 1)
                            for k in range(8):
                                ins = tensor.matmul(
                                    ps_wo[oc % 2],
                                    wo_sb[:, k * 1024 + oc * 128: k * 1024 + (oc + 1) * 128],
                                    attnT_sb[:, k * 512:(k + 1) * 512],
                                    start=(k == 0), stop=(k == 7))
                            sems.inc(ins, "s_wo", 1)

            @blk.gpsimd
            def _(g):
                for grp in range(NCHL // 4 if p1b_stage >= 5 else 0):
                    g.wait_ge(sems.h["s_p1ev"], 8 * (grp + 1))
                    sems.inc(g.dma_start(
                        out=p1T_h_t[:, :, grp * 512:(grp + 1) * 512],
                        in_=p1_sb.rearrange("p (k t) -> p k t", t=512)), "sp1", 16)

    if dev_phase is not None and dev_phase == 2:
        with nc.Block() as blk:
            @blk.gpsimd
            def _(g):
                g.wait_ge(sems.h["s_cc"], 1)
                g.wait_ge(sems.h["s_st"], sems.at("s_st"))
                sems.inc(g.dma_start(out=gmax_dbg, in_=ccx_out.ap()), "s_st", 16)
                g.wait_ge(sems.h["s_st"], sems.at("s_st"))
        return nc

    h_h_t = h_h.rearrange("(k p) t -> p k t", p=128)
    x1T_h_t = x1T_h.rearrange("(k p) t -> p k t", p=128)

    with contextlib.ExitStack() as ctx:
        sb = lambda nm, shp, dt=F32: ctx.enter_context(nc.sbuf_tensor(nm, shp, dt)).ap()
        pst = lambda nm, shp: ctx.enter_context(nc.psum_tensor(nm, shp, F32)).ap()
        wgqT = sb("wgqT", [128, 8 * I_FFN], BF16)
        wuqT = sb("wuqT", [128, 8 * I_FFN], BF16)
        wnat = [sb(f"wnat{i}", [128, 1024]) for i in range(2)]
        wq_s = sb("wq_s", [128, 1024])
        ws_g = sb("ws_g", [128, 22])
        ws_u = sb("ws_u", [128, 22])
        wsx_g = sb("wsx_g", [128, 22])
        wsx_u = sb("wsx_u", [128, 22])
        wiv = sb("wiv", [128, 2])
        xq_in = [sb(f"xq_in{i}", [128, 8 * 512]) for i in range(2)]
        xq_bf = [sb(f"xq_bf{i}", [128, 8 * 512], BF16) for i in range(2)]
        uq_s = sb("uq_s", [128, 512])
        h_ch = sb("h_ch", [128, 22 * 512])
        gs_t = sb("gs_t", [128, 512])
        us_t = sb("us_t", [128, 512])
        hm_all = sb("hm_all", [128, 22])
        hmax_run = sb("hmax_run", [128, 1])
        hmax_row = sb("hmax_row", [1, 128])
        hmax_sb = sb("hmax_sb", [1, 1])
        cx_sb = sb("cx_sb", [1, 1])
        gm_sb = sb("gm_sb", [1, 1])
        sig_sb = sb("sig_sb", [1, 1])
        t_sb = sb("t_sb", [1, 1])
        sx1 = sb("sx1", [1, 1])
        isx1 = sb("isx1", [1, 1])
        sxb = sb("sxb", [128, 1])
        invsx = sb("invsx", [128, 1])
        ps_w = [pst(f"ps_w{i}", [128, 512]) for i in range(2)]
        ps_g = [pst(f"ps_g{i}", [128, 512]) for i in range(2)]
        ps_u = [pst(f"ps_u{i}", [128, 512]) for i in range(2)]
        ps_b = pst("ps_b", [128, 128])

        ld0 = sems.at("s_ld3")
        st0 = sems.at("s_st")
        misc0 = sems.at("s_misc")
        sq0 = sems.at("s_sq")
        tpA0 = sems.at("s_tpA")
        wq0p2 = sems.at("s_wq")
        wnA0p2 = sems.at("wnA")
        wnB0p2 = sems.at("wnB")
        cst0p2 = sems.at("s_cst") + 1
        with nc.Block() as blk:
            @blk.sync
            def _(sync):
                sems.inc(sync.dma_start(out=identb, in_=identb_c.ap()), "s_ld3", 16)
                sems.inc(sync.dma_start(out=cx_sb, in_=l0_cx), "s_ld3", 16)
                for wi, w_src in enumerate((l0_wg, l0_wu)):
                    for it in range(IT):
                        W = wi * IT + it
                        if W >= 2:
                            sync.wait_ge(sems.h["s_wq"], wq0p2 + W - 1)
                            sync.wait_ge(sems.h["s_tpA"], tpA0 + 2 * (W - 1))
                        sems.inc(sync.dma_start(
                            out=wnat[W % 2],
                            in_=w_src[it * 128:(it + 1) * 128, :]),
                            "wnA" if W % 2 == 0 else "wnB", 16)
                for tc in range(NTC):
                    if tc >= 2:
                        sync.wait_ge(sems.h["s_qx"], 8 * (tc - 1))
                    sems.inc(sync.dma_start(
                        out=xq_in[tc % 2].rearrange("p (k t) -> p k t", t=512),
                        in_=xnT_h_t[:, :, tc * 512:(tc + 1) * 512]),
                        "xqA" if tc % 2 == 0 else "xqB", 16)

            @blk.gpsimd
            def _(g):
                g.wait_ge(sems.h["s_cc"], 1)
                sems.inc(g.dma_start(out=gm_sb, in_=ccx_out.ap()), "gmld", 16)
                for tc in range(NTC):
                    g.wait_ge(sems.h["s_guev"], 22 * (tc + 1))
                    sems.inc(g.dma_start(
                        out=h_h_t[:, :, tc * 512:(tc + 1) * 512],
                        in_=h_ch.rearrange("p (k t) -> p k t", t=512)), "shs", 16)
                g.wait_ge(sems.h["s_hm"], NTC + 3)
                sems.inc(g.dma_start(out=cch_in.ap(), in_=hmax_sb), "s_st", 16)
                g.wait_ge(sems.h["s_st"], sems.at("s_st"))
                sems.inc(g.collective_compute(
                    "AllReduce", ALU.max,
                    replica_groups=[list(range(NCORES))],
                    ins=[cch_in.ap()], outs=[cch_out.ap()]), "s_cc", 1)

            @blk.scalar
            def _(scalar):
                scalar.wait_ge(sems.h["sp1"], sems.at("sp1"))
                scalar.wait_ge(sems.h["s_ld3"], ld0 + 32)
                sems.inc(scalar.activation(out=sig_sb, in_=cx_sb,
                                           func=AF.Sigmoid), "s_scl", 1)
                for wi in range(2):
                    wdst = wgqT if wi == 0 else wuqT
                    for it in range(IT):
                        W = wi * IT + it
                        for g4 in range(2):
                            TP4 = 2 * W + g4
                            scalar.wait_ge(sems.h["s_tpA"], tpA0 + TP4 + 1)
                            dst = wdst.rearrange("p (k i) -> p k i", i=I_FFN)[
                                :, g4 * 4:(g4 + 1) * 4, it * 128:(it + 1) * 128]
                            sems.inc(scalar.copy(
                                out=dst,
                                in_=ps_w[TP4 % 2].rearrange("p (a b) -> p a b", b=128)),
                                "s_wT", 1)
                for tc in range(NTC):
                    for ic in range(IT):
                        GU = IT * tc + ic
                        if GU == 0:
                            scalar.wait_ge(sems.h["s_scl"], 6)
                        scalar.wait_ge(sems.h["s_gu"], GU + 1)
                        if GU >= 1:
                            scalar.wait_ge(sems.h["s_guev"], GU)
                        sems.inc(scalar.activation(
                            out=gs_t, in_=ps_g[GU % 2],
                            func=AF.Silu, scale=wsx_g[:, ic:ic + 1]), "s_guA", 1)

            @blk.vector
            def _(vector):
                vector.wait_ge(sems.h["sp1"], sems.at("sp1"))
                sems.inc(vector.memset(ones_r, 1.0), "s_cst", 1)
                for wi in range(2):
                    ws_dst = ws_g if wi == 0 else ws_u
                    for it in range(IT):
                        W = wi * IT + it
                        vector.wait_ge(
                            sems.h["wnA" if W % 2 == 0 else "wnB"],
                            (wnA0p2 if W % 2 == 0 else wnB0p2) + 16 * (W // 2 + 1))
                        sems.inc(vector.reduce_max(
                            ws_dst[:, it:it + 1], wnat[W % 2], axis=AX.X,
                            apply_absolute_value=True), "s_vh", 1)
                        vector.wait_ge(sems.h["s_vh"], sems.at("s_vh"))
                        sems.inc(vector.reciprocal(wiv[:, W % 2:W % 2 + 1],
                                                   ws_dst[:, it:it + 1]), "s_vh", 1)
                        vector.wait_ge(sems.h["s_vh"], sems.at("s_vh"))
                        vector.tensor_scalar_mul(
                            wq_s, wnat[W % 2], wiv[:, W % 2:W % 2 + 1])
                        vector.tensor_scalar(wq_s, wq_s, QMAX, RC, ALU.mult, ALU.add)
                        vector.tensor_scalar(wq_s, wq_s, RC, QMAX, ALU.subtract, ALU.min)
                        sems.inc(vector.tensor_scalar(
                            wnat[W % 2], wq_s, -QMAX - 1.0, None, ALU.max),
                            "s_wq", 1)
                vector.wait_ge(sems.h["gmld"], 16)
                vector.wait_ge(sems.h["s_scl"], 1)
                sems.inc(vector.tensor_mul(t_sb, sig_sb, gm_sb), "s_vh", 1)
                vector.wait_ge(sems.h["s_vh"], sems.at("s_vh"))
                sems.inc(vector.tensor_scalar_mul(sx1, t_sb, 1.0 / QMAX), "s_vh", 1)
                vector.wait_ge(sems.h["s_vh"], sems.at("s_vh"))
                sems.inc(vector.reciprocal(isx1, sx1), "s_scl", 1)
                vector.wait_ge(sems.h["s_bc"], 2)
                vector.tensor_copy(sxb, ps_b[:, 0:1])
                sems.inc(vector.tensor_copy(invsx, ps_b[:, 1:2]), "s_vh", 1)
                vector.wait_ge(sems.h["s_vh"], sems.at("s_vh"))
                vector.wait_ge(sems.h["s_wq"], wq0p2 + 2 * IT)
                vector.tensor_scalar(wsx_g, ws_g, sxb, 1.0 / QMAX, ALU.mult, ALU.mult)
                sems.inc(vector.tensor_scalar(
                    wsx_u, ws_u, sxb, 1.0 / QMAX, ALU.mult, ALU.mult), "s_scl", 4)
                vector.memset(hmax_run, 0.0)
                for tc in range(NTC):
                    vector.wait_ge(sems.h["xqA" if tc % 2 == 0 else "xqB"],
                                   16 * (tc // 2 + 1))
                    for k in range(KT):
                        Q = 8 * tc + k
                        vector.tensor_scalar(
                            uq_s, xq_in[tc % 2][:, k * 512:(k + 1) * 512],
                            invsx, RC, ALU.mult, ALU.add)
                        vector.tensor_scalar(uq_s, uq_s, RC, QMAX,
                                             ALU.subtract, ALU.min)
                        sems.inc(vector.tensor_scalar_max(
                            xq_bf[tc % 2][:, k * 512:(k + 1) * 512],
                            uq_s, -QMAX - 1.0), "s_qx", 1)
                    if tc >= 1:
                        vector.wait_ge(sems.h["shs"], 16 * tc)
                    for ic in range(IT):
                        GU = IT * tc + ic
                        vector.wait_ge(sems.h["s_gu"], GU + 1)
                        vector.tensor_scalar_mul(us_t, ps_u[GU % 2],
                                                 wsx_u[:, ic:ic + 1])
                        vector.wait_ge(sems.h["s_guA"], GU + 1)
                        sems.inc(vector.tensor_mul(
                            h_ch[:, ic * 512:(ic + 1) * 512], gs_t, us_t),
                            "s_guev", 1)
                        vector.reduce_max(hm_all[:, ic:ic + 1],
                                          h_ch[:, ic * 512:(ic + 1) * 512],
                                          axis=AX.X, apply_absolute_value=True)
                    sems.inc(vector.engine_nop(), "s_vh", 1)
                    vector.wait_ge(sems.h["s_vh"], sems.at("s_vh"))
                    sems.inc(vector.reduce_max(hm_all[:, 0:1], hm_all, axis=AX.X),
                             "s_vh", 1)
                    vector.wait_ge(sems.h["s_vh"], sems.at("s_vh"))
                    sems.inc(vector.tensor_max(hmax_run, hmax_run, hm_all[:, 0:1]),
                             "s_hm", 1)
                vector.wait_ge(sems.h["s_hm"], NTC + 1)
                sems.inc(vector.tensor_copy(hmax_row, ps_b[0:1, 0:128]), "s_vh", 1)
                vector.wait_ge(sems.h["s_vh"], sems.at("s_vh"))
                sems.inc(vector.reduce_max(hmax_sb, hmax_row, axis=AX.X), "s_hm", 2)

            @blk.tensor
            def _(tensor):
                tensor.wait_ge(sems.h["s_ld3"], ld0 + 16)
                for wi in range(2):
                    wdst = wgqT if wi == 0 else wuqT
                    for it in range(IT):
                        W = wi * IT + it
                        tensor.wait_ge(sems.h["s_wq"], wq0p2 + W + 1)
                        for k in range(KT):
                            TP = KT * W + k
                            if TP >= 8:
                                tensor.wait_ge(sems.h["s_wT"], (TP - 8) // 4 + 1)
                            ins = tensor.transpose(
                                ps_w[(TP // 4) % 2][:, (k % 4) * 128:(k % 4 + 1) * 128],
                                wnat[W % 2][:, k * 128:(k + 1) * 128], ident)
                            if k % 4 == 3:
                                sems.inc(ins, "s_tpA", 1)
                tensor.wait_ge(sems.h["s_cst"], cst0p2)
                tensor.wait_ge(sems.h["s_scl"], 2)
                tensor.matmul(ps_b[:, 0:1], ones_r, sx1, start=True, stop=True)
                sems.inc(tensor.matmul(ps_b[:, 1:2], ones_r, isx1,
                                       start=True, stop=True), "s_bc", 2)
                tensor.wait_ge(sems.h["s_wT"], 4 * IT)
                for tc in range(NTC):
                    for ic in range(IT):
                        GU = IT * tc + ic
                        if ic == 0:
                            tensor.wait_ge(sems.h["s_qx"], 8 * (tc + 1))
                        if GU >= 2:
                            tensor.wait_ge(sems.h["s_guA"], GU - 1)
                            tensor.wait_ge(sems.h["s_guev"], GU - 1)
                        for k in range(KT):
                            tensor.matmul(
                                ps_g[GU % 2],
                                wgqT[:, k * I_FFN + ic * 128: k * I_FFN + (ic + 1) * 128],
                                xq_bf[tc % 2][:, k * 512:(k + 1) * 512],
                                start=(k == 0), stop=(k == 7))
                        for k in range(KT):
                            ins = tensor.matmul(
                                ps_u[GU % 2],
                                wuqT[:, k * I_FFN + ic * 128: k * I_FFN + (ic + 1) * 128],
                                xq_bf[tc % 2][:, k * 512:(k + 1) * 512],
                                start=(k == 0), stop=(k == 7))
                        sems.inc(ins, "s_gu", 1)
                tensor.wait_ge(sems.h["s_hm"], NTC)
                sems.inc(tensor.transpose(ps_b[0:1, 0:128], hmax_run, ident),
                         "s_hm", 1)


    if dev_phase is not None and dev_phase == 3:
        with nc.Block() as blk:
            @blk.gpsimd
            def _(g):
                g.wait_ge(sems.h["s_cc"], 2)
                g.wait_ge(sems.h["s_st"], sems.at("s_st"))
                sems.inc(g.dma_start(out=gmax_dbg, in_=cch_out.ap()), "s_st", 16)
                g.wait_ge(sems.h["s_st"], sems.at("s_st"))
        return nc

    with contextlib.ExitStack() as ctx:
        sb = lambda nm, shp, dt=F32: ctx.enter_context(nc.sbuf_tensor(nm, shp, dt)).ap()
        pst = lambda nm, shp: ctx.enter_context(nc.psum_tensor(nm, shp, F32)).ap()
        wdqT = sb("wdqT", [128, IT * 1024], BF16)
        wdn = [sb(f"wdn{i}", [128, I_FFN]) for i in range(2)]
        wdq_s = sb("wdq_s", [128, I_FFN])
        ws_d = sb("ws_d", [128, 8])
        wsx_d = sb("wsx_d", [128, 8])
        wiv_d = sb("wiv_d", [128, 2])
        h_in = sb("h_in", [128, IT * 512])
        qh_bf = [sb(f"qh_bf{i}", [128, IT * 512], BF16) for i in range(2)]
        uq3 = sb("uq3", [128, 512])
        p1_in = sb("p1_in", [128, 8 * 512])
        x1_sb = sb("x1_sb", [128, 8 * 512])
        ch_sb = sb("ch_sb", [1, 1])
        cd_sb = sb("cd_sb", [1, 1])
        sigd_sb = sb("sigd_sb", [1, 1])
        td_sb = sb("td_sb", [1, 1])
        sh1 = sb("sh1", [1, 1])
        ish1 = sb("ish1", [1, 1])
        shb = sb("shb", [128, 1])
        invsh = sb("invsh", [128, 1])
        ps_w3 = [pst(f"ps_w3{i}", [128, 512]) for i in range(2)]
        ps_d = [pst(f"ps_d{i}", [128, 512]) for i in range(2)]
        ps_b3 = pst("ps_b3", [128, 128])

        ld0 = sems.at("s_ld3")
        st0 = sems.at("s_st")
        sq0 = sems.at("s_sq")
        tpA0 = sems.at("s_tpA")
        wq0 = sems.at("s_wq")
        wT0 = sems.at("s_wT")
        scl0 = sems.at("s_scl")
        qx0 = sems.at("s_qx")
        with nc.Block() as blk:
            @blk.sync
            def _(sync):
                sems.inc(sync.dma_start(out=cd_sb, in_=l0_cd), "s_ld3", 16)
                for nm in ("shs",):
                    sync.wait_ge(sems.h[nm], sems.at(nm))
                for oc in range(8):
                    if oc >= 2:
                        sync.wait_ge(sems.h["s_wq"], wq0 + oc - 1)
                        sync.wait_ge(sems.h["s_tpA"], tpA0 + 6 * (oc - 1))
                    sems.inc(sync.dma_start(
                        out=wdn[oc % 2],
                        in_=l0_wd[oc * 128:(oc + 1) * 128, :]),
                        "wdA" if oc % 2 == 0 else "wdB", 16)
                for tc in range(NTC):
                    if tc >= 1:
                        sync.wait_ge(sems.h["s_qx"], qx0 + IT * tc)
                    sems.inc(sync.dma_start(
                        out=h_in.rearrange("p (k t) -> p k t", t=512),
                        in_=h_h_t[:, :, tc * 512:(tc + 1) * 512]), "hin1", 16)
                    if tc >= 1:
                        sync.wait_ge(sems.h["s_x1ev"], 8 * tc)
                    sems.inc(sync.dma_start(
                        out=p1_in.rearrange("p (k t) -> p k t", t=512),
                        in_=p1T_h_t[:, :, tc * 512:(tc + 1) * 512]), "p1in1", 16)

            @blk.scalar
            def _(scalar):
                scalar.wait_ge(sems.h["shs"], sems.at("shs"))
                scalar.wait_ge(sems.h["s_ld3"], ld0 + 16)
                sems.inc(scalar.activation(out=sigd_sb, in_=cd_sb,
                                           func=AF.Sigmoid), "s_scl", 1)
                for oc in range(8):
                    for g6 in range(6):
                        TP4 = 6 * oc + g6
                        nblk = 4 if g6 < 5 else 2
                        scalar.wait_ge(sems.h["s_tpA"], tpA0 + TP4 + 1)
                        dst = wdqT.rearrange("p (k i) -> p k i", i=1024)[
                            :, g6 * 4:g6 * 4 + nblk, oc * 128:(oc + 1) * 128]
                        sems.inc(scalar.copy(
                            out=dst,
                            in_=ps_w3[TP4 % 2].rearrange(
                                "p (a b) -> p a b", b=128)[:, 0:nblk]),
                            "s_wT", 1)

            @blk.vector
            def _(vector):
                vector.wait_ge(sems.h["shs"], sems.at("shs"))
                for oc in range(8):
                    W = oc
                    vector.wait_ge(sems.h["wdA" if oc % 2 == 0 else "wdB"],
                                   16 * (oc // 2 + 1))
                    sems.inc(vector.reduce_max(
                        ws_d[:, oc:oc + 1], wdn[oc % 2], axis=AX.X,
                        apply_absolute_value=True), "s_vh", 1)
                    vector.wait_ge(sems.h["s_vh"], sems.at("s_vh"))
                    sems.inc(vector.reciprocal(wiv_d[:, oc % 2:oc % 2 + 1],
                                               ws_d[:, oc:oc + 1]), "s_vh", 1)
                    vector.wait_ge(sems.h["s_vh"], sems.at("s_vh"))
                    vector.tensor_scalar_mul(wdq_s, wdn[oc % 2],
                                              wiv_d[:, oc % 2:oc % 2 + 1])
                    vector.tensor_scalar(wdq_s, wdq_s, QMAX, RC, ALU.mult, ALU.add)
                    vector.tensor_scalar(wdq_s, wdq_s, RC, QMAX, ALU.subtract, ALU.min)
                    sems.inc(vector.tensor_scalar_max(
                        wdn[oc % 2], wdq_s, -QMAX - 1.0), "s_wq", 1)
                vector.wait_ge(sems.h["s_cc"], 2)
                vector.wait_ge(sems.h["chld"], 16)
                vector.wait_ge(sems.h["s_scl"], scl0 + 1)
                sems.inc(vector.tensor_mul(td_sb, sigd_sb, ch_sb), "s_vh", 1)
                vector.wait_ge(sems.h["s_vh"], sems.at("s_vh"))
                sems.inc(vector.tensor_scalar_mul(sh1, td_sb, 1.0 / QMAX), "s_vh", 1)
                vector.wait_ge(sems.h["s_vh"], sems.at("s_vh"))
                sems.inc(vector.reciprocal(ish1, sh1), "s_scl", 1)
                vector.wait_ge(sems.h["s_bc"], 4)
                vector.tensor_copy(shb, ps_b3[:, 0:1])
                sems.inc(vector.tensor_copy(invsh, ps_b3[:, 1:2]), "s_vh", 1)
                vector.wait_ge(sems.h["s_vh"], sems.at("s_vh"))
                vector.wait_ge(sems.h["s_wq"], wq0 + 8)
                sems.inc(vector.tensor_scalar(
                    wsx_d, ws_d, shb, 1.0 / QMAX, ALU.mult, ALU.mult), "s_scl", 4)
                for tc in range(NTC):
                    vector.wait_ge(sems.h["hin1"], 16 * (tc + 1))
                    for ic in range(IT):
                        Q = IT * tc + ic
                        vector.tensor_scalar(
                            uq3, h_in[:, ic * 512:(ic + 1) * 512],
                            invsh, RC, ALU.mult, ALU.add)
                        vector.tensor_scalar(uq3, uq3, RC, QMAX,
                                             ALU.subtract, ALU.min)
                        sems.inc(vector.tensor_scalar_max(
                            qh_bf[tc % 2][:, ic * 512:(ic + 1) * 512],
                            uq3, -QMAX - 1.0), "s_qx", 1)
                    vector.wait_ge(sems.h["p1in1"], 16 * (tc + 1))
                    if tc >= 1:
                        vector.wait_ge(sems.h["sx1s"], 16 * tc)
                    for oc in range(8):
                        DN = 8 * tc + oc
                        vector.wait_ge(sems.h["s_dn"], DN + 1)
                        sems.inc(vector.scalar_tensor_tensor(
                            out=x1_sb[:, oc * 512:(oc + 1) * 512],
                            in0=ps_d[DN % 2],
                            scalar=wsx_d[:, oc:oc + 1],
                            in1=p1_in[:, oc * 512:(oc + 1) * 512],
                            op0=ALU.mult, op1=ALU.add), "s_x1ev", 1)

            @blk.tensor
            def _(tensor):
                for oc in range(8):
                    tensor.wait_ge(sems.h["s_wq"], wq0 + oc + 1)
                    for k in range(IT):
                        TP = IT * oc + k
                        g6 = 6 * oc + (k // 4)
                        if g6 >= 2 and k % 4 == 0:
                            tensor.wait_ge(sems.h["s_wT"], wT0 + g6 - 1)
                        ins = tensor.transpose(
                            ps_w3[g6 % 2][:, (k % 4) * 128:(k % 4 + 1) * 128],
                            wdn[oc % 2][:, k * 128:(k + 1) * 128], ident)
                        if k % 4 == 3 or k == IT - 1:
                            sems.inc(ins, "s_tpA", 1)
                tensor.wait_ge(sems.h["s_scl"], scl0 + 2)
                tensor.matmul(ps_b3[:, 0:1], ones_r, sh1, start=True, stop=True)
                sems.inc(tensor.matmul(ps_b3[:, 1:2], ones_r, ish1,
                                       start=True, stop=True), "s_bc", 2)
                tensor.wait_ge(sems.h["s_wT"], wT0 + 48)
                for tc in range(NTC):
                    for oc in range(8):
                        DN = 8 * tc + oc
                        if oc == 0:
                            tensor.wait_ge(sems.h["s_qx"], qx0 + IT * (tc + 1))
                        if DN >= 2:
                            tensor.wait_ge(sems.h["s_x1ev"], DN - 1)
                        for k in range(IT):
                            ins = tensor.matmul(
                                ps_d[DN % 2],
                                wdqT[:, k * 1024 + oc * 128: k * 1024 + (oc + 1) * 128],
                                qh_bf[tc % 2][:, k * 512:(k + 1) * 512],
                                start=(k == 0), stop=(k == IT - 1))
                        sems.inc(ins, "s_dn", 1)

            @blk.gpsimd
            def _(g):
                g.wait_ge(sems.h["s_cc"], 2)
                sems.inc(g.dma_start(out=ch_sb, in_=cch_out.ap()), "chld", 16)
                for tc in range(NTC):
                    g.wait_ge(sems.h["s_x1ev"], 8 * (tc + 1))
                    sems.inc(g.dma_start(
                        out=x1T_h_t[:, :, tc * 512:(tc + 1) * 512],
                        in_=x1_sb.rearrange("p (k t) -> p k t", t=512)), "sx1s", 16)

    if dev_phase is not None and dev_phase == 4:
        with nc.Block() as blk:
            @blk.gpsimd
            def _(g):
                g.wait_ge(sems.h["s_st"], sems.at("s_st"))
                sems.inc(g.dma_start(out=gmax_dbg, in_=cch_out.ap()), "s_st", 16)
                g.wait_ge(sems.h["s_st"], sems.at("s_st"))
        return nc

    xn1T_h_t = xn1T_h.rearrange("(k p) t -> p k t", p=128)
    q1T_h_t = q1T_h.rearrange("(k p) t -> p k t", p=128)
    k1T_h_t = k1T_h.rearrange("(k p) t -> p k t", p=128)
    v1_h_t = v1_h.rearrange("(c a p) d -> c p a d", a=4, p=128)
    fullT_h_t = fullT_h.rearrange("(k p) t -> p k t", p=128)
    xcT_h_t = xcT_h.rearrange("(b p) t -> p b t", p=128)
    p2T_h_t = p2T_h.rearrange("(k p) t -> p k t", p=128)
    out_h_t = out_h.rearrange("(c a p) d -> c p a d", a=4, p=128)

    with contextlib.ExitStack() as ctx:
        sb = lambda nm, shp, dt=F32: ctx.enter_context(nc.sbuf_tensor(nm, shp, dt)).ap()
        pst = lambda nm, shp: ctx.enter_context(nc.psum_tensor(nm, shp, F32)).ap()
        w4 = {}
        for wn in ("q", "k", "v"):
            w4[wn] = (sb(f"w4{wn}h", [128, 8 * 1024], BF16),
                      sb(f"w4{wn}l", [128, 8 * 1024], BF16))
        x1in_ = sb("x1in", [128, 8 * 512])
        x1in = [x1in_, x1in_]
        xn1o_ = sb("xn1o", [128, 8 * 512])
        xn1o = [xn1o_, xn1o_]
        x4h = [sb(f"x4h{i}", [128, 8 * 512], BF16) for i in range(2)]
        x4l = [sb(f"x4l{i}", [128, 8 * 512], BF16) for i in range(2)]
        sq1 = sb("sq1", [128, 512])
        mrow = sb("mrow", [1, 512])
        e2row = sb("e2row", [1, 512])
        vrow = e2row
        srow = sb("srow", [1, 512])
        rrow = srow
        mb_sb = sb("mb_sb", [128, 512])
        rb_sb = sb("rb_sb", [128, 512])
        o1buf = sb("o1buf", [128, 8 * 512])
        o2buf = sb("o2buf", [128, 8 * 512])
        q1o, k1o, v1o = o1buf, o2buf, o1buf
        fm_e = sb("fm_e", [128, 512])
        ps_s1 = pst("ps_s1", [128, 512])
        ps_bc1 = pst("ps_bc1", [128, 512])
        ps_p4 = [pst(f"ps_p4{i}", [128, 512]) for i in range(4)]

        ld0 = sems.at("s_ld2")
        st0 = sems.at("s_st2")
        xn0 = sems.at("s_xn")
        var0 = sems.at("s_var")
        std0 = sems.at("s_std")
        mm0 = sems.at("s_mm")
        ln0 = sems.at("s_ln")
        lnb0 = sems.at("s_lnb")
        sq0v = sems.at("s_sq")
        pj0 = sems.at("s_pj")
        pjA0 = sems.at("s_pjA")
        pjB0 = sems.at("s_pjB")
        fm0 = sems.at("s_fm")
        evA0 = sems.at("s_evA")
        sq10 = sems.at("sq1s")
        sv10 = sems.at("sv1s")
        wq4 = sems.at("s_wq")
        sq4s = sems.at("s_sq")
        pj0s = sems.at("s_pj")
        qx4 = sems.at("s_qx")
        wnA4 = sems.at("wnA")
        wnB4 = sems.at("wnB")
        scl4 = sems.at("s_scl")
        with nc.Block() as blk:
            @blk.sync
            def _(sync):
                sync.wait_ge(sems.h["sx1s"], sems.at("sx1s"))
                for wi, w_hh in enumerate((l1_wq, l1_wk, l1_wv)):
                    for hf in range(4):
                        WH = 4 * wi + hf
                        if WH >= 1:
                            sync.wait_ge(sems.h["s_wq"], wq4 + WH)
                        sems.inc(sync.dma_start(
                            out=o1buf[:, 0:2048].rearrange("p (k d) -> p k d", d=1024),
                            in_=w_hh.rearrange("(k p) d -> p k d", p=128)[
                                :, hf * 2:(hf + 1) * 2]),
                            "wnA" if WH % 2 == 0 else "wnB", 16)
                for tc in range(NTC):
                    if tc >= 1:
                        sync.wait_ge(sems.h["s_xn"], xn0 + tc)
                    sems.inc(sync.dma_start(
                        out=x1in[tc % 2].rearrange("p (k t) -> p k t", t=512),
                        in_=x1T_h_t[:, :, tc * 512:(tc + 1) * 512]), "x1inA", 16)

            @blk.scalar
            def _(scalar):
                scalar.wait_ge(sems.h["sx1s"], sems.at("sx1s"))
                for wi, wn in enumerate(("q", "k", "v")):
                    for hf in range(4):
                        WH = 4 * wi + hf
                        scalar.wait_ge(sems.h["wnA" if WH % 2 == 0 else "wnB"],
                                       (wnA4 if WH % 2 == 0 else wnB4)
                                       + 16 * (WH // 2 + 1))
                        sems.inc(scalar.copy(
                            out=w4[wn][0][:, hf * 2048:(hf + 1) * 2048],
                            in_=o1buf[:, 0:2048]), "s_sq", 1)
                for tc in range(NTC):
                    scalar.wait_ge(sems.h["x1inA"], 16 * (tc + 1))
                    for k in range(KT):
                        SQ = 8 * tc + k
                        if SQ >= 1:
                            scalar.wait_ge(sems.h["s_ln"], ln0 + SQ)
                        sems.inc(scalar.activation(
                            out=sq1, in_=x1in[tc % 2][:, k * 512:(k + 1) * 512],
                            func=AF.Square), "s_sq", 1)
                    scalar.wait_ge(sems.h["s_var"], var0 + tc + 1)
                    sems.inc(scalar.activation(
                        out=srow, in_=vrow, func=AF.Sqrt, bias=cst[0:1, 0:1]),
                        "s_std", 1)
                    scalar.wait_ge(sems.h["s_xn"], xn0 + tc + 1)
                    if tc >= 2:
                        scalar.wait_ge(sems.h["s_pj"], pj0s + 24 * (tc - 1))
                    sems.inc(scalar.copy(out=x4h[tc % 2], in_=xn1o[tc % 2]),
                             "s_scl", 1)
                    for j in range(16):
                        FM = 16 * tc + j
                        scalar.wait_ge(sems.h["s_fm"], fm0 + 3 * FM + 1)
                        sems.inc(scalar.activation(
                            out=fm_e, in_=fm_e, func=AF.Exp), "s_fm", 2)

            @blk.tensor
            def _(tensor):
                for tc in range(NTC):
                    tensor.wait_ge(sems.h["x1inA"], 16 * (tc + 1))
                    if tc >= 1:
                        tensor.wait_ge(sems.h["s_var"], var0 + tc)
                    for k in range(KT):
                        tensor.matmul(ps_s1[0:1, 0:512], ones_c,
                                      x1in[tc % 2][:, k * 512:(k + 1) * 512],
                                      start=(k == 0), stop=(k == 7))
                    for k in range(KT):
                        tensor.wait_ge(sems.h["s_sq"], sq0v + 12 + 8 * tc + k + 1)
                        ins = tensor.matmul(ps_s1[64:65, 0:512],
                                            ones_c, sq1,
                                            start=(k == 0), stop=(k == 7))
                        sems.inc(ins, "s_ln", 1)
                    tensor.wait_ge(sems.h["s_var"], var0 + tc + 1)
                    if tc >= 1:
                        tensor.wait_ge(sems.h["s_evA"], evA0 + 2 * tc)
                    tensor.matmul(ps_bc1[:, 0:256], ones_r, mrow[0:1, 0:256],
                                  start=True, stop=True)
                    sems.inc(tensor.matmul(ps_bc1[:, 256:512], ones_r,
                                           mrow[0:1, 256:512],
                                           start=True, stop=True), "s_lnb", 1)
                    tensor.wait_ge(sems.h["s_mm"], mm0 + tc + 1)
                    tensor.wait_ge(sems.h["s_evA"], evA0 + 2 * tc + 1)
                    tensor.matmul(ps_bc1[:, 0:256], ones_r, rrow[0:1, 0:256],
                                  start=True, stop=True)
                    sems.inc(tensor.matmul(ps_bc1[:, 256:512], ones_r,
                                           rrow[0:1, 256:512],
                                           start=True, stop=True), "s_lnb", 1)
                    if tc == 0:
                        tensor.wait_ge(sems.h["s_wq"], wq4 + 12)
                    tensor.wait_ge(sems.h["s_scl"], scl4 + tc + 1)
                    tensor.wait_ge(sems.h["s_qx"], qx4 + tc + 1)
                    xh, xl = x4h[tc % 2], x4l[tc % 2]
                    tiles = [("q", oc) for oc in range(8)] + \
                            [("k", oc) for oc in range(8)] + \
                            [("v", (su, oh)) for su in range(4) for oh in range(2)]
                    for ti, (kindt, idx) in enumerate(tiles):
                        G = 24 * tc + ti
                        if G >= 4:
                            tensor.wait_ge(sems.h["s_pjA"], pjA0 + G - 3)
                        pb = ps_p4[G % 4]
                        if kindt in ("q", "k"):
                            wh, wl = w4[kindt]
                            oc = idx
                            np_ = 0
                            for k in range(KT):
                                wsl = slice(k * 1024 + oc * 128,
                                            k * 1024 + (oc + 1) * 128)
                                xsl = slice(k * 512, (k + 1) * 512)
                                for a, b in ((wh, xh), (wh, xl), (wl, xh)):
                                    ins = tensor.matmul(
                                        pb, a[:, wsl], b[:, xsl],
                                        start=(np_ == 0), stop=(np_ == 23))
                                    np_ += 1
                        else:
                            su, oh = idx
                            np_ = 0
                            for k in range(KT):
                                xsl = slice(k * 512 + su * 128,
                                            k * 512 + (su + 1) * 128)
                                wsl = slice(k * 1024 + oh * 512,
                                            k * 1024 + (oh + 1) * 512)
                                for a, b in ((xh, w4["v"][0]), (xh, w4["v"][1]),
                                             (xl, w4["v"][0])):
                                    ins = tensor.matmul(
                                        pb, a[:, xsl], b[:, wsl],
                                        start=(np_ == 0), stop=(np_ == 23))
                                    np_ += 1
                        sems.inc(ins, "s_pj", 1)

            @blk.vector
            def _(vector):
                vector.wait_ge(sems.h["sx1s"], sems.at("sx1s"))
                for wi, wn in enumerate(("q", "k", "v")):
                    for hf in range(4):
                        WH = 4 * wi + hf
                        vector.wait_ge(sems.h["s_sq"], sq4s + WH + 1)
                        sems.inc(vector.tensor_sub(
                            w4[wn][1][:, hf * 2048:(hf + 1) * 2048],
                            o1buf[:, 0:2048],
                            w4[wn][0][:, hf * 2048:(hf + 1) * 2048]), "s_wq", 1)
                for tc in range(NTC):
                    vector.wait_ge(sems.h["s_ln"], ln0 + 8 * (tc + 1))
                    vector.tensor_scalar_mul(mrow, ps_s1[0:1, 0:512], 1.0 / D)
                    sems.inc(vector.tensor_scalar_mul(
                        e2row, ps_s1[64:65, 0:512], 1.0 / D), "s_vh", 1)
                    vector.wait_ge(sems.h["s_vh"], sems.at("s_vh"))
                    sems.inc(vector.tensor_mul(srow, mrow, mrow), "s_vh", 1)
                    vector.wait_ge(sems.h["s_vh"], sems.at("s_vh"))
                    sems.inc(vector.tensor_sub(vrow, e2row, srow), "s_var", 1)
                    vector.wait_ge(sems.h["s_std"], std0 + tc + 1)
                    sems.inc(vector.reciprocal(rrow, srow), "s_mm", 1)
                    vector.wait_ge(sems.h["s_lnb"], lnb0 + 2 * tc + 1)
                    sems.inc(vector.tensor_copy(mb_sb, ps_bc1), "s_evA", 1)
                    vector.wait_ge(sems.h["s_lnb"], lnb0 + 2 * tc + 2)
                    sems.inc(vector.tensor_copy(rb_sb, ps_bc1), "s_evA", 1)
                    sems.inc(vector.engine_nop(), "s_vh", 1)
                    vector.wait_ge(sems.h["s_vh"], sems.at("s_vh"))
                    if tc >= 1:
                        pv = tc - 1
                        vector.wait_ge(sems.h["sxn1A" if pv % 2 == 0 else "sxn1B"],
                                       16 * (pv // 2 + 1))
                        vector.wait_ge(sems.h["s_scl"], scl4 + tc)
                    for k in range(KT):
                        vector.tensor_sub(xn1o[tc % 2][:, k * 512:(k + 1) * 512],
                                          x1in[tc % 2][:, k * 512:(k + 1) * 512],
                                          mb_sb)
                        ins = vector.tensor_mul(
                            xn1o[tc % 2][:, k * 512:(k + 1) * 512],
                            xn1o[tc % 2][:, k * 512:(k + 1) * 512], rb_sb)
                    sems.inc(ins, "s_xn", 1)
                    if tc >= 2:
                        vector.wait_ge(sems.h["s_pj"], pj0 + 24 * (tc - 1))
                    vector.wait_ge(sems.h["s_scl"], scl4 + tc + 1)
                    sems.inc(vector.tensor_sub(x4l[tc % 2], xn1o[tc % 2],
                                               x4h[tc % 2]), "s_qx", 1)
                    for ti in range(24):
                        G = 24 * tc + ti
                        vector.wait_ge(sems.h["s_pj"], pj0 + G + 1)
                        pb = ps_p4[G % 4]
                        if ti == 0 and tc >= 1:
                            vector.wait_ge(sems.h["sv1s"], sv10 + 16 * tc)
                        if ti == 16:
                            vector.wait_ge(sems.h["sq1s"], sq10 + 16 * (tc + 1))
                        if ti < 16:
                            dst = (q1o if ti < 8 else k1o)[:, (ti % 8) * 512:(ti % 8 + 1) * 512]
                            FM = 16 * tc + ti
                            sems.inc(vector.tensor_scalar_min(fm_e, pb, 0.0),
                                     "s_fm", 1)
                            sems.inc(vector.tensor_relu(dst, pb), "s_pjA", 1)
                            vector.wait_ge(sems.h["s_fm"], fm0 + 3 * FM + 3)
                            sems.inc(vector.tensor_add(dst, dst, fm_e), "s_pjB", 1)
                        else:
                            su, oh = (ti - 16) // 2, (ti - 16) % 2
                            dst = v1o[:, su * 1024 + oh * 512: su * 1024 + (oh + 1) * 512]
                            sems.inc(vector.tensor_copy(dst, pb), "s_pjA", 1)
                            sems.inc(vector.engine_nop(), "s_pjB", 1)

            @blk.gpsimd
            def _(g):
                for tc in range(NTC):
                    g.wait_ge(sems.h["s_xn"], xn0 + tc + 1)
                    sems.inc(g.dma_start(
                        out=xn1T_h_t[:, :, tc * 512:(tc + 1) * 512],
                        in_=xn1o[tc % 2].rearrange("p (k t) -> p k t", t=512)),
                        "sxn1A" if tc % 2 == 0 else "sxn1B", 16)
                    g.wait_ge(sems.h["s_pjB"], pjB0 + 24 * tc + 8)
                    sems.inc(g.dma_start(
                        out=q1T_h_t[:, :, tc * 512:(tc + 1) * 512],
                        in_=q1o.rearrange("p (k t) -> p k t", t=512)), "sq1s", 16)
                    g.wait_ge(sems.h["s_pjB"], pjB0 + 24 * tc + 16)
                    sems.inc(g.dma_start(
                        out=k1T_h_t[:, :, tc * 512:(tc + 1) * 512],
                        in_=k1o.rearrange("p (k t) -> p k t", t=512)), "sk1s", 16)
                    g.wait_ge(sems.h["s_pjB"], pjB0 + 24 * (tc + 1))
                    sems.inc(g.dma_start(
                        out=v1_h_t[tc],
                        in_=v1o.rearrange("p (a d) -> p a d", d=1024)), "sv1s", 16)

    with contextlib.ExitStack() as ctx:
        sb = lambda nm, shp, dt=F32: ctx.enter_context(nc.sbuf_tensor(nm, shp, dt)).ap()
        pst = lambda nm, shp: ctx.enter_context(nc.psum_tensor(nm, shp, F32)).ap()
        l1w_sb = sb("l1w_sb", [128, 8 * 1024])
        sobd_sb = sb("sobd_sb", [128, 8 * 256])
        sibd_sb = sb("sibd_sb", [128, 8 * 256])
        cvb_sb = sb("cvb_sb", [128, 4])
        xnin = [sb(f"xnin{i}", [128, 8 * 512]) for i in range(2)]
        fullo = sb("fullo", [128, 8 * 512])
        xco = sb("xco", [128, 4 * 512])
        sqx = [sb(f"sqx{i}", [128, 512]) for i in range(2)]
        bn_acc = sb("bn_acc", [128, 8])
        bn_t = sb("bn_t", [128, 8])
        ps_f = [pst(f"ps_f{i}", [128, 512]) for i in range(2)]
        ps_x = [pst(f"ps_x{i}", [128, 512]) for i in range(2)]

        fu0 = sems.at("s_pj")
        fuev0 = sems.at("s_pjA")
        xc20 = sems.at("s_sc")
        xcev0 = sems.at("s_exp")
        sfu0 = sems.at("sq1s")
        sxc0 = sems.at("sk1s")
        bna0 = sems.at("s_hm")
        sq0v = sems.at("s_sq")
        xiA0 = sems.at("xiA")
        xiB0 = sems.at("xiB")
        ld20 = sems.at("s_ld2") + 64
        with nc.Block() as blk:
            @blk.sync
            def _(sync):
                for nm in ("sxn1A", "sxn1B"):
                    sync.wait_ge(sems.h[nm], sems.at(nm))
                sems.inc(sync.dma_start(
                    out=l1w_sb.rearrange("p (k d) -> p k d", d=1024),
                    in_=l1_w.rearrange("(k p) d -> p k d", p=128)), "s_ld2", 16)
                sems.inc(sync.dma_start(
                    out=sobd_sb.rearrange("p (k d) -> p k d", d=256),
                    in_=so_bd.rearrange("(k p) d -> p k d", p=128)), "s_ld2", 16)
                sems.inc(sync.dma_start(
                    out=sibd_sb.rearrange("p (k d) -> p k d", d=256),
                    in_=si_bd.rearrange("(k p) d -> p k d", p=128)), "s_ld2", 16)
                sems.inc(sync.dma_start(out=cvb_sb, in_=cvb_c), "s_ld2", 16)
                for tc in range(NTC):
                    if tc >= 2:
                        sync.wait_ge(sems.h["s_pj"], fu0 + 8 * (tc - 1))
                        sync.wait_ge(sems.h["s_sc"], xc20 + 4 * (tc - 1))
                    sems.inc(sync.dma_start(
                        out=xnin[tc % 2].rearrange("p (k t) -> p k t", t=512),
                        in_=xn1T_h_t[:, :, tc * 512:(tc + 1) * 512]),
                        "xiA" if tc % 2 == 0 else "xiB", 16)

            @blk.tensor
            def _(tensor):
                tensor.wait_ge(sems.h["s_ld2"], ld20)
                for tc in range(NTC):
                    tensor.wait_ge(sems.h["xiA" if tc % 2 == 0 else "xiB"],
                                   (xiA0 if tc % 2 == 0 else xiB0) + 16 * (tc // 2 + 1))
                    for oc in range(8):
                        G = 8 * tc + oc
                        if G >= 2:
                            tensor.wait_ge(sems.h["s_pjA"], fuev0 + G - 1)
                        for k in range(KT):
                            ins = tensor.matmul(
                                ps_f[G % 2],
                                l1w_sb[:, k * 1024 + oc * 128: k * 1024 + (oc + 1) * 128],
                                xnin[tc % 2][:, k * 512:(k + 1) * 512],
                                start=(k == 0), stop=(k == 7))
                        sems.inc(ins, "s_pj", 1)
                    for bj in range(4):
                        XG = 4 * tc + bj
                        br, gt = bj // 2, bj % 2
                        bd = sobd_sb if br == 0 else sibd_sb
                        if XG >= 2:
                            tensor.wait_ge(sems.h["s_exp"], xcev0 + XG - 1)
                        for k in range(KT):
                            ins = tensor.matmul(
                                ps_x[XG % 2],
                                bd[:, k * 256 + gt * 128: k * 256 + (gt + 1) * 128],
                                xnin[tc % 2][:, k * 512:(k + 1) * 512],
                                start=(k == 0), stop=(k == 7))
                        sems.inc(ins, "s_sc", 1)

            @blk.scalar
            def _(scalar):
                for nm in ("sxn1A", "sxn1B", "sq1s", "sk1s", "sv1s"):
                    scalar.wait_ge(sems.h[nm], sems.at(nm))
                sqr0s = sems.at("s_sqr")
                for tc in range(NTC):
                    for bj in range(4):
                        XG = 4 * tc + bj
                        scalar.wait_ge(sems.h["s_exp"], xcev0 + XG + 1)
                        if XG >= 2:
                            scalar.wait_ge(sems.h["s_sqr"], sqr0s + XG - 1)
                        sems.inc(scalar.activation(
                            out=sqx[XG % 2], in_=xco[:, bj * 512:(bj + 1) * 512],
                            func=AF.Square), "s_sq", 1)

            @blk.vector
            def _(vector):
                for nm in ("sxn1A", "sxn1B", "sq1s", "sk1s", "sv1s"):
                    vector.wait_ge(sems.h[nm], sems.at(nm))
                sqr0v = sems.at("s_sqr")
                vector.memset(bn_acc, 0.0)
                for tc in range(NTC):
                    if tc >= 1:
                        vector.wait_ge(sems.h["sq1s"], sfu0 + 16 * tc)
                    for oc in range(8):
                        G = 8 * tc + oc
                        vector.wait_ge(sems.h["s_pj"], fu0 + G + 1)
                        sems.inc(vector.tensor_copy(
                            fullo[:, oc * 512:(oc + 1) * 512], ps_f[G % 2]),
                            "s_pjA", 1)
                    if tc >= 1:
                        vector.wait_ge(sems.h["sk1s"], sxc0 + 16 * tc)
                    for bj in range(4):
                        XG = 4 * tc + bj
                        vector.wait_ge(sems.h["s_sc"], xc20 + XG + 1)
                        sems.inc(vector.tensor_scalar_add(
                            xco[:, bj * 512:(bj + 1) * 512], ps_x[XG % 2],
                            cvb_sb[:, bj:bj + 1]), "s_exp", 1)
                        vector.reduce_sum(bn_t[:, bj:bj + 1],
                                          xco[:, bj * 512:(bj + 1) * 512],
                                          axis=AX.X)
                        vector.wait_ge(sems.h["s_sq"], sq0v + XG + 1)
                        sems.inc(vector.reduce_sum(
                            bn_t[:, 4 + bj:5 + bj], sqx[XG % 2], axis=AX.X),
                            "s_sqr", 1)
                    sems.inc(vector.engine_nop(), "s_vh", 1)
                    vector.wait_ge(sems.h["s_vh"], sems.at("s_vh"))
                    sems.inc(vector.tensor_add(bn_acc, bn_acc, bn_t), "s_hm", 1)

            @blk.gpsimd
            def _(g):
                for tc in range(NTC):
                    g.wait_ge(sems.h["s_pjA"], fuev0 + 8 * (tc + 1))
                    sems.inc(g.dma_start(
                        out=fullT_h_t[:, :, tc * 512:(tc + 1) * 512],
                        in_=fullo.rearrange("p (k t) -> p k t", t=512)), "sq1s", 16)
                    g.wait_ge(sems.h["s_hm"], bna0 + tc + 1)
                    sems.inc(g.dma_start(
                        out=xcT_h_t[:, :, tc * 512:(tc + 1) * 512],
                        in_=xco.rearrange("p (b t) -> p b t", t=512)), "sk1s", 16)
                g.wait_ge(sems.h["s_hm"], bna0 + NTC)
                sems.inc(g.dma_start(out=ccbn_in.ap(), in_=bn_acc), "s_st", 16)
                g.wait_ge(sems.h["s_st"], sems.at("s_st"))
                sems.inc(g.collective_compute(
                    "AllReduce", ALU.add,
                    replica_groups=[list(range(NCORES))],
                    ins=[ccbn_in.ap()], outs=[ccbn_out.ap()]), "s_cc", 1)

    if dev_phase is not None and dev_phase == 5:
        with nc.Block() as blk:
            @blk.gpsimd
            def _(g):
                g.wait_ge(sems.h["s_st"], sems.at("s_st"))
                g.wait_ge(sems.h["s_st2"], sems.at("s_st2"))
                sems.inc(g.dma_start(out=gmax_dbg, in_=cch_out.ap()), "s_st", 16)
                g.wait_ge(sems.h["s_st"], sems.at("s_st"))
        return nc

    CL = 256
    NCL = ncl_limit if ncl_limit is not None else T // CL
    CPS = SEQ // CL
    with contextlib.ExitStack() as ctx:
        sb = lambda nm, shp, dt=F32: ctx.enter_context(nc.sbuf_tensor(nm, shp, dt)).ap()
        pst = lambda nm, shp: ctx.enter_context(nc.psum_tensor(nm, shp, F32)).ap()
        w1o = sb("w1o", [128, 8 * 1024])
        qc = [sb(f"qc{i}", [128, 8 * CL]) for i in range(2)]
        kc = [sb(f"kc{i}", [128, 8 * CL]) for i in range(2)]
        vc = [sb(f"vc{i}", [128, 2 * 1024]) for i in range(2)]
        AT_sb = sb("AT_sb", [128, 2 * 512])
        knat_sb = sb("knat_sb", [128, 2 * 128])
        Sz_sb = sb("Sz_sb", [128, 8 * 65])
        zt_sb = sb("zt_sb", [128, 8])
        den_sb = sb("den_sb3", [1, 2 * 256])
        rden3 = sb("rden3", [1, 2 * 256])
        rdenb3 = sb("rdenb3", [128, 2 * 256])
        yT_sb = sb("yT_sb", [128, 8 * 512])
        x1r = sb("x1r", [128, 8 * 512])
        p2o = sb("p2o", [128, 8 * 512])
        l1m = [sb(f"l1m{i}", [128, 256]) for i in range(2)]
        identL = sb("identL", [128, 64])
        psA = [pst(f"psA{i}", [128, 512]) for i in range(2)]
        psN = [pst(f"psN{i}", [128, 256]) for i in range(2)]
        psM = [pst(f"psM{i}", [128, 512]) for i in range(2)]
        psD = [pst(f"psD{i}", [128, 512]) for i in range(2)]

        ld0 = sems.at("s_ld")

        with nc.Block() as blk:
            @blk.sync
            def _(sync):
                for nm in ("sq1s", "sk1s", "sv1s", "sxn1A", "sxn1B"):
                    sync.wait_ge(sems.h[nm], sems.at(nm))
                sems.inc(sync.dma_start(
                    out=w1o.rearrange("p (k d) -> p k d", d=1024),
                    in_=l1_wo.rearrange("(k p) d -> p k d", p=128)), "s_ld", 16)
                sems.inc(sync.dma_start(out=l1m[0], in_=l1m0_c.ap()), "s_ld", 16)
                sems.inc(sync.dma_start(out=l1m[1], in_=l1m1_c.ap()), "s_ld", 16)
                sems.inc(sync.dma_start(out=identL, in_=identL_c.ap()), "s_ld", 16)
                for ci in range(NCL):
                    if ci >= 2:
                        sync.wait_ge(sems.h["s_A"], 16 * (ci - 1))
                        sync.wait_ge(sems.h["s_num"], 16 * (ci - 1))
                        sync.wait_ge(sems.h["s_den"], 16 * (ci - 1))
                        sync.wait_ge(sems.h["s_Smm"], 16 * (ci - 1))
                        sync.wait_ge(sems.h["s_ktr"], 16 * (ci - 1))
                        sync.wait_ge(sems.h["s_Sup"], 16 * (ci - 1))
                    qkv = "qcA" if ci % 2 == 0 else "qcB"
                    sems.inc(sync.dma_start(
                        out=qc[ci % 2].rearrange("p (k t) -> p k t", t=CL),
                        in_=q1T_h_t[:, :, ci * CL:(ci + 1) * CL]), qkv, 16)
                    sems.inc(sync.dma_start(
                        out=kc[ci % 2].rearrange("p (k t) -> p k t", t=CL),
                        in_=k1T_h_t[:, :, ci * CL:(ci + 1) * CL]), qkv, 16)
                    sems.inc(sync.dma_start(
                        out=vc[ci % 2].rearrange("p (a d) -> p a d", d=1024),
                        in_=v1_h.rearrange("(c a p) d -> c p a d", a=2, p=128)[ci]),
                        qkv, 16)
                    if ci % 2 == 0:
                        grp = ci // 2
                        if grp >= 1:
                            sync.wait_ge(sems.h["s_p2"], 8 * grp)
                        sems.inc(sync.dma_start(
                            out=x1r.rearrange("p (k t) -> p k t", t=512),
                            in_=x1T_h_t[:, :, grp * 512:(grp + 1) * 512]), "x1rA", 16)

            @blk.tensor
            def _(tensor):
                tensor.wait_ge(sems.h["s_ld"], ld0 + 64)
                for ci in range(NCL):
                    tensor.wait_ge(sems.h["qcA" if ci % 2 == 0 else "qcB"],
                                   48 * (ci // 2 + 1))
                    hsl = lambda buf, h: buf[(h % 2) * 64:(h % 2) * 64 + 64,
                                             (h // 2) * CL:(h // 2 + 1) * CL]

                    def emit_A(h):
                        E = 16 * ci + h
                        if ci % 2 == 0 and h < 2 and ci >= 2:
                            tensor.wait_ge(sems.h["s_p2"], 8 * (ci // 2))
                        if E >= 2:
                            tensor.wait_ge(sems.h["s_bce"], E - 1)
                            tensor.wait_ge(sems.h["s_msk"], 2 * (E - 2) + 2)
                        for sub in range(2):
                            ins = tensor.matmul(
                                psA[h % 2][:, sub * 256:(sub + 1) * 256],
                                kc[ci % 2][(h % 2) * 64:(h % 2) * 64 + 64,
                                           (h // 2) * CL + sub * 128:
                                           (h // 2) * CL + (sub + 1) * 128],
                                hsl(qc[ci % 2], h), start=True, stop=True)
                        sems.inc(ins, "s_A", 1)

                    def emit_ktr(h):
                        E = 16 * ci + h
                        if E >= 2:
                            tensor.wait_ge(sems.h["s_knv"], E - 1)
                            tensor.wait_ge(sems.h["s_Sup"], E - 1)
                        for sub in range(2):
                            ins = tensor.transpose(
                                psM[h % 2][:, sub * 64:(sub + 1) * 64],
                                kc[ci % 2][(h % 2) * 64:(h % 2) * 64 + 64,
                                           (h // 2) * CL + sub * 128:
                                           (h // 2) * CL + (sub + 1) * 128],
                                identL[(h % 2) * 64:(h % 2) * 64 + 64, 0:64])
                        sems.inc(ins, "s_ktr", 1)

                    def emit_bc(h):
                        E = 16 * ci + h
                        tensor.wait_ge(sems.h["s_rdn"], E + 1)
                        sems.inc(tensor.matmul(
                            psA[h % 2][:, 256:512], ones_r,
                            rden3[0:1, (h % 2) * 256:(h % 2 + 1) * 256],
                            start=True, stop=True), "s_bcm", 1)

                    def emit_work(h):
                        E = 16 * ci + h
                        rows = slice((h % 2) * 64, (h % 2) * 64 + 64)
                        tensor.wait_ge(sems.h["s_msk"], 2 * E + 2)
                        if E >= 2:
                            tensor.wait_ge(sems.h["s_dnv"], E - 1)
                        for sub in range(2):
                            tensor.matmul(
                                psD[h % 2][0:1, 0:256], ones_c,
                                AT_sb[:, (h % 2) * 512 + sub * 256:
                                      (h % 2) * 512 + (sub + 1) * 256],
                                start=(sub == 0), stop=False)
                        sems.inc(tensor.matmul(
                            psD[h % 2][0:1, 0:256],
                            Sz_sb[rows, (h // 2) * 65 + 64:(h // 2) * 65 + 65],
                            hsl(qc[ci % 2], h), start=False, stop=True), "s_den", 1)
                        if ci % CPS == 0:
                            tensor.wait_ge(sems.h["s_seq"], ci // CPS + 1)
                        else:
                            tensor.wait_ge(sems.h["s_Sup"], 16 * (ci - 1) + h + 1)
                        if E >= 2:
                            tensor.wait_ge(sems.h["s_yml"], E - 1)
                        for sub in range(2):
                            tensor.matmul(
                                psN[h % 2][rows, 0:256],
                                vc[ci % 2][:, sub * 1024 + h * 64:
                                           sub * 1024 + (h + 1) * 64],
                                AT_sb[:, (h % 2) * 512 + sub * 256:
                                      (h % 2) * 512 + (sub + 1) * 256],
                                start=(sub == 0), stop=False)
                        sems.inc(tensor.matmul(
                            psN[h % 2][rows, 0:256],
                            Sz_sb[rows, (h // 2) * 65:(h // 2) * 65 + 64],
                            hsl(qc[ci % 2], h), start=False, stop=True), "s_num", 1)
                        tensor.wait_ge(sems.h["s_knv"], E + 1)
                        for sub in range(2):
                            ins = tensor.matmul(
                                psM[h % 2][rows, 128:192],
                                knat_sb[:, (h % 2) * 128 + sub * 64:
                                        (h % 2) * 128 + (sub + 1) * 64],
                                vc[ci % 2][:, sub * 1024 + h * 64:
                                           sub * 1024 + (h + 1) * 64],
                                start=(sub == 0), stop=(sub == 1))
                        sems.inc(ins, "s_Smm", 1)

                    emit_A(0)
                    emit_ktr(0)
                    for h in range(16):
                        if h >= 1:
                            emit_bc(h - 1)
                        if h + 1 < 16:
                            emit_A(h + 1)
                            emit_ktr(h + 1)
                        emit_work(h)
                    emit_bc(15)
                    if ci % 2 == 1:
                        grp = ci // 2
                        tensor.wait_ge(sems.h["s_yml"], 16 * (ci + 1))
                        tensor.wait_ge(sems.h["s_bce"], 16 * (ci + 1))
                        for oc in range(8):
                            W = 8 * grp + oc
                            if W >= 2:
                                tensor.wait_ge(sems.h["s_p2"], W - 1)
                            for k in range(KT):
                                ins = tensor.matmul(
                                    psA[oc % 2],
                                    w1o[:, k * 1024 + oc * 128: k * 1024 + (oc + 1) * 128],
                                    yT_sb[:, k * 512:(k + 1) * 512],
                                    start=(k == 0), stop=(k == 7))
                            sems.inc(ins, "s_wo2", 1)

            @blk.scalar
            def _(scalar):
                for nm in ("sq1s", "sk1s"):
                    scalar.wait_ge(sems.h[nm], sems.at(nm))
                for ci in range(NCL):
                    for h in range(16):
                        E = 16 * ci + h
                        if h >= 1:
                            scalar.wait_ge(sems.h["s_bcm"], E)
                            sems.inc(scalar.copy(
                                out=rdenb3[:, ((h - 1) % 2) * 256:((h - 1) % 2 + 1) * 256],
                                in_=psA[(h - 1) % 2][:, 256:512]), "s_bce", 1)
                        scalar.wait_ge(sems.h["s_ktr"], E + 1)
                        sems.inc(scalar.copy(
                            out=knat_sb[:, (h % 2) * 128:(h % 2 + 1) * 128],
                            in_=psM[h % 2][:, 0:128]), "s_knv", 1)
                        scalar.wait_ge(sems.h["s_den"], E + 1)
                        sems.inc(scalar.copy(
                            out=den_sb[0:1, (h % 2) * 256:(h % 2 + 1) * 256],
                            in_=psD[h % 2][0:1, 0:256]), "s_dnv", 1)
                    scalar.wait_ge(sems.h["s_bcm"], 16 * (ci + 1))
                    sems.inc(scalar.copy(
                        out=rdenb3[:, (15 % 2) * 256:(15 % 2 + 1) * 256],
                        in_=psA[15 % 2][:, 256:512]), "s_bce", 1)

            @blk.vector
            def _(vector):
                for nm in ("sq1s", "sk1s"):
                    vector.wait_ge(sems.h[nm], sems.at(nm))
                for ci in range(NCL):
                    if ci % CPS == 0:
                        if ci > 0:
                            vector.wait_ge(sems.h["s_num"], 16 * ci)
                            vector.wait_ge(sems.h["s_den"], 16 * ci)
                        sems.inc(vector.memset(Sz_sb, 0.0), "s_seq", 1)

                    def fm(h):
                        E = 16 * ci + h
                        vector.wait_ge(sems.h["s_A"], E + 1)
                        if E >= 2:
                            vector.wait_ge(sems.h["s_num"], E - 1)
                            vector.wait_ge(sems.h["s_den"], E - 1)
                        for sub in range(2):
                            vector.tensor_mul(
                                AT_sb[:, (h % 2) * 512 + sub * 256:
                                      (h % 2) * 512 + (sub + 1) * 256],
                                psA[h % 2][:, sub * 256:(sub + 1) * 256],
                                l1m[sub])
                        sems.inc(vector.engine_nop(), "s_msk", 2)
                        rows = slice((h % 2) * 64, (h % 2) * 64 + 64)
                        vector.reduce_sum(
                            zt_sb[rows, h // 2:h // 2 + 1],
                            kc[ci % 2][rows, (h // 2) * CL:(h // 2 + 1) * CL],
                            axis=AX.X)

                    def fd(h):
                        E = 16 * ci + h
                        vector.wait_ge(sems.h["s_dnv"], E + 1)
                        sems.inc(vector.tensor_scalar_add(
                            rden3[0:1, (h % 2) * 256:(h % 2 + 1) * 256],
                            den_sb[0:1, (h % 2) * 256:(h % 2 + 1) * 256],
                            ATT_EPS), "s_vh", 1)
                        vector.wait_ge(sems.h["s_vh"], sems.at("s_vh"))
                        sems.inc(vector.reciprocal(
                            rden3[0:1, (h % 2) * 256:(h % 2 + 1) * 256],
                            rden3[0:1, (h % 2) * 256:(h % 2 + 1) * 256]), "s_rdn", 1)

                    def back(h):
                        E = 16 * ci + h
                        rows = slice((h % 2) * 64, (h % 2) * 64 + 64)
                        vector.wait_ge(sems.h["s_num"], E + 1)
                        vector.wait_ge(sems.h["s_bce"], E + 1)
                        if ci % 2 == 0 and ci >= 2 and h == 0:
                            vector.wait_ge(sems.h["s_wo2"], 8 * (ci // 2))
                        sems.inc(vector.tensor_mul(
                            yT_sb[rows, (h // 2) * 512 + (ci % 2) * 256:
                                  (h // 2) * 512 + (ci % 2) * 256 + 256],
                            psN[h % 2][rows, 0:256],
                            rdenb3[rows, (h % 2) * 256:(h % 2 + 1) * 256]),
                            "s_yml", 1)
                        vector.wait_ge(sems.h["s_Smm"], E + 1)
                        vector.tensor_add(
                            Sz_sb[rows, (h // 2) * 65:(h // 2) * 65 + 64],
                            Sz_sb[rows, (h // 2) * 65:(h // 2) * 65 + 64],
                            psM[h % 2][rows, 128:192])
                        sems.inc(vector.tensor_add(
                            Sz_sb[rows, (h // 2) * 65 + 64:(h // 2) * 65 + 65],
                            Sz_sb[rows, (h // 2) * 65 + 64:(h // 2) * 65 + 65],
                            zt_sb[rows, h // 2:h // 2 + 1]), "s_Sup", 1)

                    fm(0)
                    for h in range(16):
                        if h + 1 < 16:
                            fm(h + 1)
                        fd(h)
                        back(h)
                    if ci % 2 == 1:
                        grp = ci // 2
                        vector.wait_ge(sems.h["x1rA"], 16 * (grp + 1))
                        if grp >= 1:
                            vector.wait_ge(sems.h["sp2"], 16 * grp)
                        for oc in range(8):
                            vector.wait_ge(sems.h["s_wo2"], 8 * grp + oc + 1)
                            sems.inc(vector.tensor_add(
                                p2o[:, oc * 512:(oc + 1) * 512],
                                psA[oc % 2],
                                x1r[:, oc * 512:(oc + 1) * 512]), "s_p2", 1)

            @blk.gpsimd
            def _(g):
                for grp in range(NCL // 2):
                    g.wait_ge(sems.h["s_p2"], 8 * (grp + 1))
                    sems.inc(g.dma_start(
                        out=p2T_h_t[:, :, grp * 512:(grp + 1) * 512],
                        in_=p2o.rearrange("p (k t) -> p k t", t=512)), "sp2", 16)

    with contextlib.ExitStack() as ctx:
        sb = lambda nm, shp, dt=F32: ctx.enter_context(nc.sbuf_tensor(nm, shp, dt)).ap()
        pst = lambda nm, shp: ctx.enter_context(nc.psum_tensor(nm, shp, F32)).ap()
        l1w2 = sb("l1w2", [128, 8 * 1024])
        solw = sb("solw", [128, 2 * 1024])
        silw = sb("silw", [128, 2 * 1024])
        solb = sb("solb", [128, 1024])
        silb = sb("silb", [128, 1024])
        gam_sb = sb("gam_sb", [128, 4])
        bet_sb = sb("bet_sb", [128, 4])
        b8_sb = sb("b8_sb", [128, 8])
        bn_g = sb("bn_g", [128, 8])
        bn_m = sb("bn_m", [128, 4])
        bn_e2 = sb("bn_e2", [128, 4])
        bn_v = sb("bn_v", [128, 4])
        bn_s = sb("bn_s", [128, 4])
        bn_r = sb("bn_r", [128, 4])
        sc_a = sb("sc_a", [128, 4])
        sh_b = sb("sh_b", [128, 4])
        xcin = sb("xcin", [128, 4 * 512])
        xnc = sb("xnc", [128, 8 * 512])
        fullc = sb("fullc", [128, 8 * 512])
        p2c = sb("p2c", [128, 8 * 512])
        xs_sb = sb("xs_sb", [128, 8 * 512])
        outB = sb("outB", [128, 8 * 512])
        out_nat = sb("out_nat", [128, 4 * 1024])
        logit_sb = sb("logit_sb", [128, 1024])
        p_sb3 = sb("p_sb3", [128, 1024])
        lgA = sb("lgA", [128, 1024])
        lgB = sb("lgB", [128, 1024])
        M_sb = sb("M_sb", [128, 1024])
        m8 = sb("m8", [128, 4 * 8])
        rsum = sb("rsum", [128, 1])
        ps_lg = [pst(f"ps_lg{i}", [128, 512]) for i in range(2)]
        ps_mt = [pst(f"ps_mt{i}", [128, 512]) for i in range(2)]
        ps_in = [pst(f"ps_in{i}", [128, 512]) for i in range(2)]
        ps_ot = pst("ps_ot", [128, 512])

        glA0 = sems.at("s_guA")
        gu0 = sems.at("s_gu")
        guev0 = sems.at("s_guev")
        qx0c = sems.at("s_qx")
        fm3 = sems.at("s_fm")
        tpB0 = sems.at("s_tpB")
        pf0 = sems.at("s_pf")
        dn0 = sems.at("s_dn")
        oev0 = sems.at("s_oev")
        wo0 = sems.at("s_wo")
        evA0c = sems.at("s_evA")
        scl3 = sems.at("s_scl")
        gm0 = sems.at("gmld")
        xqA0 = sems.at("xqA")
        xqB0 = sems.at("xqB")
        wnA0 = sems.at("wnA")
        wnB0 = sems.at("wnB")
        ld20c = sems.at("s_ld2")
        sqr3 = sems.at("s_sqr")

        with nc.Block() as blk:
            @blk.sync
            def _(sync):
                for nm in ("sq1s", "sk1s", "sp2"):
                    sync.wait_ge(sems.h[nm], sems.at(nm))
                sems.inc(sync.dma_start(
                    out=l1w2.rearrange("p (k d) -> p k d", d=1024),
                    in_=l1_w.rearrange("(k p) d -> p k d", p=128)), "s_ld2", 16)
                sems.inc(sync.dma_start(
                    out=solw.rearrange("p (k d) -> p k d", d=1024),
                    in_=so_linT.rearrange("(k p) d -> p k d", p=128)), "s_ld2", 16)
                sems.inc(sync.dma_start(
                    out=silw.rearrange("p (k d) -> p k d", d=1024),
                    in_=si_linT.rearrange("(k p) d -> p k d", p=128)), "s_ld2", 16)
                sems.inc(sync.dma_start(out=solb, in_=so_lb), "s_ld2", 16)
                sems.inc(sync.dma_start(out=silb, in_=si_lb), "s_ld2", 16)
                sems.inc(sync.dma_start(out=gam_sb, in_=gam_c), "s_ld2", 16)
                sems.inc(sync.dma_start(out=bet_sb, in_=bet_c), "s_ld2", 16)
                sems.inc(sync.dma_start(out=b8_sb, in_=l1_b8), "s_ld2", 16)
                for tc in range(NTC):
                    if tc >= 1:
                        sync.wait_ge(sems.h["s_guA"], glA0 + 4 * tc)
                        sync.wait_ge(sems.h["s_gu"], gu0 + 16 * tc)
                    sems.inc(sync.dma_start(
                        out=xcin.rearrange("p (b t) -> p b t", t=512),
                        in_=xcT_h_t[:, :, tc * 512:(tc + 1) * 512]), "xqA", 16)
                    if tc >= 1:
                        sync.wait_ge(sems.h["s_pf"], pf0 + 16 * tc)
                    sems.inc(sync.dma_start(
                        out=xnc.rearrange("p (k t) -> p k t", t=512),
                        in_=xn1T_h_t[:, :, tc * 512:(tc + 1) * 512]), "xqB", 16)
                    sems.inc(sync.dma_start(
                        out=fullc.rearrange("p (k t) -> p k t", t=512),
                        in_=fullT_h_t[:, :, tc * 512:(tc + 1) * 512]), "wnA", 16)
                    if tc >= 1:
                        sync.wait_ge(sems.h["s_oev"], oev0 + 8 * tc)
                    sems.inc(sync.dma_start(
                        out=p2c.rearrange("p (k t) -> p k t", t=512),
                        in_=p2T_h_t[:, :, tc * 512:(tc + 1) * 512]), "wnB", 16)

            @blk.gpsimd
            def _(g):
                g.wait_ge(sems.h["s_cc"], 3)
                sems.inc(g.dma_start(out=bn_g, in_=ccbn_out.ap()), "gmld", 16)
                for tc in range(NTC):
                    g.wait_ge(sems.h["s_evA"], evA0c + 8 * (tc + 1))
                    sems.inc(g.dma_start(
                        out=out_h_t[tc],
                        in_=out_nat.rearrange("p (a d) -> p a d", d=1024)),
                        "sout", 16)
                g.wait_ge(sems.h["sout"], 16 * NTC)

            @blk.scalar
            def _(scalar):
                scalar.wait_ge(sems.h["sp2"], sems.at("sp2"))
                scalar.wait_ge(sems.h["s_scl"], scl3 + 1)
                sems.inc(scalar.activation(out=bn_s, in_=bn_v, func=AF.Sqrt,
                                           bias=cst[:, 3:4]), "s_scl", 1)
                for tc in range(NTC):
                    scalar.wait_ge(sems.h["xqA"], xqA0 + 16 * (tc + 1))
                    if tc == 0:
                        scalar.wait_ge(sems.h["s_scl"], scl3 + 3)
                    for bj in range(4):
                        sems.inc(scalar.activation(
                            out=xcin[:, bj * 512:(bj + 1) * 512],
                            in_=xcin[:, bj * 512:(bj + 1) * 512],
                            func=AF.Gelu, scale=sc_a[:, bj:bj + 1],
                            bias=sh_b[:, bj:bj + 1]), "s_guA", 1)
                    for TB in range(8):
                        TBg = 8 * tc + TB
                        scalar.wait_ge(sems.h["s_guev"], guev0 + 2 * TBg + 2)
                        if TBg >= 1:
                            scalar.wait_ge(sems.h["s_qx"], qx0c + TBg)
                        sems.inc(scalar.activation(
                            out=p_sb3, in_=logit_sb, func=AF.Exp), "s_fm", 1)
                    for oc in range(8):
                        scalar.wait_ge(sems.h["s_wo"], wo0 + 8 * tc + oc + 1)
                        if oc == 0 and tc >= 1:
                            scalar.wait_ge(sems.h["sout"], 16 * tc)
                        sems.inc(scalar.copy(
                            out=out_nat.rearrange("p (a d) -> p a d", d=1024)[
                                :, :, oc * 128:(oc + 1) * 128],
                            in_=ps_ot.rearrange("p (a b) -> p a b", b=128)),
                            "s_evA", 1)

            @blk.tensor
            def _(tensor):
                tensor.wait_ge(sems.h["s_ld2"], ld20c + 128)
                for tc in range(NTC):
                    for sub in range(4):
                        for br in range(2):
                            TB = sub * 2 + br
                            TBg = 8 * tc + TB
                            tensor.wait_ge(sems.h["s_guA"],
                                           glA0 + 4 * tc + 2 * br + 2)
                            if TBg >= 1:
                                tensor.wait_ge(sems.h["s_guev"],
                                               guev0 + 2 * TBg)
                            lw = solw if br == 0 else silw
                            for half in range(2):
                                for gt in range(2):
                                    ins = tensor.matmul(
                                        ps_lg[half],
                                        xcin[:, (br * 2 + gt) * 512 + sub * 128:
                                             (br * 2 + gt) * 512 + (sub + 1) * 128],
                                        lw[:, gt * 1024 + half * 512:
                                           gt * 1024 + (half + 1) * 512],
                                        start=(gt == 0), stop=(gt == 1))
                                sems.inc(ins, "s_gu", 1)
                            tensor.wait_ge(sems.h["s_qx"], qx0c + TBg + 1)
                            if TBg >= 1:
                                tensor.wait_ge(sems.h["s_pf"], pf0 + 2 * TBg)
                            for gr in range(2):
                                for k4 in range(4):
                                    k = gr * 4 + k4
                                    ins = tensor.transpose(
                                        ps_mt[gr][:, k4 * 128:(k4 + 1) * 128],
                                        M_sb[:, k * 128:(k + 1) * 128], ident)
                                sems.inc(ins, "s_tpB", 1)
                    for oc in range(8):
                        if oc == 0:
                            tensor.wait_ge(sems.h["s_pf"], pf0 + 16 * (tc + 1))
                        DN = 8 * tc + oc
                        if DN >= 2:
                            tensor.wait_ge(sems.h["s_oev"], oev0 + DN - 1)
                        for k in range(KT):
                            ins = tensor.matmul(
                                ps_in[oc % 2],
                                l1w2[:, k * 1024 + oc * 128: k * 1024 + (oc + 1) * 128],
                                xs_sb[:, k * 512:(k + 1) * 512],
                                start=(k == 0), stop=(k == 7))
                        sems.inc(ins, "s_dn", 1)
                    for oc in range(8):
                        tensor.wait_ge(sems.h["s_oev"], oev0 + 8 * tc + oc + 1)
                        tensor.wait_ge(sems.h["s_evA"], evA0c + 8 * tc + oc)
                        for j in range(4):
                            ins = tensor.transpose(
                                ps_ot[:, j * 128:(j + 1) * 128],
                                outB[:, oc * 512 + j * 128: oc * 512 + (j + 1) * 128],
                                ident)
                        sems.inc(ins, "s_wo", 1)

            @blk.vector
            def _(vector):
                vector.wait_ge(sems.h["sp2"], sems.at("sp2"))
                vector.wait_ge(sems.h["gmld"], gm0 + 16)
                vector.wait_ge(sems.h["s_ld2"], ld20c + 128)
                vector.tensor_scalar_mul(bn_m, bn_g[:, 0:4], 1.0 / 65536.0)
                sems.inc(vector.tensor_scalar_mul(bn_e2, bn_g[:, 4:8],
                                                  1.0 / 65536.0), "s_vh", 1)
                vector.wait_ge(sems.h["s_vh"], sems.at("s_vh"))
                sems.inc(vector.tensor_mul(bn_v, bn_m, bn_m), "s_vh", 1)
                vector.wait_ge(sems.h["s_vh"], sems.at("s_vh"))
                sems.inc(vector.tensor_sub(bn_v, bn_e2, bn_v), "s_scl", 1)
                vector.wait_ge(sems.h["s_scl"], scl3 + 2)
                sems.inc(vector.reciprocal(bn_r, bn_s), "s_vh", 1)
                vector.wait_ge(sems.h["s_vh"], sems.at("s_vh"))
                sems.inc(vector.tensor_mul(sc_a, gam_sb, bn_r), "s_vh", 1)
                vector.wait_ge(sems.h["s_vh"], sems.at("s_vh"))
                sems.inc(vector.tensor_mul(bn_r, bn_m, sc_a), "s_vh", 1)
                vector.wait_ge(sems.h["s_vh"], sems.at("s_vh"))
                sems.inc(vector.tensor_sub(sh_b, bet_sb, bn_r), "s_scl", 1)
                NEG = -1.0e30
                for tc in range(NTC):
                    for sub in range(4):
                        for br in range(2):
                            TB = sub * 2 + br
                            TBg = 8 * tc + TB
                            lb = solb if br == 0 else silb
                            if TBg >= 1:
                                vector.wait_ge(sems.h["s_fm"], fm3 + TBg)
                            for half in range(2):
                                vector.wait_ge(sems.h["s_gu"],
                                               gu0 + 2 * TBg + half + 1)
                                sems.inc(vector.tensor_add(
                                    logit_sb[:, half * 512:(half + 1) * 512],
                                    ps_lg[half],
                                    lb[:, half * 512:(half + 1) * 512]),
                                    "s_guev", 1)
                            vector.wait_ge(sems.h["s_fm"], fm3 + TBg + 1)
                            sems.inc(vector.reduce_sum(rsum, p_sb3, axis=AX.X),
                                     "s_vh", 1)
                            vector.wait_ge(sems.h["s_vh"], sems.at("s_vh"))
                            vector.reciprocal(rsum, rsum)
                            sems.inc(vector.max(out=m8[:, 0:8], in_=logit_sb),
                                     "s_vh", 1)
                            vector.wait_ge(sems.h["s_vh"], sems.at("s_vh"))
                            vector.match_replace(out=lgA, in_to_replace=m8[:, 0:8],
                                                 in_values=logit_sb, imm_value=NEG)
                            sems.inc(vector.max(out=m8[:, 8:16], in_=lgA), "s_vh", 1)
                            vector.wait_ge(sems.h["s_vh"], sems.at("s_vh"))
                            vector.match_replace(out=lgB, in_to_replace=m8[:, 8:16],
                                                 in_values=lgA, imm_value=NEG)
                            sems.inc(vector.max(out=m8[:, 16:24], in_=lgB), "s_vh", 1)
                            vector.wait_ge(sems.h["s_vh"], sems.at("s_vh"))
                            vector.match_replace(out=lgA, in_to_replace=m8[:, 16:24],
                                                 in_values=lgB, imm_value=NEG)
                            sems.inc(vector.max(out=m8[:, 24:32], in_=lgA), "s_vh", 1)
                            vector.wait_ge(sems.h["s_vh"], sems.at("s_vh"))
                            vector.scalar_tensor_tensor(
                                out=M_sb, in0=logit_sb, scalar=m8[:, 31:32],
                                in1=p_sb3, op0=ALU.is_ge, op1=ALU.mult)
                            sems.inc(vector.tensor_scalar_mul(M_sb, M_sb, rsum),
                                     "s_qx", 1)
                            if br == 0:
                                vector.wait_ge(sems.h["wnA"],
                                               wnA0 + 16 * (tc + 1))
                                if TBg == 0 or (tc >= 1 and TB == 0):
                                    vector.wait_ge(sems.h["s_wo"], wo0 + 8 * tc)
                                other, dstb = fullc, outB
                            else:
                                vector.wait_ge(sems.h["xqB"],
                                               xqB0 + 16 * (tc + 1))
                                if tc >= 1 and TB <= 1:
                                    vector.wait_ge(sems.h["s_dn"], dn0 + 8 * tc)
                                other, dstb = xnc, xs_sb
                            for gr in range(2):
                                vector.wait_ge(sems.h["s_tpB"],
                                               tpB0 + 2 * TBg + gr + 1)
                                dv = dstb.rearrange("p (k t) -> p k t", t=512)[
                                    :, gr * 4:(gr + 1) * 4, sub * 128:(sub + 1) * 128]
                                ov = other.rearrange("p (k t) -> p k t", t=512)[
                                    :, gr * 4:(gr + 1) * 4, sub * 128:(sub + 1) * 128]
                                sems.inc(vector.tensor_mul(
                                    dv,
                                    ps_mt[gr].rearrange("p (a b) -> p a b", b=128),
                                    ov), "s_pf", 1)
                    vector.wait_ge(sems.h["wnB"], wnB0 + 16 * (tc + 1))
                    for oc in range(8):
                        DN = 8 * tc + oc
                        vector.wait_ge(sems.h["s_dn"], dn0 + DN + 1)
                        vector.scalar_tensor_tensor(
                            out=outB[:, oc * 512:(oc + 1) * 512],
                            in0=ps_in[oc % 2], scalar=b8_sb[:, oc:oc + 1],
                            in1=outB[:, oc * 512:(oc + 1) * 512],
                            op0=ALU.add, op1=ALU.add)
                        sems.inc(vector.tensor_add(
                            outB[:, oc * 512:(oc + 1) * 512],
                            outB[:, oc * 512:(oc + 1) * 512],
                            p2c[:, oc * 512:(oc + 1) * 512]), "s_oev", 1)

    if dev_phase is not None:
        return nc
    return nc


def _layouts(inputs):
    g = {}
    g["l0_wqT"] = np.ascontiguousarray(inputs["L0_attn_Wq"].T)
    g["l0_wkT"] = np.ascontiguousarray(inputs["L0_attn_Wk"].T)
    g["l0_wvT"] = np.ascontiguousarray(inputs["L0_attn_Wv"].T)
    g["l0_woT"] = np.ascontiguousarray(inputs["L0_attn_Wo"].T)
    g["l0_cx"] = inputs["L0_ffn_clip_x"].reshape(1, 1)
    g["l0_cd"] = inputs["L0_ffn_clip_down"].reshape(1, 1)
    g["l0_wg"] = inputs["L0_ffn_w_gate"]
    g["l0_wu"] = inputs["L0_ffn_w_up"]
    g["l0_wd"] = inputs["L0_ffn_w_down"]
    g["l1_wq"] = inputs["L1_attn_Wq"]
    g["l1_wk"] = inputs["L1_attn_Wk"]
    g["l1_wv"] = inputs["L1_attn_Wv"]
    g["l1_wo"] = inputs["L1_attn_Wo"]
    g["l1_w"] = inputs["L1_ffn_weight"]
    for nm, cw in (("so_bd", inputs["L1_ffn_so_conv_w"]),
                   ("si_bd", inputs["L1_ffn_si_conv_w"])):
        bd = np.zeros((D, 256), np.float32)
        bd[np.arange(D), np.arange(D) // 4] = np.asarray(cw, np.float32).reshape(-1)
        g[nm] = bd
    def pack4(a, b):
        out = np.empty((128, 4), np.float32)
        out[:, 0:2] = np.asarray(a, np.float32).reshape(2, 128).T
        out[:, 2:4] = np.asarray(b, np.float32).reshape(2, 128).T
        return out
    g["cvb_c"] = pack4(inputs["L1_ffn_so_conv_b"], inputs["L1_ffn_si_conv_b"])
    g["gam_c"] = pack4(inputs["L1_ffn_so_gamma"], inputs["L1_ffn_si_gamma"])
    g["bet_c"] = pack4(inputs["L1_ffn_so_beta"], inputs["L1_ffn_si_beta"])
    g["so_linT"] = np.ascontiguousarray(np.asarray(inputs["L1_ffn_so_lin_w"], np.float32).T)
    g["si_linT"] = np.ascontiguousarray(np.asarray(inputs["L1_ffn_si_lin_w"], np.float32).T)
    g["so_lb"] = np.ascontiguousarray(np.broadcast_to(
        np.asarray(inputs["L1_ffn_so_lin_b"], np.float32), (128, D)))
    g["si_lb"] = np.ascontiguousarray(np.broadcast_to(
        np.asarray(inputs["L1_ffn_si_lin_b"], np.float32), (128, D)))
    g["l1_b8"] = np.ascontiguousarray(
        np.asarray(inputs["L1_ffn_bias"], np.float32).reshape(8, 128).T)
    return g


def kernel(**inputs):
    nc = build_nc()
    shared = _layouts(inputs)
    x = np.ascontiguousarray(inputs["L0_attn_x"].astype(np.float32, copy=False))
    in_maps = []
    for c in range(NCORES):
        m = dict(shared)
        m["x_in"] = np.ascontiguousarray(
            x[c * SEQ_PER_CORE:(c + 1) * SEQ_PER_CORE].reshape(T, D))
        in_maps.append(m)
    res = run_bass_kernel_spmd(nc, in_maps, core_ids=list(range(NCORES)))
    out = np.concatenate([r["out"].reshape(SEQ_PER_CORE, SEQ, D)
                          for r in res.results], axis=0)
    return out


# version 07
# speedup vs baseline: 1.0640x; 1.0175x over previous
import contextlib

import numpy as np

import concourse.bass as bass
import concourse.mybir as mybir
from concourse.bass_utils import run_bass_kernel_spmd

F32 = mybir.dt.float32
BF16 = mybir.dt.bfloat16
AF = mybir.ActivationFunctionType
ALU = mybir.AluOpType
AX = mybir.AxisListType

NCORES = 8
BATCH, SEQ, D = 32, 2048, 1024
SEQ_PER_CORE = BATCH // NCORES
T = SEQ_PER_CORE * SEQ
TC = 512
NTC = T // TC
KT = D // 128
I_FFN = 2816
IT = I_FFN // 128
NHEAD, HD, WIN = 16, 64, 128
NCH = T // 128
RC = 12582912.0
LN_EPS = 1e-5
ATT_EPS = 1e-6
BN_EPS = 1e-5
QMAX = 127.0


class Sems:

    def __init__(self, nc, names):
        self.h = {}
        self.c = {}
        self._cm = []
        for n in names:
            cm = nc.semaphore(n)
            sem = cm.__enter__()
            self._cm.append(cm)
            self.h[n] = sem
            self.c[n] = 0

    def inc(self, ins, name, n=1):
        self.c[name] += n
        return ins.then_inc(self.h[name], n)

    def at(self, name):
        return self.c[name]


def build_nc(dev_phase=None, nch_limit=None, p1b_stage=5, ncl_limit=None,
             race_detect=False):
    nc = bass.Bass("TRN2", target_bir_lowering=False,
                   detect_race_conditions=race_detect)

    def kind_for(phase):
        if dev_phase is not None and phase <= dev_phase:
            return "ExternalOutput"
        return "Internal"

    x_in = nc.dram_tensor("x_in", [T, D], F32, kind="ExternalInput").ap()
    l0_wqT = nc.dram_tensor("l0_wqT", [D, D], F32, kind="ExternalInput").ap()
    l0_wkT = nc.dram_tensor("l0_wkT", [D, D], F32, kind="ExternalInput").ap()
    l0_wvT = nc.dram_tensor("l0_wvT", [D, D], F32, kind="ExternalInput").ap()
    l0_woT = nc.dram_tensor("l0_woT", [D, D], F32, kind="ExternalInput").ap()
    l0_cx = nc.dram_tensor("l0_cx", [1, 1], F32, kind="ExternalInput").ap()
    l0_cd = nc.dram_tensor("l0_cd", [1, 1], F32, kind="ExternalInput").ap()
    l0_wg = nc.dram_tensor("l0_wg", [I_FFN, D], F32, kind="ExternalInput").ap()
    l0_wu = nc.dram_tensor("l0_wu", [I_FFN, D], F32, kind="ExternalInput").ap()
    l0_wd = nc.dram_tensor("l0_wd", [D, I_FFN], F32, kind="ExternalInput").ap()
    l1_wq = nc.dram_tensor("l1_wq", [D, D], F32, kind="ExternalInput").ap()
    l1_wk = nc.dram_tensor("l1_wk", [D, D], F32, kind="ExternalInput").ap()
    l1_wv = nc.dram_tensor("l1_wv", [D, D], F32, kind="ExternalInput").ap()
    l1_wo = nc.dram_tensor("l1_wo", [D, D], F32, kind="ExternalInput").ap()
    l1_w = nc.dram_tensor("l1_w", [D, D], F32, kind="ExternalInput").ap()
    so_bd = nc.dram_tensor("so_bd", [D, 256], F32, kind="ExternalInput").ap()
    si_bd = nc.dram_tensor("si_bd", [D, 256], F32, kind="ExternalInput").ap()
    cvb_c = nc.dram_tensor("cvb_c", [128, 4], F32, kind="ExternalInput").ap()
    gam_c = nc.dram_tensor("gam_c", [128, 4], F32, kind="ExternalInput").ap()
    bet_c = nc.dram_tensor("bet_c", [128, 4], F32, kind="ExternalInput").ap()
    so_linT = nc.dram_tensor("so_linT", [256, D], F32, kind="ExternalInput").ap()
    si_linT = nc.dram_tensor("si_linT", [256, D], F32, kind="ExternalInput").ap()
    so_lb = nc.dram_tensor("so_lb", [128, D], F32, kind="ExternalInput").ap()
    si_lb = nc.dram_tensor("si_lb", [128, D], F32, kind="ExternalInput").ap()
    l1_b8 = nc.dram_tensor("l1_b8", [128, 8], F32, kind="ExternalInput").ap()
    out_h = nc.dram_tensor("out", [T, D], F32, kind="ExternalOutput").ap()

    xnT_h = nc.dram_tensor("xnT_h", [D, T], F32, kind=kind_for(0)).ap()
    xT_h = nc.dram_tensor("xT_h", [D, T], F32, kind=kind_for(0)).ap()
    qT_h = nc.dram_tensor("qT_h", [D, T], F32, kind=kind_for(1)).ap()
    kT_h = nc.dram_tensor("kT_h", [D, T], F32, kind=kind_for(1)).ap()
    v_h = nc.dram_tensor("v_h", [T, D], F32, kind=kind_for(1)).ap()
    p1T_h = nc.dram_tensor("p1T_h", [D, T], F32, kind=kind_for(2)).ap()
    h_h = nc.dram_tensor("h_h", [I_FFN, T], F32, kind=kind_for(3)).ap()
    x1T_h = nc.dram_tensor("x1T_h", [D, T], F32, kind=kind_for(4)).ap()
    xn1T_h = nc.dram_tensor("xn1T_h", [D, T], F32, kind=kind_for(5)).ap()
    q1T_h = nc.dram_tensor("q1T_h", [D, T], F32, kind=kind_for(5)).ap()
    k1T_h = nc.dram_tensor("k1T_h", [D, T], F32, kind=kind_for(5)).ap()
    v1_h = nc.dram_tensor("v1_h", [T, D], F32, kind=kind_for(5)).ap()
    fullT_h = nc.dram_tensor("fullT_h", [D, T], F32, kind=kind_for(6)).ap()
    xcT_h = nc.dram_tensor("xcT_h", [512, T], F32, kind=kind_for(6)).ap()
    p2T_h = nc.dram_tensor("p2T_h", [D, T], F32, kind=kind_for(6)).ap()

    ccx_in = nc.dram_tensor("ccx_in", [1, 1], F32, kind="Internal")
    ccx_out = nc.dram_tensor("ccx_out", [1, 1], F32, kind="Internal")
    cch_in = nc.dram_tensor("cch_in", [1, 1], F32, kind="Internal")
    cch_out = nc.dram_tensor("cch_out", [1, 1], F32, kind="Internal")
    ccbn_in = nc.dram_tensor("ccbn_in", [128, 8], F32, kind="Internal")
    ccbn_out = nc.dram_tensor("ccbn_out", [128, 8], F32, kind="Internal")

    gmax_dbg = None
    if dev_phase is not None:
        gmax_dbg = nc.dram_tensor("gmax_dbg", [1, 1], F32, kind="ExternalOutput").ap()

    import ml_dtypes
    ident_c = nc.inline_tensor(np.eye(128, dtype=np.float32), "ident_c")
    identb_c = nc.inline_tensor(np.eye(128, dtype=ml_dtypes.bfloat16), "identb_c")
    _i = np.arange(128)[:, None]
    _j = np.arange(256)[None, :]
    _band = ((_j >= _i + 1) & (_j <= _i + WIN)).astype(np.float32)
    maskG_c = nc.inline_tensor(np.ascontiguousarray(_band), "maskG_c")
    mask0_c = nc.inline_tensor(
        np.ascontiguousarray(_band * (_j >= WIN)), "mask0_c")
    _jj = np.arange(128)[:, None]
    _ii = np.arange(256)[None, :]
    l1m0_c = nc.inline_tensor(
        np.ascontiguousarray((_jj <= _ii).astype(np.float32)), "l1m0_c")
    l1m1_c = nc.inline_tensor(
        np.ascontiguousarray(((_jj + 128) <= _ii).astype(np.float32)), "l1m1_c")
    _il = np.zeros((128, 64), np.float32)
    _il[np.arange(128), np.arange(128) % 64] = 1.0
    identL_c = nc.inline_tensor(_il, "identL_c")

    sems = Sems(nc, [
        "s_ld", "s_ld2", "s_st", "s_st2", "s_sq", "s_sqr", "s_var", "s_std",
        "s_xn", "s_tpA", "s_tpB", "s_evA", "s_evB", "s_mm", "s_cc", "s_misc",
        "s_vh", "s_pj", "s_pjA", "s_pjB", "s_sc", "s_exp", "s_pf", "s_pt",
        "s_ptev", "s_av", "s_oev", "s_wo", "s_p1ev",
        "s_ld3", "s_wq", "s_wT", "s_scl", "s_qx", "s_gu", "s_guev", "s_hm",
        "s_dn", "s_x1ev", "s_guA", "s_bc",
        "s_ln", "s_lnb", "s_fm",
        "s_A", "s_msk", "s_den", "s_dnv", "s_rdn", "s_bcm", "s_num", "s_yml",
        "s_ktr", "s_knv", "s_bce", "s_Smm", "s_Sup", "s_seq", "s_p2", "s_wo2",
        "s_cst",
        "xldA", "xldB", "sxnA", "sxnB", "sxtA", "sxtB",
        "xiA", "xiB", "sq1s", "sk1s", "sv1s",
        "qkvA", "qkvB", "qkvC", "xrA", "xrB", "sp1",
        "wnA", "wnB", "xqA", "xqB", "shs",
        "wdA", "wdB", "hin1", "p1in1", "sx1s",
        "x1inA", "sxn1A", "sxn1B", "qcA", "qcB", "x1rA", "sp2",
        "gmld", "chld", "sout",
    ])

    ident = nc.alloc_sbuf_tensor("ident", [128, 128], F32).ap()
    identb = nc.alloc_sbuf_tensor("identb", [128, 128], BF16).ap()
    cst = nc.alloc_sbuf_tensor("cst", [128, 8], F32).ap()
    ones_r = nc.alloc_sbuf_tensor("ones_r", [1, 128], F32).ap()
    ones_c = nc.alloc_sbuf_tensor("ones_c", [128, 1], F32).ap()

    x_in_t = x_in.rearrange("(c a p) d -> c p a d", a=4, p=128)
    xnT_h_t = xnT_h.rearrange("(k p) t -> p k t", p=128)
    xT_h_t = xT_h.rearrange("(k p) t -> p k t", p=128)
    qT_h_t = qT_h.rearrange("(k p) t -> p k t", p=128)
    kT_h_t = kT_h.rearrange("(k p) t -> p k t", p=128)
    v_h_t = v_h.rearrange("(c a p) d -> c p a d", a=4, p=128)
    p1T_h_t = p1T_h.rearrange("(k p) t -> p k t", p=128)

    with contextlib.ExitStack() as ctx:
        sb = lambda nm, shp, dt=F32: ctx.enter_context(nc.sbuf_tensor(nm, shp, dt)).ap()
        ps = lambda nm, shp: ctx.enter_context(nc.psum_tensor(nm, shp, F32)).ap()
        xb = [sb(f"xb{i}", [128, 4 * 1024]) for i in range(2)]
        sqb = [sb(f"sqb{i}", [128, 1024]) for i in range(2)]
        xnb = [sb(f"xnb{i}", [128, 4 * 1024]) for i in range(2)]
        xnT_sb = [sb(f"xnT_sb{i}", [128, 8 * 512]) for i in range(2)]
        xT_sb = [sb(f"xT_sb{i}", [128, 8 * 512]) for i in range(2)]
        st_m = [sb(f"st_m{i}", [128, 4]) for i in range(2)]
        st_r = [sb(f"st_r{i}", [128, 4]) for i in range(2)]
        st_s1 = sb("st_s1", [128, 4])
        st_s2 = sb("st_s2", [128, 4])
        st_var = [sb(f"st_var{i}", [128, 4]) for i in range(2)]
        st_std = [sb(f"st_std{i}", [128, 4]) for i in range(2)]
        amax_run = sb("amax_run", [128, 1])
        amax_t = sb("amax_t", [128, 1])
        amax_row = sb("amax_row", [1, 128])
        gmax_sb = sb("gmax_sb", [1, 1])
        ps_tp = [ps(f"ps_tp{i}", [128, 512]) for i in range(4)]
        ps_sm = ps("ps_sm", [128, 128])

        with nc.Block() as blk:
            @blk.sync
            def _(sync):
                sync.dma_start(out=ident, in_=ident_c.ap()).then_inc(sems.h["s_misc"], 16)
                for tc in range(NTC):
                    if tc >= 2:
                        sync.wait_ge(sems.h["s_xn"], tc - 1)
                        sync.wait_ge(sems.h["s_sq"], 4 * (tc - 1))
                        sync.wait_ge(sems.h["s_tpB"], 8 * (tc - 1))
                    sems.inc(sync.dma_start(
                        out=xb[tc % 2].rearrange("p (a d) -> p a d", a=4),
                        in_=x_in_t[tc]), "xldA" if tc % 2 == 0 else "xldB", 16)

            @blk.scalar
            def _(scalar):
                for tc in range(NTC):
                    scalar.wait_ge(sems.h["xldA" if tc % 2 == 0 else "xldB"],
                                   16 * (tc // 2 + 1))
                    for sub in range(4):
                        q = 4 * tc + sub
                        if q >= 2:
                            scalar.wait_ge(sems.h["s_sqr"], q - 1)
                        sems.inc(scalar.activation(
                            out=sqb[q % 2],
                            in_=xb[tc % 2][:, sub * 1024:(sub + 1) * 1024],
                            func=AF.Square), "s_sq", 1)
                    scalar.wait_ge(sems.h["s_var"], tc + 1)
                    if tc == 0:
                        scalar.wait_ge(sems.h["s_misc"], 16)
                        scalar.wait_ge(sems.h["s_cst"], 1)
                    sems.inc(scalar.activation(
                        out=st_std[tc % 2], in_=st_var[tc % 2],
                        func=AF.Sqrt, bias=cst[:, 0:1]), "s_std", 1)
                    if tc >= 2:
                        scalar.wait_ge(sems.h["sxnA" if tc % 2 == 0 else "sxnB"],
                                       16 * (tc // 2))
                    for j in range(8):
                        g = 8 * tc + j
                        scalar.wait_ge(sems.h["s_tpA"], g + 1)
                        sems.inc(scalar.copy(
                            out=xnT_sb[tc % 2][:, j * 512:(j + 1) * 512],
                            in_=ps_tp[j % 2]), "s_evA", 1)

            @blk.vector
            def _(vector):
                vector.memset(amax_run, 0.0)
                vector.memset(cst[:, 0:1], LN_EPS)
                vector.memset(cst[:, 1:2], RC)
                vector.memset(ones_c, 1.0)
                vector.memset(cst[:, 3:4], BN_EPS)
                sems.inc(vector.memset(cst[:, 2:3], 0.125), "s_cst", 1)
                for tc in range(NTC):
                    vector.wait_ge(sems.h["xldA" if tc % 2 == 0 else "xldB"],
                                   16 * (tc // 2 + 1))
                    for sub in range(4):
                        q = 4 * tc + sub
                        vector.reduce_sum(
                            st_s1[:, sub:sub + 1],
                            xb[tc % 2][:, sub * 1024:(sub + 1) * 1024], axis=AX.X)
                        vector.wait_ge(sems.h["s_sq"], q + 1)
                        sems.inc(vector.reduce_sum(
                            st_s2[:, sub:sub + 1], sqb[q % 2], axis=AX.X), "s_sqr", 1)
                    vector.tensor_scalar_mul(st_m[tc % 2], st_s1, 1.0 / D)
                    vector.tensor_scalar_mul(st_s2, st_s2, 1.0 / D)
                    sems.inc(vector.tensor_mul(st_s1, st_m[tc % 2], st_m[tc % 2]),
                             "s_vh", 1)
                    vector.wait_ge(sems.h["s_vh"], sems.at("s_vh"))
                    sems.inc(vector.tensor_sub(st_var[tc % 2], st_s2, st_s1),
                             "s_var", 1)
                    vector.wait_ge(sems.h["s_std"], tc + 1)
                    sems.inc(vector.reciprocal(st_r[tc % 2], st_std[tc % 2]), "s_vh", 1)
                    vector.wait_ge(sems.h["s_vh"], sems.at("s_vh"))
                    for sub in range(4):
                        vector.tensor_scalar(
                            xnb[tc % 2][:, sub * 1024:(sub + 1) * 1024],
                            xb[tc % 2][:, sub * 1024:(sub + 1) * 1024],
                            st_m[tc % 2][:, sub:sub + 1],
                            st_r[tc % 2][:, sub:sub + 1],
                            ALU.subtract, ALU.mult)
                    sems.inc(vector.reduce_max(
                        amax_t, xnb[tc % 2], axis=AX.X,
                        apply_absolute_value=True), "s_vh", 1)
                    vector.wait_ge(sems.h["s_vh"], sems.at("s_vh"))
                    sems.inc(vector.tensor_max(amax_run, amax_run, amax_t), "s_xn", 1)
                    if tc >= 2:
                        vector.wait_ge(sems.h["sxtA" if tc % 2 == 0 else "sxtB"],
                                       16 * (tc // 2))
                    for j in range(8):
                        g = 8 * tc + j
                        vector.wait_ge(sems.h["s_tpB"], g + 1)
                        sems.inc(vector.tensor_copy(
                            xT_sb[tc % 2][:, j * 512:(j + 1) * 512],
                            ps_tp[2 + j % 2]), "s_evB", 1)

            @blk.tensor
            def _(tensor):
                tensor.wait_ge(sems.h["s_misc"], 16)
                for tc in range(NTC):
                    tensor.wait_ge(sems.h["s_xn"], tc + 1)
                    for j in range(8):
                        g = 8 * tc + j
                        if g >= 2:
                            tensor.wait_ge(sems.h["s_evA"], g - 1)
                        for sub in range(4):
                            ins = tensor.transpose(
                                ps_tp[j % 2][:, sub * 128:(sub + 1) * 128],
                                xnb[tc % 2][:, sub * 1024 + j * 128: sub * 1024 + (j + 1) * 128],
                                ident)
                            if sub == 3:
                                sems.inc(ins, "s_tpA", 1)
                    for j in range(8):
                        g = 8 * tc + j
                        if g >= 2:
                            tensor.wait_ge(sems.h["s_evB"], g - 1)
                        for sub in range(4):
                            ins = tensor.transpose(
                                ps_tp[2 + j % 2][:, sub * 128:(sub + 1) * 128],
                                xb[tc % 2][:, sub * 1024 + j * 128: sub * 1024 + (j + 1) * 128],
                                ident)
                            if sub == 3:
                                sems.inc(ins, "s_tpB", 1)
                tensor.wait_ge(sems.h["s_xn"], NTC)
                sems.inc(tensor.transpose(ps_sm[0:1, :], amax_run, ident), "s_mm", 1)

            @blk.gpsimd
            def _(g):
                for tc in range(NTC):
                    g.wait_ge(sems.h["s_evA"], 8 * (tc + 1))
                    sems.inc(g.dma_start(
                        out=xnT_h_t[:, :, tc * 512:(tc + 1) * 512],
                        in_=xnT_sb[tc % 2].rearrange("p (k t) -> p k t", t=512)),
                        "sxnA" if tc % 2 == 0 else "sxnB", 16)
                    g.wait_ge(sems.h["s_evB"], 8 * (tc + 1))
                    sems.inc(g.dma_start(
                        out=xT_h_t[:, :, tc * 512:(tc + 1) * 512],
                        in_=xT_sb[tc % 2].rearrange("p (k t) -> p k t", t=512)),
                        "sxtA" if tc % 2 == 0 else "sxtB", 16)

            @blk.vector
            def _(vector):
                vector.wait_ge(sems.h["s_mm"], 1)
                sems.inc(vector.tensor_copy(amax_row, ps_sm[0:1, :]), "s_vh", 1)
                vector.wait_ge(sems.h["s_vh"], sems.at("s_vh"))
                sems.inc(vector.reduce_max(gmax_sb, amax_row, axis=AX.X), "s_mm", 1)

            @blk.gpsimd
            def _(g):
                g.wait_ge(sems.h["s_mm"], 2)
                sems.inc(g.dma_start(out=ccx_in.ap(), in_=gmax_sb), "s_st", 16)
                g.wait_ge(sems.h["s_st"], sems.at("s_st"))
                sems.inc(g.collective_compute(
                    "AllReduce", ALU.max,
                    replica_groups=[list(range(NCORES))],
                    ins=[ccx_in.ap()], outs=[ccx_out.ap()]), "s_cc", 1)

    if dev_phase is not None and dev_phase == 0:
        with nc.Block() as blk:
            @blk.gpsimd
            def _(g):
                g.wait_ge(sems.h["s_cc"], 1)
                sems.inc(g.dma_start(out=gmax_dbg, in_=ccx_out.ap()), "s_st", 16)
                g.wait_ge(sems.h["s_st"], sems.at("s_st"))
        return nc

    with contextlib.ExitStack() as ctx:
        sb = lambda nm, shp, dt=F32: ctx.enter_context(nc.sbuf_tensor(nm, shp, dt)).ap()
        pst = lambda nm, shp: ctx.enter_context(nc.psum_tensor(nm, shp, F32)).ap()
        wstg = sb("wstg", [128, 2 * 1024])
        wsp = {}
        for wn in ("q", "k", "v"):
            wsp[wn] = (sb(f"w{wn}h", [128, 8 * 1024], BF16),
                       sb(f"w{wn}l", [128, 8 * 1024], BF16))
        xin_s = sb("xin_s", [128, 8 * 512])
        xhi = [sb(f"xhi{i}", [128, 8 * 512], BF16) for i in range(2)]
        xlo = [sb(f"xlo{i}", [128, 8 * 512], BF16) for i in range(2)]
        qTo_sb = sb("qTo_sb", [128, 8 * 512])
        kTo_sb = sb("kTo_sb", [128, 8 * 512])
        vo_sb = sb("vo_sb", [128, 4 * 1024])
        ps_pj = [pst(f"ps_pj{i}", [128, 512]) for i in range(4)]

        st20 = sems.at("s_st2")
        std0 = sems.at("s_std")
        var0 = sems.at("s_var")
        wspl0 = sems.at("s_wq")
        sq0p1 = sems.at("s_sq")
        with nc.Block() as blk:
            @blk.sync
            def _(sync):
                for nm in ("sxnA", "sxnB", "sxtA", "sxtB"):
                    sync.wait_ge(sems.h[nm], sems.at(nm))
                for wi, w_h in enumerate((l0_wqT, l0_wkT, l0_wvT)):
                    for hf in range(4):
                        WH = 4 * wi + hf
                        if WH >= 1:
                            sync.wait_ge(sems.h["s_wq"], wspl0 + WH)
                        sems.inc(sync.dma_start(
                            out=wstg.rearrange("p (k d) -> p k d", d=1024),
                            in_=w_h.rearrange("(k p) d -> p k d", p=128)[
                                :, hf * 2:(hf + 1) * 2]),
                            "wnA" if WH % 2 == 0 else "wnB", 16)
                for tc in range(NTC):
                    if tc >= 1:
                        sync.wait_ge(sems.h["s_std"], std0 + tc)
                        sync.wait_ge(sems.h["s_var"], var0 + tc)
                    sems.inc(sync.dma_start(
                        out=xin_s.rearrange("p (k t) -> p k t", t=512),
                        in_=xnT_h_t[:, :, tc * 512:(tc + 1) * 512]),
                        "xiA" if tc % 2 == 0 else "xiB", 16)

            @blk.scalar
            def _(scalar):
                for nm in ("sxnA", "sxnB", "sxtA", "sxtB"):
                    scalar.wait_ge(sems.h[nm], sems.at(nm))
                for wi, wn in enumerate(("q", "k", "v")):
                    for hf in range(4):
                        WH = 4 * wi + hf
                        scalar.wait_ge(sems.h["wnA" if WH % 2 == 0 else "wnB"],
                                       16 * (WH // 2 + 1))
                        sems.inc(scalar.copy(
                            out=wsp[wn][0][:, hf * 2048:(hf + 1) * 2048],
                            in_=wstg), "s_sq", 1)
                for tc in range(NTC):
                    scalar.wait_ge(sems.h["xiA" if tc % 2 == 0 else "xiB"],
                                   16 * (tc // 2 + 1))
                    if tc >= 2:
                        scalar.wait_ge(sems.h["s_pj"], 24 * (tc - 1))
                    sems.inc(scalar.copy(out=xhi[tc % 2], in_=xin_s), "s_std", 1)

            @blk.vector
            def _(vector):
                for nm in ("sxnA", "sxnB", "sxtA", "sxtB"):
                    vector.wait_ge(sems.h[nm], sems.at(nm))
                for wi, wn in enumerate(("q", "k", "v")):
                    for hf in range(4):
                        WH = 4 * wi + hf
                        vector.wait_ge(sems.h["s_sq"], sq0p1 + WH + 1)
                        sems.inc(vector.tensor_sub(
                            wsp[wn][1][:, hf * 2048:(hf + 1) * 2048],
                            wstg, wsp[wn][0][:, hf * 2048:(hf + 1) * 2048]),
                            "s_wq", 1)
                for tc in range(NTC):
                    vector.wait_ge(sems.h["s_std"], std0 + tc + 1)
                    sems.inc(vector.tensor_sub(xlo[tc % 2], xin_s, xhi[tc % 2]),
                             "s_var", 1)
                    if tc >= 1:
                        vector.wait_ge(sems.h["sq1s"], 16 * tc)
                        vector.wait_ge(sems.h["sk1s"], 16 * tc)
                        vector.wait_ge(sems.h["sv1s"], 16 * tc)
                    for ti in range(24):
                        G = 24 * tc + ti
                        vector.wait_ge(sems.h["s_pj"], G + 1)
                        pb = ps_pj[G % 4]
                        if ti < 8:
                            dst = qTo_sb[:, ti * 512:(ti + 1) * 512]
                        elif ti < 16:
                            dst = kTo_sb[:, (ti - 8) * 512:(ti - 7) * 512]
                        else:
                            su, oh = (ti - 16) // 2, (ti - 16) % 2
                            dst = vo_sb[:, su * 1024 + oh * 512: su * 1024 + (oh + 1) * 512]
                        sems.inc(vector.tensor_copy(dst, pb),
                                 "s_pjA" if ti % 2 == 0 else "s_pjB", 1)

            @blk.tensor
            def _(tensor):
                tensor.wait_ge(sems.h["s_wq"], wspl0 + 12)
                for tc in range(NTC):
                    tensor.wait_ge(sems.h["s_std"], std0 + tc + 1)
                    tensor.wait_ge(sems.h["s_var"], var0 + tc + 1)
                    xh, xl = xhi[tc % 2], xlo[tc % 2]
                    tiles = [("q", oc) for oc in range(8)] + \
                            [("k", oc) for oc in range(8)] + \
                            [("v", (su, oh)) for su in range(4) for oh in range(2)]
                    for ti, (kindt, idx) in enumerate(tiles):
                        G = 24 * tc + ti
                        if G >= 4:
                            pG = G - 4
                            sname = "s_pjA" if pG % 2 == 0 else "s_pjB"
                            tensor.wait_ge(sems.h[sname], pG // 2 + 1)
                        pb = ps_pj[G % 4]
                        if kindt in ("q", "k"):
                            wh, wl = wsp[kindt]
                            oc = idx
                            np_ = 0
                            for k in range(8):
                                wsl = slice(k * 1024 + oc * 128,
                                            k * 1024 + (oc + 1) * 128)
                                xsl = slice(k * 512, (k + 1) * 512)
                                for a, b in ((wh, xh), (wh, xl), (wl, xh)):
                                    ins = tensor.matmul(
                                        pb, a[:, wsl], b[:, xsl],
                                        start=(np_ == 0), stop=(np_ == 23))
                                    np_ += 1
                        else:
                            su, oh = idx
                            np_ = 0
                            for k in range(8):
                                xsl = slice(k * 512 + su * 128,
                                            k * 512 + (su + 1) * 128)
                                wsl = slice(k * 1024 + oh * 512,
                                            k * 1024 + (oh + 1) * 512)
                                for a, b in ((xh, wsp["v"][0]), (xh, wsp["v"][1]),
                                             (xl, wsp["v"][0])):
                                    ins = tensor.matmul(
                                        pb, a[:, xsl], b[:, wsl],
                                        start=(np_ == 0), stop=(np_ == 23))
                                    np_ += 1
                        sems.inc(ins, "s_pj", 1)

            @blk.gpsimd
            def _(g):
                for tc in range(NTC):
                    g.wait_ge(sems.h["s_pjA"], 12 * (tc + 1))
                    g.wait_ge(sems.h["s_pjB"], 12 * (tc + 1))
                    sems.inc(g.dma_start(
                        out=qT_h_t[:, :, tc * 512:(tc + 1) * 512],
                        in_=qTo_sb.rearrange("p (k t) -> p k t", t=512)), "sq1s", 16)
                    sems.inc(g.dma_start(
                        out=kT_h_t[:, :, tc * 512:(tc + 1) * 512],
                        in_=kTo_sb.rearrange("p (k t) -> p k t", t=512)), "sk1s", 16)
                    sems.inc(g.dma_start(
                        out=v_h_t[tc],
                        in_=vo_sb.rearrange("p (a d) -> p a d", d=1024)), "sv1s", 16)

    with contextlib.ExitStack() as ctx:
        sb = lambda nm, shp, dt=F32: ctx.enter_context(nc.sbuf_tensor(nm, shp, dt)).ap()
        pst = lambda nm, shp: ctx.enter_context(nc.psum_tensor(nm, shp, F32)).ap()
        wo_sb = sb("wo_sb", [128, 8 * 1024])
        qb = [sb(f"qb{i}", [128, 8 * 128]) for i in range(3)]
        kroll = [sb(f"kroll{i}", [128, 8 * 128]) for i in range(3)]
        vroll = [sb(f"vroll{i}", [128, 1024]) for i in range(3)]
        p_sb = sb("p_sb", [128, 16 * 256])
        den_sb = sb("den_sb", [128, 16])
        rden_sb = sb("rden_sb", [128, 16])
        pT_sb = sb("pT_sb", [128, 16 * 256])
        attnT_sb = sb("attnT_sb", [128, 8 * 512])
        xr_sb = [sb(f"xr_sb{i}", [128, 8 * 512]) for i in range(2)]
        p1_sb = sb("p1_sb", [128, 8 * 512])
        maskG = sb("maskG", [128, 256])
        mask0 = sb("mask0", [128, 256])
        ps_sc = [pst(f"ps_sc{i}", [128, 256]) for i in range(2)]
        ps_pt = [pst(f"ps_pt{i}", [128, 256]) for i in range(2)]
        ps_ov = [pst(f"ps_ov{i}", [128, 128]) for i in range(2)]
        ps_wo = [pst(f"ps_wo{i}", [128, 512]) for i in range(2)]

        ld0 = sems.at("s_ld")
        cst0 = sems.at("s_cst") + 1
        NCHL = nch_limit if nch_limit is not None else NCH

        with nc.Block() as blk:
            @blk.sync
            def _(sync):
                sync.wait_ge(sems.h["s_st2"], sems.at("s_st2"))
                sems.inc(sync.dma_start(
                    out=wo_sb.rearrange("p (k d) -> p k d", d=1024),
                    in_=l0_woT.rearrange("(k p) d -> p k d", p=128)), "s_ld", 16)
                sems.inc(sync.dma_start(out=maskG, in_=maskG_c.ap()), "s_ld", 16)
                sems.inc(sync.dma_start(out=mask0, in_=mask0_c.ap()), "s_ld", 16)
                cst0s = sems.at("s_cst")
                for c in range(NCHL):
                    if c == 2:
                        sync.wait_ge(sems.h["s_cst"], cst0s + 1)
                    if c >= 3:
                        sync.wait_ge(sems.h["s_sc"], 16 * (c - 1))
                        sync.wait_ge(sems.h["s_av"], 16 * (c - 1))
                    qkv = ("qkvA", "qkvB", "qkvC")[c % 3]
                    sems.inc(sync.dma_start(
                        out=qb[c % 3].rearrange("p (k t) -> p k t", t=128),
                        in_=qT_h_t[:, :, c * 128:(c + 1) * 128]), qkv, 16)
                    sems.inc(sync.dma_start(
                        out=kroll[c % 3].rearrange("p (k t) -> p k t", t=128),
                        in_=kT_h_t[:, :, c * 128:(c + 1) * 128]), qkv, 16)
                    sems.inc(sync.dma_start(
                        out=vroll[c % 3], in_=v_h[c * 128:(c + 1) * 128, :]), qkv, 16)
                    if c % 4 == 0:
                        grp = c // 4
                        if grp >= 2:
                            sync.wait_ge(sems.h["s_p1ev"], 8 * (grp - 1))
                        sems.inc(sync.dma_start(
                            out=xr_sb[grp % 2].rearrange("p (k t) -> p k t", t=512),
                            in_=xT_h_t[:, :, grp * 512:(grp + 1) * 512]),
                            "xrA" if grp % 2 == 0 else "xrB", 16)

            @blk.vector
            def _(vector):
                for nm in ("sq1s", "sk1s", "sv1s"):
                    vector.wait_ge(sems.h[nm], sems.at(nm))
                vector.memset(kroll[2], 0.0)
                sems.inc(vector.memset(vroll[2], 0.0), "s_cst", 1)
                for c in range(NCHL):
                    if p1b_stage < 2:
                        break
                    mask = mask0 if c % 16 == 0 else maskG
                    for h in range(16):
                        E = 16 * c + h
                        vector.wait_ge(sems.h["s_exp"], E + 1)
                        vector.tensor_mul(p_sb[:, h * 256:(h + 1) * 256],
                                          p_sb[:, h * 256:(h + 1) * 256], mask)
                        ins = vector.reduce_sum(den_sb[:, h:h + 1],
                                                p_sb[:, h * 256:(h + 1) * 256],
                                                axis=AX.X)
                        if h == 15:
                            sems.inc(ins, "s_vh", 1)
                    vector.wait_ge(sems.h["s_vh"], sems.at("s_vh"))
                    sems.inc(vector.reciprocal(rden_sb, den_sb), "s_vh", 1)
                    vector.wait_ge(sems.h["s_vh"], sems.at("s_vh"))
                    for h in range(16):
                        sems.inc(vector.tensor_scalar_mul(
                            p_sb[:, h * 256:(h + 1) * 256],
                            p_sb[:, h * 256:(h + 1) * 256],
                            rden_sb[:, h:h + 1]), "s_pf", 1)
                    for hp in range(8):
                        if p1b_stage < 4:
                            break
                        O = 8 * c + hp
                        if c % 4 == 0 and c >= 4 and hp == 0:
                            vector.wait_ge(sems.h["s_wo"], 8 * (c // 4))
                        vector.wait_ge(sems.h["s_av"], 16 * c + 2 * hp + 2)
                        sems.inc(vector.tensor_copy(
                            attnT_sb[:, hp * 512 + (c % 4) * 128: hp * 512 + (c % 4 + 1) * 128],
                            ps_ov[hp % 2]), "s_oev", 1)
                    if c % 4 == 3 and p1b_stage >= 5:
                        grp = c // 4
                        vector.wait_ge(sems.h["xrA" if grp % 2 == 0 else "xrB"],
                                       16 * (grp // 2 + 1))
                        if grp >= 1:
                            vector.wait_ge(sems.h["sp1"], 16 * grp)
                        for oc in range(8):
                            vector.wait_ge(sems.h["s_wo"], 8 * grp + oc + 1)
                            sems.inc(vector.tensor_add(
                                p1_sb[:, oc * 512:(oc + 1) * 512],
                                ps_wo[oc % 2],
                                xr_sb[grp % 2][:, oc * 512:(oc + 1) * 512]), "s_p1ev", 1)

            @blk.scalar
            def _(scalar):
                for nm in ("sq1s", "sk1s", "sv1s"):
                    scalar.wait_ge(sems.h[nm], sems.at(nm))
                for c in range(NCHL):
                    for h in range(16):
                        E = 16 * c + h
                        scalar.wait_ge(sems.h["s_sc"], E + 1)
                        if c >= 1 and p1b_stage >= 3:
                            scalar.wait_ge(sems.h["s_pt"], 16 * (c - 1) + h + 1)
                        sems.inc(scalar.activation(
                            out=p_sb[:, h * 256:(h + 1) * 256],
                            in_=ps_sc[h % 2],
                            func=AF.Exp, scale=cst[:, 2:3]), "s_exp", 1)
                    for h in range(16):
                        if p1b_stage < 3:
                            break
                        E = 16 * c + h
                        scalar.wait_ge(sems.h["s_pt"], E + 1)
                        if c >= 1 and p1b_stage >= 4:
                            scalar.wait_ge(sems.h["s_av"], 16 * (c - 1) + h + 1)
                        sems.inc(scalar.copy(
                            out=pT_sb[:, h * 256:(h + 1) * 256],
                            in_=ps_pt[h % 2]), "s_ptev", 1)

            @blk.tensor
            def _(tensor):
                tensor.wait_ge(sems.h["s_ld"], ld0 + 48)
                tensor.wait_ge(sems.h["s_cst"], cst0)
                for c in range(NCHL):
                    cm, pm = c % 3, (c - 1) % 3
                    tensor.wait_ge(sems.h[("qkvA", "qkvB", "qkvC")[c % 3]],
                                   48 * (c // 3 + 1))
                    for h in range(16):
                        E = 16 * c + h
                        if E >= 2:
                            tensor.wait_ge(sems.h["s_exp"], E - 1)
                        qs = qb[cm][(h % 2) * 64:(h % 2) * 64 + 64,
                                    (h // 2) * 128:(h // 2 + 1) * 128]
                        tensor.matmul(
                            ps_sc[h % 2][:, 0:128], qs,
                            kroll[pm][(h % 2) * 64:(h % 2) * 64 + 64,
                                      (h // 2) * 128:(h // 2 + 1) * 128],
                            start=True, stop=True)
                        sems.inc(tensor.matmul(
                            ps_sc[h % 2][:, 128:256], qs,
                            kroll[cm][(h % 2) * 64:(h % 2) * 64 + 64,
                                      (h // 2) * 128:(h // 2 + 1) * 128],
                            start=True, stop=True), "s_sc", 1)
                    for h in range(16):
                        if p1b_stage < 3:
                            break
                        E = 16 * c + h
                        tensor.wait_ge(sems.h["s_pf"], E + 1)
                        if E >= 2:
                            tensor.wait_ge(sems.h["s_ptev"], E - 1)
                        tensor.transpose(ps_pt[h % 2][:, 0:128],
                                         p_sb[:, h * 256:h * 256 + 128], ident)
                        sems.inc(tensor.transpose(
                            ps_pt[h % 2][:, 128:256],
                            p_sb[:, h * 256 + 128:(h + 1) * 256], ident), "s_pt", 1)
                    for h in range(16):
                        if p1b_stage < 4:
                            break
                        E = 16 * c + h
                        hp = h // 2
                        O = 8 * c + hp
                        tensor.wait_ge(sems.h["s_ptev"], E + 1)
                        if h % 2 == 0 and O >= 2:
                            tensor.wait_ge(sems.h["s_oev"], O - 1)
                        dst = ps_ov[hp % 2][(h % 2) * 64:(h % 2) * 64 + 64, :]
                        tensor.matmul(dst, vroll[pm][:, h * 64:(h + 1) * 64],
                                      pT_sb[:, h * 256:h * 256 + 128],
                                      start=True, stop=False)
                        sems.inc(tensor.matmul(
                            dst, vroll[cm][:, h * 64:(h + 1) * 64],
                            pT_sb[:, h * 256 + 128:(h + 1) * 256],
                            start=False, stop=True), "s_av", 1)
                    if c % 4 == 3 and p1b_stage >= 5:
                        grp = c // 4
                        tensor.wait_ge(sems.h["s_oev"], 8 * (c + 1))
                        for oc in range(8):
                            W = 8 * grp + oc
                            if W >= 2:
                                tensor.wait_ge(sems.h["s_p1ev"], W - 1)
                            for k in range(8):
                                ins = tensor.matmul(
                                    ps_wo[oc % 2],
                                    wo_sb[:, k * 1024 + oc * 128: k * 1024 + (oc + 1) * 128],
                                    attnT_sb[:, k * 512:(k + 1) * 512],
                                    start=(k == 0), stop=(k == 7))
                            sems.inc(ins, "s_wo", 1)

            @blk.gpsimd
            def _(g):
                for grp in range(NCHL // 4 if p1b_stage >= 5 else 0):
                    g.wait_ge(sems.h["s_p1ev"], 8 * (grp + 1))
                    sems.inc(g.dma_start(
                        out=p1T_h_t[:, :, grp * 512:(grp + 1) * 512],
                        in_=p1_sb.rearrange("p (k t) -> p k t", t=512)), "sp1", 16)

    if dev_phase is not None and dev_phase == 2:
        with nc.Block() as blk:
            @blk.gpsimd
            def _(g):
                g.wait_ge(sems.h["s_cc"], 1)
                g.wait_ge(sems.h["s_st"], sems.at("s_st"))
                sems.inc(g.dma_start(out=gmax_dbg, in_=ccx_out.ap()), "s_st", 16)
                g.wait_ge(sems.h["s_st"], sems.at("s_st"))
        return nc

    h_h_t = h_h.rearrange("(k p) t -> p k t", p=128)
    x1T_h_t = x1T_h.rearrange("(k p) t -> p k t", p=128)

    with contextlib.ExitStack() as ctx:
        sb = lambda nm, shp, dt=F32: ctx.enter_context(nc.sbuf_tensor(nm, shp, dt)).ap()
        pst = lambda nm, shp: ctx.enter_context(nc.psum_tensor(nm, shp, F32)).ap()
        wgqT = sb("wgqT", [128, 8 * I_FFN], BF16)
        wuqT = sb("wuqT", [128, 8 * I_FFN], BF16)
        wnat = [sb(f"wnat{i}", [128, 1024]) for i in range(2)]
        wq_s = sb("wq_s", [128, 1024])
        ws_g = sb("ws_g", [128, 22])
        ws_u = sb("ws_u", [128, 22])
        wsx_g = sb("wsx_g", [128, 22])
        wsx_u = sb("wsx_u", [128, 22])
        wiv = sb("wiv", [128, 2])
        xq_in = [sb(f"xq_in{i}", [128, 8 * 512]) for i in range(2)]
        xq_bf = [sb(f"xq_bf{i}", [128, 8 * 512], BF16) for i in range(2)]
        uq_s = sb("uq_s", [128, 512])
        h_ch = sb("h_ch", [128, 22 * 512])
        gs_t = sb("gs_t", [128, 512])
        us_t = sb("us_t", [128, 512])
        hm_all = sb("hm_all", [128, 22])
        hmax_run = sb("hmax_run", [128, 1])
        hmax_row = sb("hmax_row", [1, 128])
        hmax_sb = sb("hmax_sb", [1, 1])
        cx_sb = sb("cx_sb", [1, 1])
        gm_sb = sb("gm_sb", [1, 1])
        sig_sb = sb("sig_sb", [1, 1])
        t_sb = sb("t_sb", [1, 1])
        sx1 = sb("sx1", [1, 1])
        isx1 = sb("isx1", [1, 1])
        sxb = sb("sxb", [128, 1])
        invsx = sb("invsx", [128, 1])
        ps_w = [pst(f"ps_w{i}", [128, 512]) for i in range(2)]
        ps_g = [pst(f"ps_g{i}", [128, 512]) for i in range(2)]
        ps_u = [pst(f"ps_u{i}", [128, 512]) for i in range(2)]
        ps_b = pst("ps_b", [128, 128])

        ld0 = sems.at("s_ld3")
        st0 = sems.at("s_st")
        misc0 = sems.at("s_misc")
        sq0 = sems.at("s_sq")
        tpA0 = sems.at("s_tpA")
        wq0p2 = sems.at("s_wq")
        wnA0p2 = sems.at("wnA")
        wnB0p2 = sems.at("wnB")
        cst0p2 = sems.at("s_cst") + 1
        with nc.Block() as blk:
            @blk.sync
            def _(sync):
                sems.inc(sync.dma_start(out=identb, in_=identb_c.ap()), "s_ld3", 16)
                sems.inc(sync.dma_start(out=cx_sb, in_=l0_cx), "s_ld3", 16)
                for wi, w_src in enumerate((l0_wg, l0_wu)):
                    for it in range(IT):
                        W = wi * IT + it
                        if W >= 2:
                            sync.wait_ge(sems.h["s_wq"], wq0p2 + W - 1)
                            sync.wait_ge(sems.h["s_tpA"], tpA0 + 2 * (W - 1))
                        sems.inc(sync.dma_start(
                            out=wnat[W % 2],
                            in_=w_src[it * 128:(it + 1) * 128, :]),
                            "wnA" if W % 2 == 0 else "wnB", 16)
                for tc in range(NTC):
                    if tc >= 2:
                        sync.wait_ge(sems.h["s_qx"], 8 * (tc - 1))
                    sems.inc(sync.dma_start(
                        out=xq_in[tc % 2].rearrange("p (k t) -> p k t", t=512),
                        in_=xnT_h_t[:, :, tc * 512:(tc + 1) * 512]),
                        "xqA" if tc % 2 == 0 else "xqB", 16)

            @blk.gpsimd
            def _(g):
                g.wait_ge(sems.h["s_cc"], 1)
                sems.inc(g.dma_start(out=gm_sb, in_=ccx_out.ap()), "gmld", 16)
                for tc in range(NTC):
                    g.wait_ge(sems.h["s_guev"], 22 * (tc + 1))
                    sems.inc(g.dma_start(
                        out=h_h_t[:, :, tc * 512:(tc + 1) * 512],
                        in_=h_ch.rearrange("p (k t) -> p k t", t=512)), "shs", 16)
                g.wait_ge(sems.h["s_hm"], NTC + 3)
                sems.inc(g.dma_start(out=cch_in.ap(), in_=hmax_sb), "s_st", 16)
                g.wait_ge(sems.h["s_st"], sems.at("s_st"))
                sems.inc(g.collective_compute(
                    "AllReduce", ALU.max,
                    replica_groups=[list(range(NCORES))],
                    ins=[cch_in.ap()], outs=[cch_out.ap()]), "s_cc", 1)

            @blk.scalar
            def _(scalar):
                scalar.wait_ge(sems.h["sp1"], sems.at("sp1"))
                scalar.wait_ge(sems.h["s_ld3"], ld0 + 32)
                sems.inc(scalar.activation(out=sig_sb, in_=cx_sb,
                                           func=AF.Sigmoid), "s_scl", 1)
                for wi in range(2):
                    wdst = wgqT if wi == 0 else wuqT
                    for it in range(IT):
                        W = wi * IT + it
                        for g4 in range(2):
                            TP4 = 2 * W + g4
                            scalar.wait_ge(sems.h["s_tpA"], tpA0 + TP4 + 1)
                            dst = wdst.rearrange("p (k i) -> p k i", i=I_FFN)[
                                :, g4 * 4:(g4 + 1) * 4, it * 128:(it + 1) * 128]
                            sems.inc(scalar.copy(
                                out=dst,
                                in_=ps_w[TP4 % 2].rearrange("p (a b) -> p a b", b=128)),
                                "s_wT", 1)
                for tc in range(NTC):
                    for ic in range(IT):
                        GU = IT * tc + ic
                        if GU == 0:
                            scalar.wait_ge(sems.h["s_scl"], 6)
                        scalar.wait_ge(sems.h["s_gu"], GU + 1)
                        if GU >= 1:
                            scalar.wait_ge(sems.h["s_guev"], GU)
                        sems.inc(scalar.activation(
                            out=gs_t, in_=ps_g[GU % 2],
                            func=AF.Silu, scale=wsx_g[:, ic:ic + 1]), "s_guA", 1)

            @blk.vector
            def _(vector):
                vector.wait_ge(sems.h["sp1"], sems.at("sp1"))
                sems.inc(vector.memset(ones_r, 1.0), "s_cst", 1)
                for wi in range(2):
                    ws_dst = ws_g if wi == 0 else ws_u
                    for it in range(IT):
                        W = wi * IT + it
                        vector.wait_ge(
                            sems.h["wnA" if W % 2 == 0 else "wnB"],
                            (wnA0p2 if W % 2 == 0 else wnB0p2) + 16 * (W // 2 + 1))
                        sems.inc(vector.reduce_max(
                            ws_dst[:, it:it + 1], wnat[W % 2], axis=AX.X,
                            apply_absolute_value=True), "s_vh", 1)
                        vector.wait_ge(sems.h["s_vh"], sems.at("s_vh"))
                        sems.inc(vector.reciprocal(wiv[:, W % 2:W % 2 + 1],
                                                   ws_dst[:, it:it + 1]), "s_vh", 1)
                        vector.wait_ge(sems.h["s_vh"], sems.at("s_vh"))
                        vector.tensor_scalar_mul(
                            wq_s, wnat[W % 2], wiv[:, W % 2:W % 2 + 1])
                        vector.tensor_scalar(wq_s, wq_s, QMAX, RC, ALU.mult, ALU.add)
                        vector.tensor_scalar(wq_s, wq_s, RC, QMAX, ALU.subtract, ALU.min)
                        sems.inc(vector.tensor_scalar(
                            wnat[W % 2], wq_s, -QMAX - 1.0, None, ALU.max),
                            "s_wq", 1)
                vector.wait_ge(sems.h["gmld"], 16)
                vector.wait_ge(sems.h["s_scl"], 1)
                sems.inc(vector.tensor_mul(t_sb, sig_sb, gm_sb), "s_vh", 1)
                vector.wait_ge(sems.h["s_vh"], sems.at("s_vh"))
                sems.inc(vector.tensor_scalar_mul(sx1, t_sb, 1.0 / QMAX), "s_vh", 1)
                vector.wait_ge(sems.h["s_vh"], sems.at("s_vh"))
                sems.inc(vector.reciprocal(isx1, sx1), "s_scl", 1)
                vector.wait_ge(sems.h["s_bc"], 2)
                vector.tensor_copy(sxb, ps_b[:, 0:1])
                sems.inc(vector.tensor_copy(invsx, ps_b[:, 1:2]), "s_vh", 1)
                vector.wait_ge(sems.h["s_vh"], sems.at("s_vh"))
                vector.wait_ge(sems.h["s_wq"], wq0p2 + 2 * IT)
                vector.tensor_scalar(wsx_g, ws_g, sxb, 1.0 / QMAX, ALU.mult, ALU.mult)
                sems.inc(vector.tensor_scalar(
                    wsx_u, ws_u, sxb, 1.0 / QMAX, ALU.mult, ALU.mult), "s_scl", 4)
                vector.memset(hmax_run, 0.0)
                for tc in range(NTC):
                    vector.wait_ge(sems.h["xqA" if tc % 2 == 0 else "xqB"],
                                   16 * (tc // 2 + 1))
                    for k in range(KT):
                        Q = 8 * tc + k
                        vector.tensor_scalar(
                            uq_s, xq_in[tc % 2][:, k * 512:(k + 1) * 512],
                            invsx, RC, ALU.mult, ALU.add)
                        vector.tensor_scalar(uq_s, uq_s, RC, QMAX,
                                             ALU.subtract, ALU.min)
                        sems.inc(vector.tensor_scalar_max(
                            xq_bf[tc % 2][:, k * 512:(k + 1) * 512],
                            uq_s, -QMAX - 1.0), "s_qx", 1)
                    if tc >= 1:
                        vector.wait_ge(sems.h["shs"], 16 * tc)
                    for ic in range(IT):
                        GU = IT * tc + ic
                        vector.wait_ge(sems.h["s_gu"], GU + 1)
                        vector.tensor_scalar_mul(us_t, ps_u[GU % 2],
                                                 wsx_u[:, ic:ic + 1])
                        vector.wait_ge(sems.h["s_guA"], GU + 1)
                        sems.inc(vector.tensor_mul(
                            h_ch[:, ic * 512:(ic + 1) * 512], gs_t, us_t),
                            "s_guev", 1)
                        vector.reduce_max(hm_all[:, ic:ic + 1],
                                          h_ch[:, ic * 512:(ic + 1) * 512],
                                          axis=AX.X, apply_absolute_value=True)
                    sems.inc(vector.engine_nop(), "s_vh", 1)
                    vector.wait_ge(sems.h["s_vh"], sems.at("s_vh"))
                    sems.inc(vector.reduce_max(hm_all[:, 0:1], hm_all, axis=AX.X),
                             "s_vh", 1)
                    vector.wait_ge(sems.h["s_vh"], sems.at("s_vh"))
                    sems.inc(vector.tensor_max(hmax_run, hmax_run, hm_all[:, 0:1]),
                             "s_hm", 1)
                vector.wait_ge(sems.h["s_hm"], NTC + 1)
                sems.inc(vector.tensor_copy(hmax_row, ps_b[0:1, 0:128]), "s_vh", 1)
                vector.wait_ge(sems.h["s_vh"], sems.at("s_vh"))
                sems.inc(vector.reduce_max(hmax_sb, hmax_row, axis=AX.X), "s_hm", 2)

            @blk.tensor
            def _(tensor):
                tensor.wait_ge(sems.h["s_ld3"], ld0 + 16)
                for wi in range(2):
                    wdst = wgqT if wi == 0 else wuqT
                    for it in range(IT):
                        W = wi * IT + it
                        tensor.wait_ge(sems.h["s_wq"], wq0p2 + W + 1)
                        for k in range(KT):
                            TP = KT * W + k
                            if TP >= 8:
                                tensor.wait_ge(sems.h["s_wT"], (TP - 8) // 4 + 1)
                            ins = tensor.transpose(
                                ps_w[(TP // 4) % 2][:, (k % 4) * 128:(k % 4 + 1) * 128],
                                wnat[W % 2][:, k * 128:(k + 1) * 128], ident)
                            if k % 4 == 3:
                                sems.inc(ins, "s_tpA", 1)
                tensor.wait_ge(sems.h["s_cst"], cst0p2)
                tensor.wait_ge(sems.h["s_scl"], 2)
                tensor.matmul(ps_b[:, 0:1], ones_r, sx1, start=True, stop=True)
                sems.inc(tensor.matmul(ps_b[:, 1:2], ones_r, isx1,
                                       start=True, stop=True), "s_bc", 2)
                tensor.wait_ge(sems.h["s_wT"], 4 * IT)
                for tc in range(NTC):
                    for ic in range(IT):
                        GU = IT * tc + ic
                        if ic == 0:
                            tensor.wait_ge(sems.h["s_qx"], 8 * (tc + 1))
                        if GU >= 2:
                            tensor.wait_ge(sems.h["s_guA"], GU - 1)
                            tensor.wait_ge(sems.h["s_guev"], GU - 1)
                        for k in range(KT):
                            tensor.matmul(
                                ps_g[GU % 2],
                                wgqT[:, k * I_FFN + ic * 128: k * I_FFN + (ic + 1) * 128],
                                xq_bf[tc % 2][:, k * 512:(k + 1) * 512],
                                start=(k == 0), stop=(k == 7))
                        for k in range(KT):
                            ins = tensor.matmul(
                                ps_u[GU % 2],
                                wuqT[:, k * I_FFN + ic * 128: k * I_FFN + (ic + 1) * 128],
                                xq_bf[tc % 2][:, k * 512:(k + 1) * 512],
                                start=(k == 0), stop=(k == 7))
                        sems.inc(ins, "s_gu", 1)
                tensor.wait_ge(sems.h["s_hm"], NTC)
                sems.inc(tensor.transpose(ps_b[0:1, 0:128], hmax_run, ident),
                         "s_hm", 1)


    if dev_phase is not None and dev_phase == 3:
        with nc.Block() as blk:
            @blk.gpsimd
            def _(g):
                g.wait_ge(sems.h["s_cc"], 2)
                g.wait_ge(sems.h["s_st"], sems.at("s_st"))
                sems.inc(g.dma_start(out=gmax_dbg, in_=cch_out.ap()), "s_st", 16)
                g.wait_ge(sems.h["s_st"], sems.at("s_st"))
        return nc

    with contextlib.ExitStack() as ctx:
        sb = lambda nm, shp, dt=F32: ctx.enter_context(nc.sbuf_tensor(nm, shp, dt)).ap()
        pst = lambda nm, shp: ctx.enter_context(nc.psum_tensor(nm, shp, F32)).ap()
        wdqT = sb("wdqT", [128, IT * 1024], BF16)
        wdn = [sb(f"wdn{i}", [128, I_FFN]) for i in range(2)]
        wdq_s = sb("wdq_s", [128, I_FFN])
        ws_d = sb("ws_d", [128, 8])
        wsx_d = sb("wsx_d", [128, 8])
        wiv_d = sb("wiv_d", [128, 2])
        h_in = sb("h_in", [128, IT * 512])
        qh_bf = [sb(f"qh_bf{i}", [128, IT * 512], BF16) for i in range(2)]
        uq3 = sb("uq3", [128, 512])
        p1_in = sb("p1_in", [128, 8 * 512])
        x1_sb = sb("x1_sb", [128, 8 * 512])
        ch_sb = sb("ch_sb", [1, 1])
        cd_sb = sb("cd_sb", [1, 1])
        sigd_sb = sb("sigd_sb", [1, 1])
        td_sb = sb("td_sb", [1, 1])
        sh1 = sb("sh1", [1, 1])
        ish1 = sb("ish1", [1, 1])
        shb = sb("shb", [128, 1])
        invsh = sb("invsh", [128, 1])
        ps_w3 = [pst(f"ps_w3{i}", [128, 512]) for i in range(2)]
        ps_d = [pst(f"ps_d{i}", [128, 512]) for i in range(2)]
        ps_b3 = pst("ps_b3", [128, 128])

        ld0 = sems.at("s_ld3")
        st0 = sems.at("s_st")
        sq0 = sems.at("s_sq")
        tpA0 = sems.at("s_tpA")
        wq0 = sems.at("s_wq")
        wT0 = sems.at("s_wT")
        scl0 = sems.at("s_scl")
        qx0 = sems.at("s_qx")
        with nc.Block() as blk:
            @blk.sync
            def _(sync):
                sems.inc(sync.dma_start(out=cd_sb, in_=l0_cd), "s_ld3", 16)
                for nm in ("shs",):
                    sync.wait_ge(sems.h[nm], sems.at(nm))
                for oc in range(8):
                    if oc >= 2:
                        sync.wait_ge(sems.h["s_wq"], wq0 + oc - 1)
                        sync.wait_ge(sems.h["s_tpA"], tpA0 + 6 * (oc - 1))
                    sems.inc(sync.dma_start(
                        out=wdn[oc % 2],
                        in_=l0_wd[oc * 128:(oc + 1) * 128, :]),
                        "wdA" if oc % 2 == 0 else "wdB", 16)
                for tc in range(NTC):
                    if tc >= 1:
                        sync.wait_ge(sems.h["s_qx"], qx0 + IT * tc)
                    sems.inc(sync.dma_start(
                        out=h_in.rearrange("p (k t) -> p k t", t=512),
                        in_=h_h_t[:, :, tc * 512:(tc + 1) * 512]), "hin1", 16)
                    if tc >= 1:
                        sync.wait_ge(sems.h["s_x1ev"], 8 * tc)
                    sems.inc(sync.dma_start(
                        out=p1_in.rearrange("p (k t) -> p k t", t=512),
                        in_=p1T_h_t[:, :, tc * 512:(tc + 1) * 512]), "p1in1", 16)

            @blk.scalar
            def _(scalar):
                scalar.wait_ge(sems.h["shs"], sems.at("shs"))
                scalar.wait_ge(sems.h["s_ld3"], ld0 + 16)
                sems.inc(scalar.activation(out=sigd_sb, in_=cd_sb,
                                           func=AF.Sigmoid), "s_scl", 1)
                for oc in range(8):
                    for g6 in range(6):
                        TP4 = 6 * oc + g6
                        nblk = 4 if g6 < 5 else 2
                        scalar.wait_ge(sems.h["s_tpA"], tpA0 + TP4 + 1)
                        dst = wdqT.rearrange("p (k i) -> p k i", i=1024)[
                            :, g6 * 4:g6 * 4 + nblk, oc * 128:(oc + 1) * 128]
                        sems.inc(scalar.copy(
                            out=dst,
                            in_=ps_w3[TP4 % 2].rearrange(
                                "p (a b) -> p a b", b=128)[:, 0:nblk]),
                            "s_wT", 1)

            @blk.vector
            def _(vector):
                vector.wait_ge(sems.h["shs"], sems.at("shs"))
                for oc in range(8):
                    W = oc
                    vector.wait_ge(sems.h["wdA" if oc % 2 == 0 else "wdB"],
                                   16 * (oc // 2 + 1))
                    sems.inc(vector.reduce_max(
                        ws_d[:, oc:oc + 1], wdn[oc % 2], axis=AX.X,
                        apply_absolute_value=True), "s_vh", 1)
                    vector.wait_ge(sems.h["s_vh"], sems.at("s_vh"))
                    sems.inc(vector.reciprocal(wiv_d[:, oc % 2:oc % 2 + 1],
                                               ws_d[:, oc:oc + 1]), "s_vh", 1)
                    vector.wait_ge(sems.h["s_vh"], sems.at("s_vh"))
                    vector.tensor_scalar_mul(wdq_s, wdn[oc % 2],
                                              wiv_d[:, oc % 2:oc % 2 + 1])
                    vector.tensor_scalar(wdq_s, wdq_s, QMAX, RC, ALU.mult, ALU.add)
                    vector.tensor_scalar(wdq_s, wdq_s, RC, QMAX, ALU.subtract, ALU.min)
                    sems.inc(vector.tensor_scalar_max(
                        wdn[oc % 2], wdq_s, -QMAX - 1.0), "s_wq", 1)
                vector.wait_ge(sems.h["s_cc"], 2)
                vector.wait_ge(sems.h["chld"], 16)
                vector.wait_ge(sems.h["s_scl"], scl0 + 1)
                sems.inc(vector.tensor_mul(td_sb, sigd_sb, ch_sb), "s_vh", 1)
                vector.wait_ge(sems.h["s_vh"], sems.at("s_vh"))
                sems.inc(vector.tensor_scalar_mul(sh1, td_sb, 1.0 / QMAX), "s_vh", 1)
                vector.wait_ge(sems.h["s_vh"], sems.at("s_vh"))
                sems.inc(vector.reciprocal(ish1, sh1), "s_scl", 1)
                vector.wait_ge(sems.h["s_bc"], 4)
                vector.tensor_copy(shb, ps_b3[:, 0:1])
                sems.inc(vector.tensor_copy(invsh, ps_b3[:, 1:2]), "s_vh", 1)
                vector.wait_ge(sems.h["s_vh"], sems.at("s_vh"))
                vector.wait_ge(sems.h["s_wq"], wq0 + 8)
                sems.inc(vector.tensor_scalar(
                    wsx_d, ws_d, shb, 1.0 / QMAX, ALU.mult, ALU.mult), "s_scl", 4)
                for tc in range(NTC):
                    vector.wait_ge(sems.h["hin1"], 16 * (tc + 1))
                    for ic in range(IT):
                        Q = IT * tc + ic
                        vector.tensor_scalar(
                            uq3, h_in[:, ic * 512:(ic + 1) * 512],
                            invsh, RC, ALU.mult, ALU.add)
                        vector.tensor_scalar(uq3, uq3, RC, QMAX,
                                             ALU.subtract, ALU.min)
                        sems.inc(vector.tensor_scalar_max(
                            qh_bf[tc % 2][:, ic * 512:(ic + 1) * 512],
                            uq3, -QMAX - 1.0), "s_qx", 1)
                    vector.wait_ge(sems.h["p1in1"], 16 * (tc + 1))
                    if tc >= 1:
                        vector.wait_ge(sems.h["sx1s"], 16 * tc)
                    for oc in range(8):
                        DN = 8 * tc + oc
                        vector.wait_ge(sems.h["s_dn"], DN + 1)
                        sems.inc(vector.scalar_tensor_tensor(
                            out=x1_sb[:, oc * 512:(oc + 1) * 512],
                            in0=ps_d[DN % 2],
                            scalar=wsx_d[:, oc:oc + 1],
                            in1=p1_in[:, oc * 512:(oc + 1) * 512],
                            op0=ALU.mult, op1=ALU.add), "s_x1ev", 1)

            @blk.tensor
            def _(tensor):
                for oc in range(8):
                    tensor.wait_ge(sems.h["s_wq"], wq0 + oc + 1)
                    for k in range(IT):
                        TP = IT * oc + k
                        g6 = 6 * oc + (k // 4)
                        if g6 >= 2 and k % 4 == 0:
                            tensor.wait_ge(sems.h["s_wT"], wT0 + g6 - 1)
                        ins = tensor.transpose(
                            ps_w3[g6 % 2][:, (k % 4) * 128:(k % 4 + 1) * 128],
                            wdn[oc % 2][:, k * 128:(k + 1) * 128], ident)
                        if k % 4 == 3 or k == IT - 1:
                            sems.inc(ins, "s_tpA", 1)
                tensor.wait_ge(sems.h["s_scl"], scl0 + 2)
                tensor.matmul(ps_b3[:, 0:1], ones_r, sh1, start=True, stop=True)
                sems.inc(tensor.matmul(ps_b3[:, 1:2], ones_r, ish1,
                                       start=True, stop=True), "s_bc", 2)
                tensor.wait_ge(sems.h["s_wT"], wT0 + 48)
                for tc in range(NTC):
                    for oc in range(8):
                        DN = 8 * tc + oc
                        if oc == 0:
                            tensor.wait_ge(sems.h["s_qx"], qx0 + IT * (tc + 1))
                        if DN >= 2:
                            tensor.wait_ge(sems.h["s_x1ev"], DN - 1)
                        for k in range(IT):
                            ins = tensor.matmul(
                                ps_d[DN % 2],
                                wdqT[:, k * 1024 + oc * 128: k * 1024 + (oc + 1) * 128],
                                qh_bf[tc % 2][:, k * 512:(k + 1) * 512],
                                start=(k == 0), stop=(k == IT - 1))
                        sems.inc(ins, "s_dn", 1)

            @blk.gpsimd
            def _(g):
                g.wait_ge(sems.h["s_cc"], 2)
                sems.inc(g.dma_start(out=ch_sb, in_=cch_out.ap()), "chld", 16)
                for tc in range(NTC):
                    g.wait_ge(sems.h["s_x1ev"], 8 * (tc + 1))
                    sems.inc(g.dma_start(
                        out=x1T_h_t[:, :, tc * 512:(tc + 1) * 512],
                        in_=x1_sb.rearrange("p (k t) -> p k t", t=512)), "sx1s", 16)

    if dev_phase is not None and dev_phase == 4:
        with nc.Block() as blk:
            @blk.gpsimd
            def _(g):
                g.wait_ge(sems.h["s_st"], sems.at("s_st"))
                sems.inc(g.dma_start(out=gmax_dbg, in_=cch_out.ap()), "s_st", 16)
                g.wait_ge(sems.h["s_st"], sems.at("s_st"))
        return nc

    xn1T_h_t = xn1T_h.rearrange("(k p) t -> p k t", p=128)
    q1T_h_t = q1T_h.rearrange("(k p) t -> p k t", p=128)
    k1T_h_t = k1T_h.rearrange("(k p) t -> p k t", p=128)
    v1_h_t = v1_h.rearrange("(c a p) d -> c p a d", a=4, p=128)
    fullT_h_t = fullT_h.rearrange("(k p) t -> p k t", p=128)
    xcT_h_t = xcT_h.rearrange("(b p) t -> p b t", p=128)
    p2T_h_t = p2T_h.rearrange("(k p) t -> p k t", p=128)
    out_h_t = out_h.rearrange("(c a p) d -> c p a d", a=4, p=128)

    with contextlib.ExitStack() as ctx:
        sb = lambda nm, shp, dt=F32: ctx.enter_context(nc.sbuf_tensor(nm, shp, dt)).ap()
        pst = lambda nm, shp: ctx.enter_context(nc.psum_tensor(nm, shp, F32)).ap()
        w4 = {}
        for wn in ("q", "k", "v"):
            w4[wn] = (sb(f"w4{wn}h", [128, 8 * 1024], BF16),
                      sb(f"w4{wn}l", [128, 8 * 1024], BF16))
        x1in_ = sb("x1in", [128, 8 * 512])
        x1in = [x1in_, x1in_]
        xn1o_ = sb("xn1o", [128, 8 * 512])
        xn1o = [xn1o_, xn1o_]
        x4h = [sb(f"x4h{i}", [128, 8 * 512], BF16) for i in range(2)]
        x4l = [sb(f"x4l{i}", [128, 8 * 512], BF16) for i in range(2)]
        sq1 = sb("sq1", [128, 512])
        mrow = sb("mrow", [1, 512])
        e2row = sb("e2row", [1, 512])
        vrow = e2row
        srow = sb("srow", [1, 512])
        rrow = srow
        mb_sb = sb("mb_sb", [128, 512])
        rb_sb = sb("rb_sb", [128, 512])
        o1buf = sb("o1buf", [128, 8 * 512])
        o2buf = sb("o2buf", [128, 8 * 512])
        q1o, k1o, v1o = o1buf, o2buf, o1buf
        fm_e = sb("fm_e", [128, 512])
        ps_s1 = pst("ps_s1", [128, 512])
        ps_bc1 = pst("ps_bc1", [128, 512])
        ps_p4 = [pst(f"ps_p4{i}", [128, 512]) for i in range(4)]

        ld0 = sems.at("s_ld2")
        st0 = sems.at("s_st2")
        xn0 = sems.at("s_xn")
        var0 = sems.at("s_var")
        std0 = sems.at("s_std")
        mm0 = sems.at("s_mm")
        ln0 = sems.at("s_ln")
        lnb0 = sems.at("s_lnb")
        sq0v = sems.at("s_sq")
        pj0 = sems.at("s_pj")
        pjA0 = sems.at("s_pjA")
        pjB0 = sems.at("s_pjB")
        fm0 = sems.at("s_fm")
        evA0 = sems.at("s_evA")
        sq10 = sems.at("sq1s")
        sv10 = sems.at("sv1s")
        wq4 = sems.at("s_wq")
        sq4s = sems.at("s_sq")
        pj0s = sems.at("s_pj")
        qx4 = sems.at("s_qx")
        wnA4 = sems.at("wnA")
        wnB4 = sems.at("wnB")
        scl4 = sems.at("s_scl")
        with nc.Block() as blk:
            @blk.sync
            def _(sync):
                sync.wait_ge(sems.h["sx1s"], sems.at("sx1s"))
                for wi, w_hh in enumerate((l1_wq, l1_wk, l1_wv)):
                    for hf in range(4):
                        WH = 4 * wi + hf
                        if WH >= 1:
                            sync.wait_ge(sems.h["s_wq"], wq4 + WH)
                        sems.inc(sync.dma_start(
                            out=o1buf[:, 0:2048].rearrange("p (k d) -> p k d", d=1024),
                            in_=w_hh.rearrange("(k p) d -> p k d", p=128)[
                                :, hf * 2:(hf + 1) * 2]),
                            "wnA" if WH % 2 == 0 else "wnB", 16)
                for tc in range(NTC):
                    if tc >= 1:
                        sync.wait_ge(sems.h["s_xn"], xn0 + tc)
                    sems.inc(sync.dma_start(
                        out=x1in[tc % 2].rearrange("p (k t) -> p k t", t=512),
                        in_=x1T_h_t[:, :, tc * 512:(tc + 1) * 512]), "x1inA", 16)

            @blk.scalar
            def _(scalar):
                scalar.wait_ge(sems.h["sx1s"], sems.at("sx1s"))
                for wi, wn in enumerate(("q", "k", "v")):
                    for hf in range(4):
                        WH = 4 * wi + hf
                        scalar.wait_ge(sems.h["wnA" if WH % 2 == 0 else "wnB"],
                                       (wnA4 if WH % 2 == 0 else wnB4)
                                       + 16 * (WH // 2 + 1))
                        sems.inc(scalar.copy(
                            out=w4[wn][0][:, hf * 2048:(hf + 1) * 2048],
                            in_=o1buf[:, 0:2048]), "s_sq", 1)
                for tc in range(NTC):
                    scalar.wait_ge(sems.h["x1inA"], 16 * (tc + 1))
                    for k in range(KT):
                        SQ = 8 * tc + k
                        if SQ >= 1:
                            scalar.wait_ge(sems.h["s_ln"], ln0 + SQ)
                        sems.inc(scalar.activation(
                            out=sq1, in_=x1in[tc % 2][:, k * 512:(k + 1) * 512],
                            func=AF.Square), "s_sq", 1)
                    scalar.wait_ge(sems.h["s_var"], var0 + tc + 1)
                    sems.inc(scalar.activation(
                        out=srow, in_=vrow, func=AF.Sqrt, bias=cst[0:1, 0:1]),
                        "s_std", 1)
                    scalar.wait_ge(sems.h["s_xn"], xn0 + tc + 1)
                    if tc >= 2:
                        scalar.wait_ge(sems.h["s_pj"], pj0s + 24 * (tc - 1))
                    sems.inc(scalar.copy(out=x4h[tc % 2], in_=xn1o[tc % 2]),
                             "s_scl", 1)
                    for j in range(16):
                        FM = 16 * tc + j
                        scalar.wait_ge(sems.h["s_fm"], fm0 + 3 * FM + 1)
                        sems.inc(scalar.activation(
                            out=fm_e, in_=fm_e, func=AF.Exp), "s_fm", 2)

            @blk.tensor
            def _(tensor):
                for tc in range(NTC):
                    tensor.wait_ge(sems.h["x1inA"], 16 * (tc + 1))
                    if tc >= 1:
                        tensor.wait_ge(sems.h["s_var"], var0 + tc)
                    for k in range(KT):
                        tensor.matmul(ps_s1[0:1, 0:512], ones_c,
                                      x1in[tc % 2][:, k * 512:(k + 1) * 512],
                                      start=(k == 0), stop=(k == 7))
                    for k in range(KT):
                        tensor.wait_ge(sems.h["s_sq"], sq0v + 12 + 8 * tc + k + 1)
                        ins = tensor.matmul(ps_s1[64:65, 0:512],
                                            ones_c, sq1,
                                            start=(k == 0), stop=(k == 7))
                        sems.inc(ins, "s_ln", 1)
                    tensor.wait_ge(sems.h["s_var"], var0 + tc + 1)
                    if tc >= 1:
                        tensor.wait_ge(sems.h["s_evA"], evA0 + 2 * tc)
                    tensor.matmul(ps_bc1[:, 0:256], ones_r, mrow[0:1, 0:256],
                                  start=True, stop=True)
                    sems.inc(tensor.matmul(ps_bc1[:, 256:512], ones_r,
                                           mrow[0:1, 256:512],
                                           start=True, stop=True), "s_lnb", 1)
                    tensor.wait_ge(sems.h["s_mm"], mm0 + tc + 1)
                    tensor.wait_ge(sems.h["s_evA"], evA0 + 2 * tc + 1)
                    tensor.matmul(ps_bc1[:, 0:256], ones_r, rrow[0:1, 0:256],
                                  start=True, stop=True)
                    sems.inc(tensor.matmul(ps_bc1[:, 256:512], ones_r,
                                           rrow[0:1, 256:512],
                                           start=True, stop=True), "s_lnb", 1)
                    if tc == 0:
                        tensor.wait_ge(sems.h["s_wq"], wq4 + 12)
                    tensor.wait_ge(sems.h["s_scl"], scl4 + tc + 1)
                    tensor.wait_ge(sems.h["s_qx"], qx4 + tc + 1)
                    xh, xl = x4h[tc % 2], x4l[tc % 2]
                    tiles = [("q", oc) for oc in range(8)] + \
                            [("k", oc) for oc in range(8)] + \
                            [("v", (su, oh)) for su in range(4) for oh in range(2)]
                    for ti, (kindt, idx) in enumerate(tiles):
                        G = 24 * tc + ti
                        if G >= 4:
                            tensor.wait_ge(sems.h["s_pjA"], pjA0 + G - 3)
                        pb = ps_p4[G % 4]
                        if kindt in ("q", "k"):
                            wh, wl = w4[kindt]
                            oc = idx
                            np_ = 0
                            for k in range(KT):
                                wsl = slice(k * 1024 + oc * 128,
                                            k * 1024 + (oc + 1) * 128)
                                xsl = slice(k * 512, (k + 1) * 512)
                                for a, b in ((wh, xh), (wh, xl), (wl, xh)):
                                    ins = tensor.matmul(
                                        pb, a[:, wsl], b[:, xsl],
                                        start=(np_ == 0), stop=(np_ == 23))
                                    np_ += 1
                        else:
                            su, oh = idx
                            np_ = 0
                            for k in range(KT):
                                xsl = slice(k * 512 + su * 128,
                                            k * 512 + (su + 1) * 128)
                                wsl = slice(k * 1024 + oh * 512,
                                            k * 1024 + (oh + 1) * 512)
                                for a, b in ((xh, w4["v"][0]), (xh, w4["v"][1]),
                                             (xl, w4["v"][0])):
                                    ins = tensor.matmul(
                                        pb, a[:, xsl], b[:, wsl],
                                        start=(np_ == 0), stop=(np_ == 23))
                                    np_ += 1
                        sems.inc(ins, "s_pj", 1)

            @blk.vector
            def _(vector):
                vector.wait_ge(sems.h["sx1s"], sems.at("sx1s"))
                for wi, wn in enumerate(("q", "k", "v")):
                    for hf in range(4):
                        WH = 4 * wi + hf
                        vector.wait_ge(sems.h["s_sq"], sq4s + WH + 1)
                        sems.inc(vector.tensor_sub(
                            w4[wn][1][:, hf * 2048:(hf + 1) * 2048],
                            o1buf[:, 0:2048],
                            w4[wn][0][:, hf * 2048:(hf + 1) * 2048]), "s_wq", 1)
                for tc in range(NTC):
                    vector.wait_ge(sems.h["s_ln"], ln0 + 8 * (tc + 1))
                    vector.tensor_scalar_mul(mrow, ps_s1[0:1, 0:512], 1.0 / D)
                    sems.inc(vector.tensor_scalar_mul(
                        e2row, ps_s1[64:65, 0:512], 1.0 / D), "s_vh", 1)
                    vector.wait_ge(sems.h["s_vh"], sems.at("s_vh"))
                    sems.inc(vector.tensor_mul(srow, mrow, mrow), "s_vh", 1)
                    vector.wait_ge(sems.h["s_vh"], sems.at("s_vh"))
                    sems.inc(vector.tensor_sub(vrow, e2row, srow), "s_var", 1)
                    vector.wait_ge(sems.h["s_std"], std0 + tc + 1)
                    sems.inc(vector.reciprocal(rrow, srow), "s_mm", 1)
                    vector.wait_ge(sems.h["s_lnb"], lnb0 + 2 * tc + 1)
                    sems.inc(vector.tensor_copy(mb_sb, ps_bc1), "s_evA", 1)
                    vector.wait_ge(sems.h["s_lnb"], lnb0 + 2 * tc + 2)
                    sems.inc(vector.tensor_copy(rb_sb, ps_bc1), "s_evA", 1)
                    sems.inc(vector.engine_nop(), "s_vh", 1)
                    vector.wait_ge(sems.h["s_vh"], sems.at("s_vh"))
                    if tc >= 1:
                        pv = tc - 1
                        vector.wait_ge(sems.h["sxn1A" if pv % 2 == 0 else "sxn1B"],
                                       16 * (pv // 2 + 1))
                        vector.wait_ge(sems.h["s_scl"], scl4 + tc)
                    for k in range(KT):
                        vector.tensor_sub(xn1o[tc % 2][:, k * 512:(k + 1) * 512],
                                          x1in[tc % 2][:, k * 512:(k + 1) * 512],
                                          mb_sb)
                        ins = vector.tensor_mul(
                            xn1o[tc % 2][:, k * 512:(k + 1) * 512],
                            xn1o[tc % 2][:, k * 512:(k + 1) * 512], rb_sb)
                    sems.inc(ins, "s_xn", 1)
                    if tc >= 2:
                        vector.wait_ge(sems.h["s_pj"], pj0 + 24 * (tc - 1))
                    vector.wait_ge(sems.h["s_scl"], scl4 + tc + 1)
                    sems.inc(vector.tensor_sub(x4l[tc % 2], xn1o[tc % 2],
                                               x4h[tc % 2]), "s_qx", 1)
                    for ti in range(24):
                        G = 24 * tc + ti
                        vector.wait_ge(sems.h["s_pj"], pj0 + G + 1)
                        pb = ps_p4[G % 4]
                        if ti == 0 and tc >= 1:
                            vector.wait_ge(sems.h["sv1s"], sv10 + 16 * tc)
                        if ti == 16:
                            vector.wait_ge(sems.h["sq1s"], sq10 + 16 * (tc + 1))
                        if ti < 16:
                            dst = (q1o if ti < 8 else k1o)[:, (ti % 8) * 512:(ti % 8 + 1) * 512]
                            FM = 16 * tc + ti
                            sems.inc(vector.tensor_scalar_min(fm_e, pb, 0.0),
                                     "s_fm", 1)
                            sems.inc(vector.tensor_relu(dst, pb), "s_pjA", 1)
                            vector.wait_ge(sems.h["s_fm"], fm0 + 3 * FM + 3)
                            sems.inc(vector.tensor_add(dst, dst, fm_e), "s_pjB", 1)
                        else:
                            su, oh = (ti - 16) // 2, (ti - 16) % 2
                            dst = v1o[:, su * 1024 + oh * 512: su * 1024 + (oh + 1) * 512]
                            sems.inc(vector.tensor_copy(dst, pb), "s_pjA", 1)
                            sems.inc(vector.engine_nop(), "s_pjB", 1)

            @blk.gpsimd
            def _(g):
                for tc in range(NTC):
                    g.wait_ge(sems.h["s_xn"], xn0 + tc + 1)
                    sems.inc(g.dma_start(
                        out=xn1T_h_t[:, :, tc * 512:(tc + 1) * 512],
                        in_=xn1o[tc % 2].rearrange("p (k t) -> p k t", t=512)),
                        "sxn1A" if tc % 2 == 0 else "sxn1B", 16)
                    g.wait_ge(sems.h["s_pjB"], pjB0 + 24 * tc + 8)
                    sems.inc(g.dma_start(
                        out=q1T_h_t[:, :, tc * 512:(tc + 1) * 512],
                        in_=q1o.rearrange("p (k t) -> p k t", t=512)), "sq1s", 16)
                    g.wait_ge(sems.h["s_pjB"], pjB0 + 24 * tc + 16)
                    sems.inc(g.dma_start(
                        out=k1T_h_t[:, :, tc * 512:(tc + 1) * 512],
                        in_=k1o.rearrange("p (k t) -> p k t", t=512)), "sk1s", 16)
                    g.wait_ge(sems.h["s_pjB"], pjB0 + 24 * (tc + 1))
                    sems.inc(g.dma_start(
                        out=v1_h_t[tc],
                        in_=v1o.rearrange("p (a d) -> p a d", d=1024)), "sv1s", 16)

    with contextlib.ExitStack() as ctx:
        sb = lambda nm, shp, dt=F32: ctx.enter_context(nc.sbuf_tensor(nm, shp, dt)).ap()
        pst = lambda nm, shp: ctx.enter_context(nc.psum_tensor(nm, shp, F32)).ap()
        l1w_sb = sb("l1w_sb", [128, 8 * 1024])
        sobd_sb = sb("sobd_sb", [128, 8 * 256])
        sibd_sb = sb("sibd_sb", [128, 8 * 256])
        cvb_sb = sb("cvb_sb", [128, 4])
        xnin = [sb(f"xnin{i}", [128, 8 * 512]) for i in range(2)]
        fullo = sb("fullo", [128, 8 * 512])
        xco = sb("xco", [128, 4 * 512])
        sqx = [sb(f"sqx{i}", [128, 512]) for i in range(2)]
        bn_acc = sb("bn_acc", [128, 8])
        bn_t = sb("bn_t", [128, 8])
        ps_f = [pst(f"ps_f{i}", [128, 512]) for i in range(2)]
        ps_x = [pst(f"ps_x{i}", [128, 512]) for i in range(2)]

        fu0 = sems.at("s_pj")
        fuev0 = sems.at("s_pjA")
        xc20 = sems.at("s_sc")
        xcev0 = sems.at("s_exp")
        sfu0 = sems.at("sq1s")
        sxc0 = sems.at("sk1s")
        bna0 = sems.at("s_hm")
        sq0v = sems.at("s_sq")
        xiA0 = sems.at("xiA")
        xiB0 = sems.at("xiB")
        ld20 = sems.at("s_ld2") + 64
        with nc.Block() as blk:
            @blk.sync
            def _(sync):
                for nm in ("sxn1A", "sxn1B"):
                    sync.wait_ge(sems.h[nm], sems.at(nm))
                sems.inc(sync.dma_start(
                    out=l1w_sb.rearrange("p (k d) -> p k d", d=1024),
                    in_=l1_w.rearrange("(k p) d -> p k d", p=128)), "s_ld2", 16)
                sems.inc(sync.dma_start(
                    out=sobd_sb.rearrange("p (k d) -> p k d", d=256),
                    in_=so_bd.rearrange("(k p) d -> p k d", p=128)), "s_ld2", 16)
                sems.inc(sync.dma_start(
                    out=sibd_sb.rearrange("p (k d) -> p k d", d=256),
                    in_=si_bd.rearrange("(k p) d -> p k d", p=128)), "s_ld2", 16)
                sems.inc(sync.dma_start(out=cvb_sb, in_=cvb_c), "s_ld2", 16)
                for tc in range(NTC):
                    if tc >= 2:
                        sync.wait_ge(sems.h["s_pj"], fu0 + 8 * (tc - 1))
                        sync.wait_ge(sems.h["s_sc"], xc20 + 4 * (tc - 1))
                    sems.inc(sync.dma_start(
                        out=xnin[tc % 2].rearrange("p (k t) -> p k t", t=512),
                        in_=xn1T_h_t[:, :, tc * 512:(tc + 1) * 512]),
                        "xiA" if tc % 2 == 0 else "xiB", 16)

            @blk.tensor
            def _(tensor):
                tensor.wait_ge(sems.h["s_ld2"], ld20)
                for tc in range(NTC):
                    tensor.wait_ge(sems.h["xiA" if tc % 2 == 0 else "xiB"],
                                   (xiA0 if tc % 2 == 0 else xiB0) + 16 * (tc // 2 + 1))
                    for oc in range(8):
                        G = 8 * tc + oc
                        if G >= 2:
                            tensor.wait_ge(sems.h["s_pjA"], fuev0 + G - 1)
                        for k in range(KT):
                            ins = tensor.matmul(
                                ps_f[G % 2],
                                l1w_sb[:, k * 1024 + oc * 128: k * 1024 + (oc + 1) * 128],
                                xnin[tc % 2][:, k * 512:(k + 1) * 512],
                                start=(k == 0), stop=(k == 7))
                        sems.inc(ins, "s_pj", 1)
                    for bj in range(4):
                        XG = 4 * tc + bj
                        br, gt = bj // 2, bj % 2
                        bd = sobd_sb if br == 0 else sibd_sb
                        if XG >= 2:
                            tensor.wait_ge(sems.h["s_exp"], xcev0 + XG - 1)
                        for k in range(KT):
                            ins = tensor.matmul(
                                ps_x[XG % 2],
                                bd[:, k * 256 + gt * 128: k * 256 + (gt + 1) * 128],
                                xnin[tc % 2][:, k * 512:(k + 1) * 512],
                                start=(k == 0), stop=(k == 7))
                        sems.inc(ins, "s_sc", 1)

            @blk.scalar
            def _(scalar):
                for nm in ("sxn1A", "sxn1B", "sq1s", "sk1s", "sv1s"):
                    scalar.wait_ge(sems.h[nm], sems.at(nm))
                sqr0s = sems.at("s_sqr")
                for tc in range(NTC):
                    for bj in range(4):
                        XG = 4 * tc + bj
                        scalar.wait_ge(sems.h["s_exp"], xcev0 + XG + 1)
                        if XG >= 2:
                            scalar.wait_ge(sems.h["s_sqr"], sqr0s + XG - 1)
                        sems.inc(scalar.activation(
                            out=sqx[XG % 2], in_=xco[:, bj * 512:(bj + 1) * 512],
                            func=AF.Square), "s_sq", 1)

            @blk.vector
            def _(vector):
                for nm in ("sxn1A", "sxn1B", "sq1s", "sk1s", "sv1s"):
                    vector.wait_ge(sems.h[nm], sems.at(nm))
                sqr0v = sems.at("s_sqr")
                vector.memset(bn_acc, 0.0)
                for tc in range(NTC):
                    if tc >= 1:
                        vector.wait_ge(sems.h["sq1s"], sfu0 + 16 * tc)
                    for oc in range(8):
                        G = 8 * tc + oc
                        vector.wait_ge(sems.h["s_pj"], fu0 + G + 1)
                        sems.inc(vector.tensor_copy(
                            fullo[:, oc * 512:(oc + 1) * 512], ps_f[G % 2]),
                            "s_pjA", 1)
                    if tc >= 1:
                        vector.wait_ge(sems.h["sk1s"], sxc0 + 16 * tc)
                    for bj in range(4):
                        XG = 4 * tc + bj
                        vector.wait_ge(sems.h["s_sc"], xc20 + XG + 1)
                        sems.inc(vector.tensor_scalar_add(
                            xco[:, bj * 512:(bj + 1) * 512], ps_x[XG % 2],
                            cvb_sb[:, bj:bj + 1]), "s_exp", 1)
                        vector.reduce_sum(bn_t[:, bj:bj + 1],
                                          xco[:, bj * 512:(bj + 1) * 512],
                                          axis=AX.X)
                        vector.wait_ge(sems.h["s_sq"], sq0v + XG + 1)
                        sems.inc(vector.reduce_sum(
                            bn_t[:, 4 + bj:5 + bj], sqx[XG % 2], axis=AX.X),
                            "s_sqr", 1)
                    sems.inc(vector.engine_nop(), "s_vh", 1)
                    vector.wait_ge(sems.h["s_vh"], sems.at("s_vh"))
                    sems.inc(vector.tensor_add(bn_acc, bn_acc, bn_t), "s_hm", 1)

            @blk.gpsimd
            def _(g):
                for tc in range(NTC):
                    g.wait_ge(sems.h["s_pjA"], fuev0 + 8 * (tc + 1))
                    sems.inc(g.dma_start(
                        out=fullT_h_t[:, :, tc * 512:(tc + 1) * 512],
                        in_=fullo.rearrange("p (k t) -> p k t", t=512)), "sq1s", 16)
                    g.wait_ge(sems.h["s_hm"], bna0 + tc + 1)
                    sems.inc(g.dma_start(
                        out=xcT_h_t[:, :, tc * 512:(tc + 1) * 512],
                        in_=xco.rearrange("p (b t) -> p b t", t=512)), "sk1s", 16)
                g.wait_ge(sems.h["s_hm"], bna0 + NTC)
                sems.inc(g.dma_start(out=ccbn_in.ap(), in_=bn_acc), "s_st", 16)
                g.wait_ge(sems.h["s_st"], sems.at("s_st"))
                sems.inc(g.collective_compute(
                    "AllReduce", ALU.add,
                    replica_groups=[list(range(NCORES))],
                    ins=[ccbn_in.ap()], outs=[ccbn_out.ap()]), "s_cc", 1)

    if dev_phase is not None and dev_phase == 5:
        with nc.Block() as blk:
            @blk.gpsimd
            def _(g):
                g.wait_ge(sems.h["s_st"], sems.at("s_st"))
                g.wait_ge(sems.h["s_st2"], sems.at("s_st2"))
                sems.inc(g.dma_start(out=gmax_dbg, in_=cch_out.ap()), "s_st", 16)
                g.wait_ge(sems.h["s_st"], sems.at("s_st"))
        return nc

    CL = 256
    NCL = ncl_limit if ncl_limit is not None else T // CL
    CPS = SEQ // CL
    with contextlib.ExitStack() as ctx:
        sb = lambda nm, shp, dt=F32: ctx.enter_context(nc.sbuf_tensor(nm, shp, dt)).ap()
        pst = lambda nm, shp: ctx.enter_context(nc.psum_tensor(nm, shp, F32)).ap()
        w1o = sb("w1o", [128, 8 * 1024])
        qc = [sb(f"qc{i}", [128, 8 * CL]) for i in range(2)]
        kc = [sb(f"kc{i}", [128, 8 * CL]) for i in range(2)]
        vc = [sb(f"vc{i}", [128, 2 * 1024]) for i in range(2)]
        AT_sb = sb("AT_sb", [128, 2 * 512])
        knat_sb = sb("knat_sb", [128, 2 * 128])
        Sz_sb = sb("Sz_sb", [128, 8 * 65])
        zt_sb = sb("zt_sb", [128, 8])
        den_sb = sb("den_sb3", [1, 2 * 256])
        rden3 = sb("rden3", [1, 2 * 256])
        rdenb3 = sb("rdenb3", [128, 2 * 256])
        yT_sb = sb("yT_sb", [128, 8 * 512])
        x1r = sb("x1r", [128, 8 * 512])
        p2o = sb("p2o", [128, 8 * 512])
        l1m = [sb(f"l1m{i}", [128, 256]) for i in range(2)]
        identL = sb("identL", [128, 64])
        psA = [pst(f"psA{i}", [128, 512]) for i in range(2)]
        psN = [pst(f"psN{i}", [128, 256]) for i in range(2)]
        psM = [pst(f"psM{i}", [128, 512]) for i in range(2)]
        psD = [pst(f"psD{i}", [128, 512]) for i in range(2)]

        ld0 = sems.at("s_ld")

        with nc.Block() as blk:
            @blk.sync
            def _(sync):
                for nm in ("sq1s", "sk1s", "sv1s", "sxn1A", "sxn1B"):
                    sync.wait_ge(sems.h[nm], sems.at(nm))
                sems.inc(sync.dma_start(
                    out=w1o.rearrange("p (k d) -> p k d", d=1024),
                    in_=l1_wo.rearrange("(k p) d -> p k d", p=128)), "s_ld", 16)
                sems.inc(sync.dma_start(out=l1m[0], in_=l1m0_c.ap()), "s_ld", 16)
                sems.inc(sync.dma_start(out=l1m[1], in_=l1m1_c.ap()), "s_ld", 16)
                sems.inc(sync.dma_start(out=identL, in_=identL_c.ap()), "s_ld", 16)
                for ci in range(NCL):
                    if ci >= 2:
                        sync.wait_ge(sems.h["s_A"], 16 * (ci - 1))
                        sync.wait_ge(sems.h["s_num"], 16 * (ci - 1))
                        sync.wait_ge(sems.h["s_den"], 16 * (ci - 1))
                        sync.wait_ge(sems.h["s_Smm"], 16 * (ci - 1))
                        sync.wait_ge(sems.h["s_ktr"], 16 * (ci - 1))
                        sync.wait_ge(sems.h["s_Sup"], 16 * (ci - 1))
                    qkv = "qcA" if ci % 2 == 0 else "qcB"
                    sems.inc(sync.dma_start(
                        out=qc[ci % 2].rearrange("p (k t) -> p k t", t=CL),
                        in_=q1T_h_t[:, :, ci * CL:(ci + 1) * CL]), qkv, 16)
                    sems.inc(sync.dma_start(
                        out=kc[ci % 2].rearrange("p (k t) -> p k t", t=CL),
                        in_=k1T_h_t[:, :, ci * CL:(ci + 1) * CL]), qkv, 16)
                    sems.inc(sync.dma_start(
                        out=vc[ci % 2].rearrange("p (a d) -> p a d", d=1024),
                        in_=v1_h.rearrange("(c a p) d -> c p a d", a=2, p=128)[ci]),
                        qkv, 16)
                    if ci % 2 == 0:
                        grp = ci // 2
                        if grp >= 1:
                            sync.wait_ge(sems.h["s_p2"], 8 * grp)
                        sems.inc(sync.dma_start(
                            out=x1r.rearrange("p (k t) -> p k t", t=512),
                            in_=x1T_h_t[:, :, grp * 512:(grp + 1) * 512]), "x1rA", 16)

            @blk.tensor
            def _(tensor):
                tensor.wait_ge(sems.h["s_ld"], ld0 + 64)
                for ci in range(NCL):
                    tensor.wait_ge(sems.h["qcA" if ci % 2 == 0 else "qcB"],
                                   48 * (ci // 2 + 1))
                    hsl = lambda buf, h: buf[(h % 2) * 64:(h % 2) * 64 + 64,
                                             (h // 2) * CL:(h // 2 + 1) * CL]

                    def emit_A(h):
                        E = 16 * ci + h
                        if ci % 2 == 0 and h < 2 and ci >= 2:
                            tensor.wait_ge(sems.h["s_p2"], 8 * (ci // 2))
                        if E >= 2:
                            tensor.wait_ge(sems.h["s_bce"], E - 1)
                            tensor.wait_ge(sems.h["s_msk"], 2 * (E - 2) + 2)
                        for sub in range(2):
                            ins = tensor.matmul(
                                psA[h % 2][:, sub * 256:(sub + 1) * 256],
                                kc[ci % 2][(h % 2) * 64:(h % 2) * 64 + 64,
                                           (h // 2) * CL + sub * 128:
                                           (h // 2) * CL + (sub + 1) * 128],
                                hsl(qc[ci % 2], h), start=True, stop=True)
                        sems.inc(ins, "s_A", 1)

                    def emit_ktr(h):
                        E = 16 * ci + h
                        if E >= 2:
                            tensor.wait_ge(sems.h["s_knv"], E - 1)
                            tensor.wait_ge(sems.h["s_Sup"], E - 1)
                        for sub in range(2):
                            ins = tensor.transpose(
                                psM[h % 2][:, sub * 64:(sub + 1) * 64],
                                kc[ci % 2][(h % 2) * 64:(h % 2) * 64 + 64,
                                           (h // 2) * CL + sub * 128:
                                           (h // 2) * CL + (sub + 1) * 128],
                                identL[(h % 2) * 64:(h % 2) * 64 + 64, 0:64])
                        sems.inc(ins, "s_ktr", 1)

                    def emit_bc(h):
                        E = 16 * ci + h
                        tensor.wait_ge(sems.h["s_rdn"], E + 1)
                        sems.inc(tensor.matmul(
                            psA[h % 2][:, 256:512], ones_r,
                            rden3[0:1, (h % 2) * 256:(h % 2 + 1) * 256],
                            start=True, stop=True), "s_bcm", 1)

                    def emit_work(h):
                        E = 16 * ci + h
                        rows = slice((h % 2) * 64, (h % 2) * 64 + 64)
                        tensor.wait_ge(sems.h["s_msk"], 2 * E + 2)
                        if E >= 2:
                            tensor.wait_ge(sems.h["s_dnv"], E - 1)
                        for sub in range(2):
                            tensor.matmul(
                                psD[h % 2][0:1, 0:256], ones_c,
                                AT_sb[:, (h % 2) * 512 + sub * 256:
                                      (h % 2) * 512 + (sub + 1) * 256],
                                start=(sub == 0), stop=False)
                        sems.inc(tensor.matmul(
                            psD[h % 2][0:1, 0:256],
                            Sz_sb[rows, (h // 2) * 65 + 64:(h // 2) * 65 + 65],
                            hsl(qc[ci % 2], h), start=False, stop=True), "s_den", 1)
                        if ci % CPS == 0:
                            tensor.wait_ge(sems.h["s_seq"], ci // CPS + 1)
                        else:
                            tensor.wait_ge(sems.h["s_Sup"], 16 * (ci - 1) + h + 1)
                        if E >= 2:
                            tensor.wait_ge(sems.h["s_yml"], E - 1)
                        for sub in range(2):
                            tensor.matmul(
                                psN[h % 2][rows, 0:256],
                                vc[ci % 2][:, sub * 1024 + h * 64:
                                           sub * 1024 + (h + 1) * 64],
                                AT_sb[:, (h % 2) * 512 + sub * 256:
                                      (h % 2) * 512 + (sub + 1) * 256],
                                start=(sub == 0), stop=False)
                        sems.inc(tensor.matmul(
                            psN[h % 2][rows, 0:256],
                            Sz_sb[rows, (h // 2) * 65:(h // 2) * 65 + 64],
                            hsl(qc[ci % 2], h), start=False, stop=True), "s_num", 1)
                        tensor.wait_ge(sems.h["s_knv"], E + 1)
                        for sub in range(2):
                            ins = tensor.matmul(
                                psM[h % 2][rows, 128:192],
                                knat_sb[:, (h % 2) * 128 + sub * 64:
                                        (h % 2) * 128 + (sub + 1) * 64],
                                vc[ci % 2][:, sub * 1024 + h * 64:
                                           sub * 1024 + (h + 1) * 64],
                                start=(sub == 0), stop=(sub == 1))
                        sems.inc(ins, "s_Smm", 1)

                    emit_A(0)
                    emit_ktr(0)
                    for h in range(16):
                        if h >= 1:
                            emit_bc(h - 1)
                        if h + 1 < 16:
                            emit_A(h + 1)
                            emit_ktr(h + 1)
                        emit_work(h)
                    emit_bc(15)
                    if ci % 2 == 1:
                        grp = ci // 2
                        tensor.wait_ge(sems.h["s_yml"], 16 * (ci + 1))
                        tensor.wait_ge(sems.h["s_bce"], 16 * (ci + 1))
                        for oc in range(8):
                            W = 8 * grp + oc
                            if W >= 2:
                                tensor.wait_ge(sems.h["s_p2"], W - 1)
                            for k in range(KT):
                                ins = tensor.matmul(
                                    psA[oc % 2],
                                    w1o[:, k * 1024 + oc * 128: k * 1024 + (oc + 1) * 128],
                                    yT_sb[:, k * 512:(k + 1) * 512],
                                    start=(k == 0), stop=(k == 7))
                            sems.inc(ins, "s_wo2", 1)

            @blk.scalar
            def _(scalar):
                for nm in ("sq1s", "sk1s"):
                    scalar.wait_ge(sems.h[nm], sems.at(nm))
                for ci in range(NCL):
                    for h in range(16):
                        E = 16 * ci + h
                        if h >= 1:
                            scalar.wait_ge(sems.h["s_bcm"], E)
                            sems.inc(scalar.copy(
                                out=rdenb3[:, ((h - 1) % 2) * 256:((h - 1) % 2 + 1) * 256],
                                in_=psA[(h - 1) % 2][:, 256:512]), "s_bce", 1)
                        scalar.wait_ge(sems.h["s_ktr"], E + 1)
                        sems.inc(scalar.copy(
                            out=knat_sb[:, (h % 2) * 128:(h % 2 + 1) * 128],
                            in_=psM[h % 2][:, 0:128]), "s_knv", 1)
                        scalar.wait_ge(sems.h["s_den"], E + 1)
                        sems.inc(scalar.copy(
                            out=den_sb[0:1, (h % 2) * 256:(h % 2 + 1) * 256],
                            in_=psD[h % 2][0:1, 0:256]), "s_dnv", 1)
                    scalar.wait_ge(sems.h["s_bcm"], 16 * (ci + 1))
                    sems.inc(scalar.copy(
                        out=rdenb3[:, (15 % 2) * 256:(15 % 2 + 1) * 256],
                        in_=psA[15 % 2][:, 256:512]), "s_bce", 1)

            @blk.vector
            def _(vector):
                for nm in ("sq1s", "sk1s"):
                    vector.wait_ge(sems.h[nm], sems.at(nm))
                for ci in range(NCL):
                    if ci % CPS == 0:
                        if ci > 0:
                            vector.wait_ge(sems.h["s_num"], 16 * ci)
                            vector.wait_ge(sems.h["s_den"], 16 * ci)
                        sems.inc(vector.memset(Sz_sb, 0.0), "s_seq", 1)

                    def fm(h):
                        E = 16 * ci + h
                        vector.wait_ge(sems.h["s_A"], E + 1)
                        if E >= 2:
                            vector.wait_ge(sems.h["s_num"], E - 1)
                            vector.wait_ge(sems.h["s_den"], E - 1)
                        for sub in range(2):
                            vector.tensor_mul(
                                AT_sb[:, (h % 2) * 512 + sub * 256:
                                      (h % 2) * 512 + (sub + 1) * 256],
                                psA[h % 2][:, sub * 256:(sub + 1) * 256],
                                l1m[sub])
                        sems.inc(vector.engine_nop(), "s_msk", 2)
                        rows = slice((h % 2) * 64, (h % 2) * 64 + 64)
                        vector.reduce_sum(
                            zt_sb[rows, h // 2:h // 2 + 1],
                            kc[ci % 2][rows, (h // 2) * CL:(h // 2 + 1) * CL],
                            axis=AX.X)

                    def fd(h):
                        E = 16 * ci + h
                        vector.wait_ge(sems.h["s_dnv"], E + 1)
                        sems.inc(vector.tensor_scalar_add(
                            rden3[0:1, (h % 2) * 256:(h % 2 + 1) * 256],
                            den_sb[0:1, (h % 2) * 256:(h % 2 + 1) * 256],
                            ATT_EPS), "s_vh", 1)
                        vector.wait_ge(sems.h["s_vh"], sems.at("s_vh"))
                        sems.inc(vector.reciprocal(
                            rden3[0:1, (h % 2) * 256:(h % 2 + 1) * 256],
                            rden3[0:1, (h % 2) * 256:(h % 2 + 1) * 256]), "s_rdn", 1)

                    def back(h):
                        E = 16 * ci + h
                        rows = slice((h % 2) * 64, (h % 2) * 64 + 64)
                        vector.wait_ge(sems.h["s_num"], E + 1)
                        vector.wait_ge(sems.h["s_bce"], E + 1)
                        if ci % 2 == 0 and ci >= 2 and h == 0:
                            vector.wait_ge(sems.h["s_wo2"], 8 * (ci // 2))
                        sems.inc(vector.tensor_mul(
                            yT_sb[rows, (h // 2) * 512 + (ci % 2) * 256:
                                  (h // 2) * 512 + (ci % 2) * 256 + 256],
                            psN[h % 2][rows, 0:256],
                            rdenb3[rows, (h % 2) * 256:(h % 2 + 1) * 256]),
                            "s_yml", 1)
                        vector.wait_ge(sems.h["s_Smm"], E + 1)
                        vector.tensor_add(
                            Sz_sb[rows, (h // 2) * 65:(h // 2) * 65 + 64],
                            Sz_sb[rows, (h // 2) * 65:(h // 2) * 65 + 64],
                            psM[h % 2][rows, 128:192])
                        sems.inc(vector.tensor_add(
                            Sz_sb[rows, (h // 2) * 65 + 64:(h // 2) * 65 + 65],
                            Sz_sb[rows, (h // 2) * 65 + 64:(h // 2) * 65 + 65],
                            zt_sb[rows, h // 2:h // 2 + 1]), "s_Sup", 1)

                    fm(0)
                    for h in range(16):
                        if h + 1 < 16:
                            fm(h + 1)
                        fd(h)
                        back(h)
                    if ci % 2 == 1:
                        grp = ci // 2
                        vector.wait_ge(sems.h["x1rA"], 16 * (grp + 1))
                        if grp >= 1:
                            vector.wait_ge(sems.h["sp2"], 16 * grp)
                        for oc in range(8):
                            vector.wait_ge(sems.h["s_wo2"], 8 * grp + oc + 1)
                            sems.inc(vector.tensor_add(
                                p2o[:, oc * 512:(oc + 1) * 512],
                                psA[oc % 2],
                                x1r[:, oc * 512:(oc + 1) * 512]), "s_p2", 1)

            @blk.gpsimd
            def _(g):
                for grp in range(NCL // 2):
                    g.wait_ge(sems.h["s_p2"], 8 * (grp + 1))
                    sems.inc(g.dma_start(
                        out=p2T_h_t[:, :, grp * 512:(grp + 1) * 512],
                        in_=p2o.rearrange("p (k t) -> p k t", t=512)), "sp2", 16)

    with contextlib.ExitStack() as ctx:
        sb = lambda nm, shp, dt=F32: ctx.enter_context(nc.sbuf_tensor(nm, shp, dt)).ap()
        pst = lambda nm, shp: ctx.enter_context(nc.psum_tensor(nm, shp, F32)).ap()
        l1w2 = sb("l1w2", [128, 8 * 1024])
        solw = sb("solw", [128, 2 * 1024])
        silw = sb("silw", [128, 2 * 1024])
        solb = sb("solb", [128, 1024])
        silb = sb("silb", [128, 1024])
        gam_sb = sb("gam_sb", [128, 4])
        bet_sb = sb("bet_sb", [128, 4])
        b8_sb = sb("b8_sb", [128, 8])
        bn_g = sb("bn_g", [128, 8])
        bn_m = sb("bn_m", [128, 4])
        bn_e2 = sb("bn_e2", [128, 4])
        bn_v = sb("bn_v", [128, 4])
        bn_s = sb("bn_s", [128, 4])
        bn_r = sb("bn_r", [128, 4])
        sc_a = sb("sc_a", [128, 4])
        sh_b = sb("sh_b", [128, 4])
        xcin = sb("xcin", [128, 4 * 512])
        xnc = sb("xnc", [128, 8 * 512])
        fullc = sb("fullc", [128, 8 * 512])
        p2c = sb("p2c", [128, 8 * 512])
        xs_sb = sb("xs_sb", [128, 8 * 512])
        outB = sb("outB", [128, 8 * 512])
        out_nat = sb("out_nat", [128, 4 * 1024])
        logit_sb = sb("logit_sb", [128, 1024])
        p_sb3 = sb("p_sb3", [128, 1024])
        lgA = sb("lgA", [128, 1024])
        lgB = sb("lgB", [128, 1024])
        M_sb = sb("M_sb", [128, 1024])
        m8 = sb("m8", [128, 4 * 8])
        rsum = sb("rsum", [128, 1])
        ps_lg = [pst(f"ps_lg{i}", [128, 512]) for i in range(2)]
        ps_mt = [pst(f"ps_mt{i}", [128, 512]) for i in range(2)]
        ps_in = [pst(f"ps_in{i}", [128, 512]) for i in range(2)]
        ps_ot = pst("ps_ot", [128, 512])

        glA0 = sems.at("s_guA")
        gu0 = sems.at("s_gu")
        guev0 = sems.at("s_guev")
        qx0c = sems.at("s_qx")
        fm3 = sems.at("s_fm")
        tpB0 = sems.at("s_tpB")
        pf0 = sems.at("s_pf")
        dn0 = sems.at("s_dn")
        oev0 = sems.at("s_oev")
        wo0 = sems.at("s_wo")
        evA0c = sems.at("s_evA")
        scl3 = sems.at("s_scl")
        gm0 = sems.at("gmld")
        xqA0 = sems.at("xqA")
        xqB0 = sems.at("xqB")
        wnA0 = sems.at("wnA")
        wnB0 = sems.at("wnB")
        ld20c = sems.at("s_ld2")
        sqr3 = sems.at("s_sqr")

        with nc.Block() as blk:
            @blk.sync
            def _(sync):
                for nm in ("sq1s", "sk1s", "sp2"):
                    sync.wait_ge(sems.h[nm], sems.at(nm))
                sems.inc(sync.dma_start(
                    out=l1w2.rearrange("p (k d) -> p k d", d=1024),
                    in_=l1_w.rearrange("(k p) d -> p k d", p=128)), "s_ld2", 16)
                sems.inc(sync.dma_start(
                    out=solw.rearrange("p (k d) -> p k d", d=1024),
                    in_=so_linT.rearrange("(k p) d -> p k d", p=128)), "s_ld2", 16)
                sems.inc(sync.dma_start(
                    out=silw.rearrange("p (k d) -> p k d", d=1024),
                    in_=si_linT.rearrange("(k p) d -> p k d", p=128)), "s_ld2", 16)
                sems.inc(sync.dma_start(out=solb, in_=so_lb), "s_ld2", 16)
                sems.inc(sync.dma_start(out=silb, in_=si_lb), "s_ld2", 16)
                sems.inc(sync.dma_start(out=gam_sb, in_=gam_c), "s_ld2", 16)
                sems.inc(sync.dma_start(out=bet_sb, in_=bet_c), "s_ld2", 16)
                sems.inc(sync.dma_start(out=b8_sb, in_=l1_b8), "s_ld2", 16)
                for tc in range(NTC):
                    if tc >= 1:
                        sync.wait_ge(sems.h["s_guA"], glA0 + 4 * tc)
                        sync.wait_ge(sems.h["s_gu"], gu0 + 16 * tc)
                    sems.inc(sync.dma_start(
                        out=xcin.rearrange("p (b t) -> p b t", t=512),
                        in_=xcT_h_t[:, :, tc * 512:(tc + 1) * 512]), "xqA", 16)
                    if tc >= 1:
                        sync.wait_ge(sems.h["s_pf"], pf0 + 16 * tc)
                    sems.inc(sync.dma_start(
                        out=xnc.rearrange("p (k t) -> p k t", t=512),
                        in_=xn1T_h_t[:, :, tc * 512:(tc + 1) * 512]), "xqB", 16)
                    sems.inc(sync.dma_start(
                        out=fullc.rearrange("p (k t) -> p k t", t=512),
                        in_=fullT_h_t[:, :, tc * 512:(tc + 1) * 512]), "wnA", 16)
                    if tc >= 1:
                        sync.wait_ge(sems.h["s_oev"], oev0 + 8 * tc)
                    sems.inc(sync.dma_start(
                        out=p2c.rearrange("p (k t) -> p k t", t=512),
                        in_=p2T_h_t[:, :, tc * 512:(tc + 1) * 512]), "wnB", 16)

            @blk.gpsimd
            def _(g):
                g.wait_ge(sems.h["s_cc"], 3)
                sems.inc(g.dma_start(out=bn_g, in_=ccbn_out.ap()), "gmld", 16)
                for tc in range(NTC):
                    g.wait_ge(sems.h["s_evA"], evA0c + 8 * (tc + 1))
                    sems.inc(g.dma_start(
                        out=out_h_t[tc],
                        in_=out_nat.rearrange("p (a d) -> p a d", d=1024)),
                        "sout", 16)
                g.wait_ge(sems.h["sout"], 16 * NTC)

            @blk.scalar
            def _(scalar):
                scalar.wait_ge(sems.h["sp2"], sems.at("sp2"))
                scalar.wait_ge(sems.h["s_scl"], scl3 + 1)
                sems.inc(scalar.activation(out=bn_s, in_=bn_v, func=AF.Sqrt,
                                           bias=cst[:, 3:4]), "s_scl", 1)
                for tc in range(NTC):
                    scalar.wait_ge(sems.h["xqA"], xqA0 + 16 * (tc + 1))
                    if tc == 0:
                        scalar.wait_ge(sems.h["s_scl"], scl3 + 3)
                    for bj in range(4):
                        sems.inc(scalar.activation(
                            out=xcin[:, bj * 512:(bj + 1) * 512],
                            in_=xcin[:, bj * 512:(bj + 1) * 512],
                            func=AF.Gelu, scale=sc_a[:, bj:bj + 1],
                            bias=sh_b[:, bj:bj + 1]), "s_guA", 1)
                    for TB in range(8):
                        TBg = 8 * tc + TB
                        scalar.wait_ge(sems.h["s_guev"], guev0 + 2 * TBg + 2)
                        if TBg >= 1:
                            scalar.wait_ge(sems.h["s_qx"], qx0c + TBg)
                        sems.inc(scalar.activation(
                            out=p_sb3, in_=logit_sb, func=AF.Exp,
                            accum_out=rsum), "s_fm", 1)
                    for oc in range(8):
                        scalar.wait_ge(sems.h["s_wo"], wo0 + 8 * tc + oc + 1)
                        if oc == 0 and tc >= 1:
                            scalar.wait_ge(sems.h["sout"], 16 * tc)
                        sems.inc(scalar.copy(
                            out=out_nat.rearrange("p (a d) -> p a d", d=1024)[
                                :, :, oc * 128:(oc + 1) * 128],
                            in_=ps_ot.rearrange("p (a b) -> p a b", b=128)),
                            "s_evA", 1)

            @blk.tensor
            def _(tensor):
                tensor.wait_ge(sems.h["s_ld2"], ld20c + 128)
                for tc in range(NTC):
                    def lin(TB):
                        sub, br = TB // 2, TB % 2
                        TBg = 8 * tc + TB
                        tensor.wait_ge(sems.h["s_guA"],
                                       glA0 + 4 * tc + 2 * br + 2)
                        if TBg >= 1:
                            tensor.wait_ge(sems.h["s_guev"], guev0 + 2 * TBg)
                        lw = solw if br == 0 else silw
                        for half in range(2):
                            for gt in range(2):
                                ins = tensor.matmul(
                                    ps_lg[half],
                                    xcin[:, (br * 2 + gt) * 512 + sub * 128:
                                         (br * 2 + gt) * 512 + (sub + 1) * 128],
                                    lw[:, gt * 1024 + half * 512:
                                       gt * 1024 + (half + 1) * 512],
                                    start=(gt == 0), stop=(gt == 1))
                            sems.inc(ins, "s_gu", 1)

                    def mtr(TB):
                        sub, br = TB // 2, TB % 2
                        TBg = 8 * tc + TB
                        tensor.wait_ge(sems.h["s_qx"], qx0c + TBg + 1)
                        if TBg >= 1:
                            tensor.wait_ge(sems.h["s_pf"], pf0 + 2 * TBg)
                        for gr in range(2):
                            for k4 in range(4):
                                k = gr * 4 + k4
                                ins = tensor.transpose(
                                    ps_mt[gr][:, k4 * 128:(k4 + 1) * 128],
                                    M_sb[:, k * 128:(k + 1) * 128], ident)
                            sems.inc(ins, "s_tpB", 1)

                    lin(0)
                    for TB in range(8):
                        if TB + 1 < 8:
                            lin(TB + 1)
                        mtr(TB)
                    for oc in range(8):
                        if oc == 0:
                            tensor.wait_ge(sems.h["s_pf"], pf0 + 16 * (tc + 1))
                        DN = 8 * tc + oc
                        if DN >= 2:
                            tensor.wait_ge(sems.h["s_oev"], oev0 + DN - 1)
                        for k in range(KT):
                            ins = tensor.matmul(
                                ps_in[oc % 2],
                                l1w2[:, k * 1024 + oc * 128: k * 1024 + (oc + 1) * 128],
                                xs_sb[:, k * 512:(k + 1) * 512],
                                start=(k == 0), stop=(k == 7))
                        sems.inc(ins, "s_dn", 1)
                    for oc in range(8):
                        tensor.wait_ge(sems.h["s_oev"], oev0 + 8 * tc + oc + 1)
                        tensor.wait_ge(sems.h["s_evA"], evA0c + 8 * tc + oc)
                        for j in range(4):
                            ins = tensor.transpose(
                                ps_ot[:, j * 128:(j + 1) * 128],
                                outB[:, oc * 512 + j * 128: oc * 512 + (j + 1) * 128],
                                ident)
                        sems.inc(ins, "s_wo", 1)

            @blk.vector
            def _(vector):
                vector.wait_ge(sems.h["sp2"], sems.at("sp2"))
                vector.wait_ge(sems.h["gmld"], gm0 + 16)
                vector.wait_ge(sems.h["s_ld2"], ld20c + 128)
                vector.tensor_scalar_mul(bn_m, bn_g[:, 0:4], 1.0 / 65536.0)
                sems.inc(vector.tensor_scalar_mul(bn_e2, bn_g[:, 4:8],
                                                  1.0 / 65536.0), "s_vh", 1)
                vector.wait_ge(sems.h["s_vh"], sems.at("s_vh"))
                sems.inc(vector.tensor_mul(bn_v, bn_m, bn_m), "s_vh", 1)
                vector.wait_ge(sems.h["s_vh"], sems.at("s_vh"))
                sems.inc(vector.tensor_sub(bn_v, bn_e2, bn_v), "s_scl", 1)
                vector.wait_ge(sems.h["s_scl"], scl3 + 2)
                sems.inc(vector.reciprocal(bn_r, bn_s), "s_vh", 1)
                vector.wait_ge(sems.h["s_vh"], sems.at("s_vh"))
                sems.inc(vector.tensor_mul(sc_a, gam_sb, bn_r), "s_vh", 1)
                vector.wait_ge(sems.h["s_vh"], sems.at("s_vh"))
                sems.inc(vector.tensor_mul(bn_r, bn_m, sc_a), "s_vh", 1)
                vector.wait_ge(sems.h["s_vh"], sems.at("s_vh"))
                sems.inc(vector.tensor_sub(sh_b, bet_sb, bn_r), "s_scl", 1)
                NEG = -1.0e30
                for tc in range(NTC):
                    for sub in range(4):
                        for br in range(2):
                            TB = sub * 2 + br
                            TBg = 8 * tc + TB
                            lb = solb if br == 0 else silb
                            if TBg >= 1:
                                vector.wait_ge(sems.h["s_fm"], fm3 + TBg)
                            for half in range(2):
                                vector.wait_ge(sems.h["s_gu"],
                                               gu0 + 2 * TBg + half + 1)
                                sems.inc(vector.tensor_add(
                                    logit_sb[:, half * 512:(half + 1) * 512],
                                    ps_lg[half],
                                    lb[:, half * 512:(half + 1) * 512]),
                                    "s_guev", 1)
                            vector.wait_ge(sems.h["s_fm"], fm3 + TBg + 1)
                            vector.reciprocal(rsum, rsum)
                            sems.inc(vector.max(out=m8[:, 0:8], in_=logit_sb),
                                     "s_vh", 1)
                            vector.wait_ge(sems.h["s_vh"], sems.at("s_vh"))
                            vector.match_replace(out=lgA, in_to_replace=m8[:, 0:8],
                                                 in_values=logit_sb, imm_value=NEG)
                            sems.inc(vector.max(out=m8[:, 8:16], in_=lgA), "s_vh", 1)
                            vector.wait_ge(sems.h["s_vh"], sems.at("s_vh"))
                            vector.match_replace(out=lgB, in_to_replace=m8[:, 8:16],
                                                 in_values=lgA, imm_value=NEG)
                            sems.inc(vector.max(out=m8[:, 16:24], in_=lgB), "s_vh", 1)
                            vector.wait_ge(sems.h["s_vh"], sems.at("s_vh"))
                            vector.match_replace(out=lgA, in_to_replace=m8[:, 16:24],
                                                 in_values=lgB, imm_value=NEG)
                            sems.inc(vector.max(out=m8[:, 24:32], in_=lgA), "s_vh", 1)
                            vector.wait_ge(sems.h["s_vh"], sems.at("s_vh"))
                            vector.scalar_tensor_tensor(
                                out=M_sb, in0=logit_sb, scalar=m8[:, 31:32],
                                in1=p_sb3, op0=ALU.is_ge, op1=ALU.mult)
                            sems.inc(vector.tensor_scalar_mul(M_sb, M_sb, rsum),
                                     "s_qx", 1)
                            if br == 0:
                                vector.wait_ge(sems.h["wnA"],
                                               wnA0 + 16 * (tc + 1))
                                if TBg == 0 or (tc >= 1 and TB == 0):
                                    vector.wait_ge(sems.h["s_wo"], wo0 + 8 * tc)
                                other, dstb = fullc, outB
                            else:
                                vector.wait_ge(sems.h["xqB"],
                                               xqB0 + 16 * (tc + 1))
                                if tc >= 1 and TB <= 1:
                                    vector.wait_ge(sems.h["s_dn"], dn0 + 8 * tc)
                                other, dstb = xnc, xs_sb
                            for gr in range(2):
                                vector.wait_ge(sems.h["s_tpB"],
                                               tpB0 + 2 * TBg + gr + 1)
                                dv = dstb.rearrange("p (k t) -> p k t", t=512)[
                                    :, gr * 4:(gr + 1) * 4, sub * 128:(sub + 1) * 128]
                                ov = other.rearrange("p (k t) -> p k t", t=512)[
                                    :, gr * 4:(gr + 1) * 4, sub * 128:(sub + 1) * 128]
                                sems.inc(vector.tensor_mul(
                                    dv,
                                    ps_mt[gr].rearrange("p (a b) -> p a b", b=128),
                                    ov), "s_pf", 1)
                    vector.wait_ge(sems.h["wnB"], wnB0 + 16 * (tc + 1))
                    for oc in range(8):
                        DN = 8 * tc + oc
                        vector.wait_ge(sems.h["s_dn"], dn0 + DN + 1)
                        vector.scalar_tensor_tensor(
                            out=outB[:, oc * 512:(oc + 1) * 512],
                            in0=ps_in[oc % 2], scalar=b8_sb[:, oc:oc + 1],
                            in1=outB[:, oc * 512:(oc + 1) * 512],
                            op0=ALU.add, op1=ALU.add)
                        sems.inc(vector.tensor_add(
                            outB[:, oc * 512:(oc + 1) * 512],
                            outB[:, oc * 512:(oc + 1) * 512],
                            p2c[:, oc * 512:(oc + 1) * 512]), "s_oev", 1)

    if dev_phase is not None:
        return nc
    return nc


def _layouts(inputs):
    g = {}
    g["l0_wqT"] = np.ascontiguousarray(inputs["L0_attn_Wq"].T)
    g["l0_wkT"] = np.ascontiguousarray(inputs["L0_attn_Wk"].T)
    g["l0_wvT"] = np.ascontiguousarray(inputs["L0_attn_Wv"].T)
    g["l0_woT"] = np.ascontiguousarray(inputs["L0_attn_Wo"].T)
    g["l0_cx"] = inputs["L0_ffn_clip_x"].reshape(1, 1)
    g["l0_cd"] = inputs["L0_ffn_clip_down"].reshape(1, 1)
    g["l0_wg"] = inputs["L0_ffn_w_gate"]
    g["l0_wu"] = inputs["L0_ffn_w_up"]
    g["l0_wd"] = inputs["L0_ffn_w_down"]
    g["l1_wq"] = inputs["L1_attn_Wq"]
    g["l1_wk"] = inputs["L1_attn_Wk"]
    g["l1_wv"] = inputs["L1_attn_Wv"]
    g["l1_wo"] = inputs["L1_attn_Wo"]
    g["l1_w"] = inputs["L1_ffn_weight"]
    for nm, cw in (("so_bd", inputs["L1_ffn_so_conv_w"]),
                   ("si_bd", inputs["L1_ffn_si_conv_w"])):
        bd = np.zeros((D, 256), np.float32)
        bd[np.arange(D), np.arange(D) // 4] = np.asarray(cw, np.float32).reshape(-1)
        g[nm] = bd
    def pack4(a, b):
        out = np.empty((128, 4), np.float32)
        out[:, 0:2] = np.asarray(a, np.float32).reshape(2, 128).T
        out[:, 2:4] = np.asarray(b, np.float32).reshape(2, 128).T
        return out
    g["cvb_c"] = pack4(inputs["L1_ffn_so_conv_b"], inputs["L1_ffn_si_conv_b"])
    g["gam_c"] = pack4(inputs["L1_ffn_so_gamma"], inputs["L1_ffn_si_gamma"])
    g["bet_c"] = pack4(inputs["L1_ffn_so_beta"], inputs["L1_ffn_si_beta"])
    g["so_linT"] = np.ascontiguousarray(np.asarray(inputs["L1_ffn_so_lin_w"], np.float32).T)
    g["si_linT"] = np.ascontiguousarray(np.asarray(inputs["L1_ffn_si_lin_w"], np.float32).T)
    g["so_lb"] = np.ascontiguousarray(np.broadcast_to(
        np.asarray(inputs["L1_ffn_so_lin_b"], np.float32), (128, D)))
    g["si_lb"] = np.ascontiguousarray(np.broadcast_to(
        np.asarray(inputs["L1_ffn_si_lin_b"], np.float32), (128, D)))
    g["l1_b8"] = np.ascontiguousarray(
        np.asarray(inputs["L1_ffn_bias"], np.float32).reshape(8, 128).T)
    return g


def kernel(**inputs):
    nc = build_nc()
    shared = _layouts(inputs)
    x = np.ascontiguousarray(inputs["L0_attn_x"].astype(np.float32, copy=False))
    in_maps = []
    for c in range(NCORES):
        m = dict(shared)
        m["x_in"] = np.ascontiguousarray(
            x[c * SEQ_PER_CORE:(c + 1) * SEQ_PER_CORE].reshape(T, D))
        in_maps.append(m)
    res = run_bass_kernel_spmd(nc, in_maps, core_ids=list(range(NCORES)))
    out = np.concatenate([r["out"].reshape(SEQ_PER_CORE, SEQ, D)
                          for r in res.results], axis=0)
    return out
